# version 04
# baseline (speedup 1.0000x reference)
import numpy as np
from contextlib import ExitStack

import concourse.bass as bass
import concourse.mybir as mybir
from concourse.bass_utils import run_bass_kernel_spmd

F32 = mybir.dt.float32
F32R = mybir.dt.float32r
AF = mybir.ActivationFunctionType
ALU = mybir.AluOpType

D = 2048
NDC = 16
TC = 1024
NTC = 8
TF = 4096
HEADS = 16
HD = 128
RANK = 16
KVD = 512
NKV = 4
HID = 8192
NHC = 64
SCALE = float(HD) ** -0.5
BEXP = 16.0
EPS = 1e-5
LORA_SC = 0.5
NEG = -1.0e30

_STAGES = 4
_DBG = False


class Tile:
    __slots__ = ("name", "w", "readers")

    def __init__(self, name):
        self.name = name
        self.w = None
        self.readers = []


class MK:
    ENGINES = ("tensor", "scalar", "vector", "gpsimd", "sync")

    def __init__(self, nc, stack):
        self.nc = nc
        self.stack = stack
        self.ops = {e: [] for e in self.ENGINES}
        self.sems = {}
        self.cnt = {}
        for name in ("sTE", "sSC", "sVE", "sGP", "sSY", "sCC", "sSCD"):
            self.sems[name] = stack.enter_context(nc.semaphore(name))
            self.cnt[name] = 0
        self.chans = {}
        self.rr = {}
        for eng, n in (("sync", 6), ("scalar", 4), ("gpsimd", 4)):
            self.chans[eng] = []
            self.rr[eng] = 0
            for i in range(n):
                nm = f"ch_{eng}{i}"
                self.sems[nm] = stack.enter_context(nc.semaphore(nm))
                self.cnt[nm] = 0
                self.chans[eng].append(nm)
        self.esem = {"tensor": "sTE", "scalar": "sSC", "vector": "sVE",
                     "gpsimd": "sGP", "sync": "sSY"}
        self.last_wait = {e: {} for e in self.ENGINES}
        self.bank_open = {}
        self.final_cnt = {}

    def _wait(self, eng, sem_name, val):
        if val is None:
            raise RuntimeError(f"un-synced dependency for {eng}")
        if self.last_wait[eng].get(sem_name, 0) >= val:
            return
        self.last_wait[eng][sem_name] = val
        sem = self.sems[sem_name]
        nc = self.nc
        fc = self.final_cnt

        def run(sem=sem, val=val, eng=eng, sn=sem_name):
            getattr(nc, eng).wait_ge(sem, min(val, fc.get(sn, val)))
        self.ops[eng].append(run)

    _SELF_FENCE = ("scalar", "vector", "gpsimd")
    DMA_W = 4

    def _wval(self, sem_name, val, windowed):
        if not windowed:
            return val
        return val + 16 * (self.DMA_W - 1)

    def dep_waits(self, eng, reads, writes):
        for t in reads:
            if t.w is not None:
                we, ws, wv, wasync, wwin = t.w
                if we != eng or wasync or eng in self._SELF_FENCE:
                    self._wait(eng, ws, self._wval(ws, wv, wwin))
        for t in writes:
            if t.w is not None:
                we, ws, wv, wasync, wwin = t.w
                if we != eng or wasync or eng in self._SELF_FENCE:
                    self._wait(eng, ws, self._wval(ws, wv, wwin))
            for (re, rs, rv, rwin) in t.readers:
                if re != eng or eng in self._SELF_FENCE:
                    self._wait(eng, rs, self._wval(rs, rv, rwin))

    def op(self, eng, fn, reads=(), writes=(), amt=1, sem=None, is_async=False,
           windowed=False):
        if sem == "sCC" and self.cnt["sCC"]:
            self._wait(eng, "sCC", self.cnt["sCC"])
        self.dep_waits(eng, reads, writes)
        sem_name = sem or self.esem[eng]
        self.cnt[sem_name] += amt
        val = self.cnt[sem_name]
        s = self.sems[sem_name]

        def run(fn=fn, s=s, amt=amt):
            ins = fn()
            ins.then_inc(s, amt)
        self.ops[eng].append(run)
        for t in reads:
            t.readers.append((eng, sem_name, val, windowed))
        for t in writes:
            t.w = (eng, sem_name, val, is_async, windowed)
            t.readers = []
        return val

    def dma(self, eng, out_ap, in_ap, reads=(), writes=()):
        nc = self.nc
        ch = self.chans[eng][self.rr[eng] % len(self.chans[eng])]
        self.rr[eng] += 1
        if self.cnt[ch]:
            self._wait(eng, ch, self.cnt[ch])
        return self.op(
            eng, lambda o=out_ap, i=in_ap: getattr(nc, eng).dma_start(out=o, in_=i),
            reads=reads, writes=writes, amt=16, sem=ch, is_async=True, windowed=False)

    def phase_barrier(self):
        for q in self.ENGINES:
            for s in list(self.cnt):
                if s == self.esem[q]:
                    continue
                if self.cnt[s]:
                    self._wait(q, s, self.cnt[s])

    def g_start(self, region):
        assert not self.bank_open.get(region), f"region {region} already open"
        self.bank_open[region] = True

    def g_stop(self, region):
        assert self.bank_open.get(region), f"region {region} not open"
        self.bank_open[region] = False

    def mm_group(self, tregs, mms):
        nc = self.nc
        all_reads = []
        for (_, _, _, _, _, rts) in mms:
            all_reads.extend(rts)
        self.dep_waits("tensor", all_reads, tregs)
        sem_name = "sTE"
        self.cnt[sem_name] += 1
        val = self.cnt[sem_name]
        s = self.sems[sem_name]
        n = len(mms)
        for i, (o, lt, rh, st, sp, _) in enumerate(mms):
            if i < n - 1:
                def run(o=o, lt=lt, rh=rh, st=st, sp=sp):
                    nc.tensor.matmul(o, lt, rh, start=st, stop=sp)
            else:
                def run(o=o, lt=lt, rh=rh, st=st, sp=sp, s=s):
                    nc.tensor.matmul(o, lt, rh, start=st, stop=sp).then_inc(s, 1)
            self.ops["tensor"].append(run)
        for t in all_reads:
            t.readers.append(("tensor", sem_name, val, False))
        for t in tregs:
            t.w = ("tensor", sem_name, val, False, False)
            t.readers = []
        return val


def _wstream(W):
    no, ni = W.shape[0] // 128, W.shape[1] // 128
    return np.ascontiguousarray(
        W.T.reshape(ni, 128, no, 128).transpose(2, 1, 0, 3).reshape(no, 128, ni * 128))


def _prep_host(inputs):
    g = lambda k: np.asarray(inputs[k], dtype=np.float32)
    shared = {}
    for n in ("q", "k", "v", "o"):
        shared[f"a{n}_w"] = _wstream(g(f"L0_attn_a_W{n}"))
        shared[f"a{n}_bT"] = np.ascontiguousarray(
            g(f"L0_attn_a_B{n}").T.reshape(NDC, 128, RANK).transpose(1, 0, 2)
            .reshape(128, NDC * RANK))
        shared[f"a{n}_aT"] = np.ascontiguousarray(g(f"L0_attn_a_A{n}").T)
        shared[f"a{n}_bias"] = np.ascontiguousarray(g(f"L0_attn_a_b{n}").reshape(NDC, 128).T)
        shared[f"a{n}_brow"] = g(f"L0_attn_a_b{n}").reshape(1, D)
    shared["av_wm"] = np.ascontiguousarray(g("L0_attn_a_Wv").T)
    shared["bq_w"] = _wstream(g("L0_attn_b_Wq"))
    shared["bk_w"] = _wstream(g("L0_attn_b_Wk"))
    shared["bv_wm"] = np.ascontiguousarray(g("L0_attn_b_Wv").T)
    shared["bp_w"] = _wstream(g("L0_attn_b_Wp"))
    shared["fg_w"] = _wstream(g("L0_ffn_w_gate"))
    shared["fu_w"] = _wstream(g("L0_ffn_w_up"))
    shared["fd_w"] = np.ascontiguousarray(
        g("L0_ffn_w_down").T.reshape(NHC, 128, 4, 4, 128).transpose(2, 0, 1, 3, 4)
        .reshape(4, NHC, 128, 512))
    shared["identF"] = np.eye(128, dtype=np.float32)
    perm = np.zeros((128, 128), dtype=np.float32)
    for d in range(128):
        perm[(d + 64) % 128, d] = 1.0
    shared["permR"] = perm
    shared["cstR"] = np.ones((128, 2), dtype=np.float32)
    shared["cstF"] = np.full((128, 1), 1.0 / D, dtype=np.float32)
    _cv = np.array([1e-5, -16.0, float(2048 ** 0) * (128.0 ** -0.5), 0.5, 1.0, -1.0, 0.0, 2.0],
                   dtype=np.float32)
    shared["actcF"] = np.tile(_cv, (128, 1))
    shared["actcR"] = np.tile(_cv, (128, 1))
    shared["row1"] = np.ones((1, 512), dtype=np.float32)
    shared["row1F"] = np.ones((1, 128), dtype=np.float32)
    dm = np.zeros((4, 128, 512), dtype=np.float32)
    for j in range(4):
        dk = np.arange(128)[:, None]
        dq = np.arange(512)[None, :]
        dm[j] = np.where(dk <= dq - 128 * j, 0.0, NEG)
    shared["dmask"] = dm.reshape(512, 512)

    x = np.asarray(inputs["L0_attn_a_x"], dtype=np.float32)
    in_maps = []
    half = np.arange(64, dtype=np.float64)
    invf = 1.0 / (10000.0 ** (half / 64.0))
    for c in range(8):
        b, gg = c // 4, c % 4
        m = dict(shared)
        m["x_tm"] = np.ascontiguousarray(x[b, TC * gg:TC * (gg + 1), :])
        pos = np.arange(TC * gg, TC * (gg + 1), dtype=np.float64)
        ang = pos[None, :] * invf[:, None]
        cos = np.cos(ang)
        sin = np.sin(ang)
        m["rope_cos"] = np.concatenate([cos, cos], 0).astype(np.float32)
        m["rope_sinS"] = np.concatenate([-sin, sin], 0).astype(np.float32)
        btbl = np.zeros((128, 64), dtype=np.float32)
        for qt in range(2):
            q128 = 8 * gg + 4 * qt
            for kc in range(32):
                btbl[:, 32 * qt + kc] = -BEXP if kc < q128 else NEG
        m["bias_tbl"] = btbl
        in_maps.append(m)
    return in_maps


def build_nc():
    nc = bass.Bass("TRN2", target_bir_lowering=False)
    stack = ExitStack()
    mk = MK(nc, stack)
    T = Tile

    di_ = {}

    def din(name, shape, dt=F32R):
        di_[name] = nc.dram_tensor(name, shape, dt, kind="ExternalInput").ap()
        return di_[name]

    x_tm = din("x_tm", [TC, D], F32)
    out_tm = nc.dram_tensor("out_tm", [TC, D], F32, kind="ExternalOutput").ap()
    for n in ("q", "k", "v", "o"):
        din(f"a{n}_w", [NDC, 128, D])
        din(f"a{n}_bT", [128, NDC * RANK])
        din(f"a{n}_aT", [RANK, D])
        din(f"a{n}_bias", [128, NDC], F32)
        din(f"a{n}_brow", [1, D], F32R)
    din("av_wm", [D, D])
    din("bq_w", [NDC, 128, D])
    din("bk_w", [NKV, 128, D])
    din("bv_wm", [D, KVD])
    din("bp_w", [NDC, 128, D])
    din("fg_w", [NHC, 128, D])
    din("fu_w", [NHC, 128, D])
    din("fd_w", [4, NHC, 128, 512])
    din("identF", [128, 128], F32)
    din("permR", [128, 128])
    din("cstR", [128, 2])
    din("cstF", [128, 1], F32)
    din("actcF", [128, 8], F32)
    din("actcR", [128, 8])
    din("row1", [1, 512])
    din("row1F", [1, 128], F32)
    din("dmask", [512, 512], F32)
    din("rope_cos", [128, TC], F32)
    din("rope_sinS", [128, TC], F32)
    din("bias_tbl", [128, 64], F32)

    dbg = {}
    if _DBG:
        for nm, shp in (("dbg_y", [D, TC]), ("dbg_q", [D, TC]), ("dbg_kv", [TC, TC]),
                        ("dbg_ao", [D, TC]), ("dbg_l", [128, 128])):
            dbg[nm] = nc.dram_tensor(nm, shp, F32R, kind="ExternalOutput").ap()
    qa_sp = nc.dram_tensor("qa_sp", [D, TC], F32R).ap()
    qb_sp = nc.dram_tensor("qb_sp", [D, TC], F32R).ap()
    h_sp = nc.dram_tensor("h_sp", [HID, TC], F32R).ap()
    kloc_sp = nc.dram_tensor("kloc_sp", [KVD, TC], F32R).ap()
    vloc_sp = nc.dram_tensor("vloc_sp", [TC, KVD], F32R).ap()
    cca_in = [nc.dram_tensor(f"cca_in{i}", [TC, TC], F32R).ap() for i in range(4)]
    cca_out = [nc.dram_tensor(f"cca_out{i}", [4 * TC, TC], F32R).ap() for i in range(4)]
    ccb_in = nc.dram_tensor("ccb_in", [TC, TC], F32R).ap()
    ccb_out = nc.dram_tensor("ccb_out", [4 * TC, TC], F32R).ap()

    t_qa = T("qa_sp")
    t_qb = T("qb_sp")
    t_h = [T(f"h_sp{i}") for i in range(NHC)]
    t_kloc = T("kloc")
    t_vloc = T("vloc")
    t_cca_in = [T(f"cca_in{i}") for i in range(4)]
    t_cca_out = [T(f"cca_out{i}") for i in range(4)]
    t_ccb_in = T("ccb_in")
    t_ccb_out = T("ccb_out")

    sb = lambda name, shape, dt=F32R: stack.enter_context(nc.sbuf_tensor(name, shape, dt))
    x_sb = sb("x_sb", [128, NDC * TC], F32)
    y_sb = sb("y_sb", [128, NDC * TC])
    ident_sb = sb("ident_sb", [128, 128], F32)
    perm_sb = sb("perm_sb", [128, 128])
    cst_sb = sb("cst_sb", [128, 2])
    avg_sb = sb("avg_sb", [128, 1], F32)
    actcF_sb = sb("actcF_sb", [128, 8], F32)
    actcR_sb = sb("actcR_sb", [128, 8])
    _CVALS = [EPS, -BEXP, SCALE, LORA_SC, 1.0, -1.0, 0.0, 2.0]
    for _i, _v in enumerate(_CVALS):
        nc.const_aps.register(F32, _v, actcF_sb[:, _i:_i + 1])
        nc.const_aps.register(F32R, _v, actcR_sb[:, _i:_i + 1])
    row1_sb = sb("row1_sb", [1, 512])
    row1F_sb = sb("row1F_sb", [1, 128], F32)
    btbl_sb = sb("btbl_sb", [128, 64], F32)
    bias_sb = {n: sb(f"bias_{n}", [128, NDC], F32) for n in ("q", "k", "v")}
    rowbuf_sb = sb("rowbuf_sb", [1, D])
    rows_sb = sb("rows_sb", [1, 2 * TC], F32)
    bc_sb = sb("bc_sb", [128, 512], F32)

    t_x = [T(f"x{i}") for i in range(NDC)]
    t_y = [T(f"y{i}") for i in range(NDC)]
    t_consts = T("consts")
    t_stat = T("stat")
    t_nmr = T("nmr")
    t_lt8 = T("lt8")
    t_lrow = T("lrow")
    t_lt8 = T("lt8")
    t_bc = T("bc")
    t_rowbuf = T("rowbuf")

    ps = stack.enter_context(nc.psum_tensor("ps", [128, 4096], F32))
    t_psreg = [T(f"psr{r}") for r in range(8)]

    def pst(off, width):
        return t_psreg[off // 512:(off + width + 511) // 512]

    mk.dma("sync", ident_sb[:, :], di_["identF"][:, :], writes=[t_consts])
    mk.dma("sync", perm_sb[:, :], di_["permR"][:, :], writes=[t_consts])
    mk.dma("sync", cst_sb[:, :], di_["cstR"][:, :], writes=[t_consts])
    mk.dma("sync", avg_sb[:, :], di_["cstF"][:, :], writes=[t_consts])
    mk.dma("sync", actcF_sb[:, :], di_["actcF"][:, :], writes=[t_consts])
    mk.dma("sync", actcR_sb[:, :], di_["actcR"][:, :], writes=[t_consts])
    mk.dma("sync", row1_sb[:, :], di_["row1"][:, :], writes=[t_consts])
    mk.dma("sync", row1F_sb[:, :], di_["row1F"][:, :], writes=[t_consts])
    mk.dma("sync", btbl_sb[:, :], di_["bias_tbl"][:, :], writes=[t_consts])
    for n in ("q", "k", "v"):
        mk.dma("sync", bias_sb[n][:, :], di_[f"a{n}_bias"][:, :], writes=[t_consts])


    with nc.sbuf_tensor("stage_in", [128, 2 * D], F32) as stage_in:
        t_st = [T("st0"), T("st1")]
        for t in range(NTC):
            sl = t % 2
            st_ap = stage_in[:, sl * D:(sl + 1) * D]
            mk.dma("sync", st_ap, x_tm[128 * t:128 * (t + 1), :], writes=[t_st[sl]])
            for dset in range(4):
                regs = pst(3072, 512)
                mms = []
                for k in range(4):
                    di = dset * 4 + k
                    o = ps[:, 3072 + 128 * k:3072 + 128 * (k + 1)]
                    mms.append((o, st_ap[:, 128 * di:128 * (di + 1)],
                                ident_sb[:, :], True, True, [t_st[sl], t_consts]))
                mk.dep_waits("tensor", [t_st[sl], t_consts], regs)
                mk.cnt["sTE"] += 1
                val = mk.cnt["sTE"]
                s = mk.sems["sTE"]
                for i, (o, inp, idn, _, _, _) in enumerate(mms):
                    if i < 3:
                        def run(o=o, inp=inp, idn=idn):
                            nc.tensor.transpose(o, inp, idn)
                    else:
                        def run(o=o, inp=inp, idn=idn, s=s):
                            nc.tensor.transpose(o, inp, idn).then_inc(s, 1)
                    mk.ops["tensor"].append(run)
                for rg in regs:
                    rg.w = ("tensor", "sTE", val, False, False)
                    rg.readers = []
                t_st[sl].readers.append(("tensor", "sTE", val, False))
                for k in range(4):
                    di = dset * 4 + k
                    mk.op("scalar",
                          lambda o=x_sb[:, TC * di + 128 * t: TC * di + 128 * (t + 1)],
                                 i=ps[:, 3072 + 128 * k:3072 + 128 * (k + 1)]:
                          nc.scalar.copy(out=o, in_=i),
                          reads=regs, writes=[t_x[di]])

    _sq_state = {"n": 0}

    def emit_ln(_unused=None, _unused2=None):
        _sq_state["n"] += 1
        mk.phase_barrier()
        _ctx = nc.sbuf_tensor("sqb%d" % _sq_state["n"], [128, 2 * TC], F32)
        sq_sb = _ctx.__enter__()
        t_sq = [T("sq0"), T("sq1")]
        t_mean = pst(0, 1024)
        t_ex2 = pst(1024, 1024)
        for reg in ((0, 512), (512, 512), (1024, 512), (1536, 512)):
            mk.g_start(reg)
        mean_mms, sq_mms = [], []
        for di in range(NDC):
            sl = di % 2
            mk.op("scalar",
                  lambda o=sq_sb[:, TC * sl:TC * (sl + 1)],
                         i=x_sb[:, TC * di:TC * (di + 1)]:
                  nc.scalar.activation(out=o, in_=i, func=AF.Square),
                  reads=[t_x[di]], writes=[t_sq[sl]])
            for qt in range(2):
                mean_mms = [(ps[0:1, 512 * qt:512 * (qt + 1)],
                             avg_sb[:, 0:1],
                             x_sb[:, TC * di + 512 * qt:TC * di + 512 * (qt + 1)],
                             di == 0, di == NDC - 1, [t_x[di], t_consts])]
                mk.mm_group(t_mean, mean_mms)
                sq_mms = [(ps[0:1, 1024 + 512 * qt:1024 + 512 * (qt + 1)],
                           avg_sb[:, 0:1],
                           sq_sb[:, TC * sl + 512 * qt:TC * sl + 512 * (qt + 1)],
                           di == 0, di == NDC - 1, [t_sq[sl], t_consts])]
                mk.mm_group(t_ex2, sq_mms)
        for reg in ((0, 512), (512, 512), (1024, 512), (1536, 512)):
            mk.g_stop(reg)
        mk.op("scalar", lambda: nc.scalar.activation(
            out=rows_sb[:, 0:TC], in_=ps[0:1, 0:TC], func=AF.Square),
            reads=t_mean, writes=[t_stat])
        mk.op("vector", lambda: nc.vector.tensor_tensor(
            out=rows_sb[:, TC:2 * TC], in0=ps[0:1, 1024:2048],
            in1=rows_sb[:, 0:TC], op=ALU.subtract),
            reads=t_ex2 + [t_stat], writes=[t_stat])
        mk.op("scalar", lambda: nc.scalar.activation(
            out=rows_sb[:, 0:TC], in_=rows_sb[:, TC:2 * TC],
            func=AF.Sqrt, bias=EPS),
            reads=[t_stat], writes=[t_stat])
        mk.op("vector", lambda: nc.vector.reciprocal(
            out=rows_sb[:, 0:TC], in_=rows_sb[:, 0:TC]),
            reads=[t_stat], writes=[t_stat])
        mk.op("vector", lambda: nc.vector.scalar_tensor_tensor(
            out=rows_sb[:, TC:2 * TC], in0=ps[0:1, 0:TC], scalar=-1.0,
            in1=rows_sb[:, 0:TC], op0=ALU.mult, op1=ALU.mult),
            reads=t_mean + [t_stat], writes=[t_nmr])
        t_rbc = pst(2048, 1024)
        t_nbc = pst(3072, 1024)
        rb, nb = [], []
        for qt in range(2):
            rb.append((ps[:, 2048 + 512 * qt:2048 + 512 * (qt + 1)], row1F_sb[0:1, 0:128],
                       rows_sb[:, 512 * qt:512 * (qt + 1)],
                       True, True, [t_stat, t_consts]))
            nb.append((ps[:, 3072 + 512 * qt:3072 + 512 * (qt + 1)], row1F_sb[0:1, 0:128],
                       rows_sb[:, TC + 512 * qt:TC + 512 * (qt + 1)],
                       True, True, [t_nmr, t_consts]))
        mk.mm_group(t_rbc, rb)
        mk.mm_group(t_nbc, nb)
        for di in range(NDC):
            mk.op("vector", lambda o=y_sb[:, TC * di:TC * (di + 1)],
                             i=x_sb[:, TC * di:TC * (di + 1)]:
                  nc.vector.tensor_tensor(out=o, in0=i, in1=ps[:, 2048:3072], op=ALU.mult),
                  reads=[t_x[di]] + t_rbc, writes=[t_y[di]])
            mk.op("vector", lambda o=y_sb[:, TC * di:TC * (di + 1)]:
                  nc.vector.tensor_tensor(out=o, in0=o, in1=ps[:, 3072:4096], op=ALU.add),
                  reads=t_nbc + [t_y[di]], writes=[t_y[di]])
        _ctx.__exit__(None, None, None)

    def emit_attn_a():
        mk.phase_barrier()
        with (
            nc.sbuf_tensor("wbuf", [128, 2 * D], F32R) as wbuf,
            nc.sbuf_tensor("wmbuf", [128, 2 * 512], F32R) as wmbuf,
            nc.sbuf_tensor("aTb", [RANK, D], F32R) as aTb,
            nc.sbuf_tensor("bTb", [128, NDC * RANK], F32R) as bTb,
            nc.sbuf_tensor("ubuf", [RANK, 3 * TC], F32R) as ubuf,
            nc.sbuf_tensor("kvout", [128, 2 * TC], F32R) as kvout,
        ):
            t_w = [T("wb0"), T("wb1")]
            t_wm = [T("wm0"), T("wm1")]
            t_aT = T("aT")
            t_bT = T("bT")
            t_u = {}
            t_kv = [T("kv0"), T("kv1")]
            mk.dma("sync", rowbuf_sb[:, :], di_["av_brow"][:, :], writes=[t_rowbuf])

            for ui, n in enumerate(("q", "k", "v")):
                t_u[n] = T(f"u_{n}")
                mk.dma("sync", bTb[:, :], di_[f"a{n}_bT"][:, :], writes=[t_bT])
                mms = []
                for di in range(NDC):
                    for qt in range(2):
                        mms.append((ps[0:RANK, 512 * qt:512 * (qt + 1)],
                                    bTb[:, RANK * di:RANK * (di + 1)],
                                    y_sb[:, TC * di + 512 * qt:TC * di + 512 * (qt + 1)],
                                    di == 0, di == NDC - 1, [t_bT, t_y[di]]))
                tp = pst(0, 1024)
                mk.g_start((0, 512)); mk.g_start((512, 512))
                mk.mm_group(tp, mms)
                mk.g_stop((0, 512)); mk.g_stop((512, 512))
                mk.op("scalar", lambda o=ubuf[:, TC * ui:TC * (ui + 1)],
                                 i=ps[0:RANK, 0:TC]:
                      nc.scalar.activation(out=o, in_=i, func=AF.Copy, scale=LORA_SC),
                      reads=tp, writes=[t_u[n]])

            kpos = [0, 0]
            for hg in range(4):
                mk.dma("sync", aTb[:, :], di_["ak_aT"][:, :], writes=[t_aT])
                for j in range(4):
                    d = 4 * hg + j
                    sl = d % 2
                    base = 1024 * sl
                    mk.dma("sync", wbuf[:, D * sl:D * (sl + 1)],
                           di_["ak_w"][d, :, :], writes=[t_w[sl]])
                    tp = pst(base, 1024)
                    mk.g_start((base, 512)); mk.g_start((base + 512, 512))
                    mms = []
                    for di in range(NDC):
                        for qt in range(2):
                            mms.append((ps[:, base + 512 * qt:base + 512 * (qt + 1)],
                                        wbuf[:, D * sl + 128 * di:D * sl + 128 * (di + 1)],
                                        y_sb[:, TC * di + 512 * qt:TC * di + 512 * (qt + 1)],
                                        di == 0, False, [t_w[sl], t_y[di]]))
                    for qt in range(2):
                        mms.append((ps[:, base + 512 * qt:base + 512 * (qt + 1)],
                                    aTb[0:RANK, 128 * d:128 * (d + 1)],
                                    ubuf[0:RANK, TC * 1 + 512 * qt:TC * 1 + 512 * (qt + 1)],
                                    False, True, [t_aT, t_u["k"]]))
                    mk.mm_group(tp, mms)
                    mk.g_stop((base, 512)); mk.g_stop((base + 512, 512))
                    mk.op("scalar", lambda o=kvout[:, TC * sl:TC * (sl + 1)],
                                     i=ps[:, base:base + TC], b=bias_sb["k"][:, d:d + 1]:
                          nc.scalar.activation(out=o, in_=i, func=AF.Identity, bias=b),
                          reads=tp, writes=[t_kv[sl]])
                    mk.dma("gpsimd", cca_in[hg][128 * j:128 * (j + 1), :],
                           kvout[:, TC * sl:TC * (sl + 1)],
                           reads=[t_kv[sl]], writes=[t_cca_in[hg]])
                mk.dma("sync", aTb[:, :], di_["av_aT"][:, :], writes=[t_aT])
                tvs = [pst(512 * t, 512) for t in range(8)]
                for t in range(8):
                    mk.g_start((512 * t, 512))
                for di in range(NDC):
                    sl = di % 2
                    mk.dma("sync", wmbuf[:, 512 * sl:512 * (sl + 1)],
                           di_["av_wm"][128 * di:128 * (di + 1), 512 * hg:512 * (hg + 1)],
                           writes=[t_wm[sl]])
                    mms = []
                    for t in range(8):
                        mms.append((ps[:, 512 * t:512 * (t + 1)],
                                    y_sb[:, TC * di + 128 * t:TC * di + 128 * (t + 1)],
                                    wmbuf[:, 512 * sl:512 * (sl + 1)],
                                    di == 0, False, [t_wm[sl], t_y[di]]))
                    mk.mm_group(pst(0, 4096), mms)
                for t in range(8):
                    mms = [(ps[:, 512 * t:512 * (t + 1)],
                            ubuf[0:RANK, TC * 2 + 128 * t:TC * 2 + 128 * (t + 1)],
                            aTb[0:RANK, 512 * hg:512 * (hg + 1)],
                            False, False, [t_aT, t_u["v"]]),
                           (ps[:, 512 * t:512 * (t + 1)],
                            row1_sb[0:1, 0:128],
                            rowbuf_sb[0:1, 512 * hg:512 * (hg + 1)],
                            False, True, [t_consts, t_rowbuf])]
                    mk.mm_group(tvs[t] if t > 0 else tvs[0], mms)
                for t in range(8):
                    mk.g_stop((512 * t, 512))
                    sl = t % 2
                    mk.op("scalar", lambda o=kvout[:, TC * sl + 512 * 0:TC * sl + 512],
                                     i=ps[:, 512 * t:512 * (t + 1)]:
                          nc.scalar.copy(out=o, in_=i),
                          reads=tvs[t], writes=[t_kv[sl]])
                    dst = cca_in[hg].rearrange("a b -> (a b)")[
                        512 * TC + t * 65536: 512 * TC + (t + 1) * 65536]
                    mk.dma("gpsimd", dst.rearrange("(p c) -> p c", p=128),
                           kvout[:, TC * sl:TC * sl + 512],
                           reads=[t_kv[sl]], writes=[t_cca_in[hg]])
                mk.op("gpsimd", lambda hg=hg: nc.gpsimd.collective_compute(
                    "AllGather", ALU.bypass,
                    replica_groups=[[0, 1, 2, 3], [4, 5, 6, 7]],
                    ins=[cca_in[hg].rearrange("a b -> (a b)").opt()],
                    outs=[cca_out[hg].rearrange("a b -> (a b)").opt()]),
                    reads=[t_cca_in[hg]], writes=[t_cca_out[hg]],
                    sem="sCC", is_async=True)

            mk.dma("sync", aTb[:, :], di_["aq_aT"][:, :], writes=[t_aT])
            for d in range(NDC):
                sl = d % 2
                base = 1024 * sl
                mk.dma("sync", wbuf[:, D * sl:D * (sl + 1)], di_["aq_w"][d, :, :],
                       writes=[t_w[sl]])
                tp = pst(base, 1024)
                mk.g_start((base, 512)); mk.g_start((base + 512, 512))
                mms = []
                for di in range(NDC):
                    for qt in range(2):
                        mms.append((ps[:, base + 512 * qt:base + 512 * (qt + 1)],
                                    wbuf[:, D * sl + 128 * di:D * sl + 128 * (di + 1)],
                                    y_sb[:, TC * di + 512 * qt:TC * di + 512 * (qt + 1)],
                                    di == 0, False, [t_w[sl], t_y[di]]))
                for qt in range(2):
                    mms.append((ps[:, base + 512 * qt:base + 512 * (qt + 1)],
                                aTb[0:RANK, 128 * d:128 * (d + 1)],
                                ubuf[0:RANK, 512 * qt:512 * (qt + 1)],
                                False, True, [t_aT, t_u["q"]]))
                mk.mm_group(tp, mms)
                mk.g_stop((base, 512)); mk.g_stop((base + 512, 512))
                mk.op("scalar", lambda o=kvout[:, TC * sl:TC * (sl + 1)],
                                 i=ps[:, base:base + TC], b=bias_sb["q"][:, d:d + 1]:
                      nc.scalar.activation(out=o, in_=i, func=AF.Identity, bias=b),
                      reads=tp, writes=[t_kv[sl]])
                mk.dma("gpsimd", qa_sp[128 * d:128 * (d + 1), :],
                       kvout[:, TC * sl:TC * (sl + 1)],
                       reads=[t_kv[sl]], writes=[t_qa])

        if _DBG:
            mk.dma("gpsimd", dbg["dbg_q"][:, :], qa_sp[:, :], reads=[t_qa], writes=[])
            mk.dma("gpsimd", dbg["dbg_kv"][:, :], cca_in[0][:, :],
                   reads=[t_cca_in[0]], writes=[])

        t_ao = [T(f"ao{i}") for i in range(NDC)]
        mk.phase_barrier()
        with (
            nc.sbuf_tensor("kbuf", [128, 2 * 2048], F32R) as kbuf,
            nc.sbuf_tensor("vbuf", [128, 2 * 2048], F32R) as vbuf,
            nc.sbuf_tensor("qbuf", [128, 2 * TC], F32R) as qbuf,
            nc.sbuf_tensor("pbuf", [128, 2 * TC], F32R) as pbuf,
        ):
            t_k = [T("k0"), T("k1")]
            t_v = [T("v0"), T("v1")]
            t_q = [T("q0"), T("q1")]
            t_p = [T("p0"), T("p1")]
            t_out = pst(2048, 1024)
            t_l = pst(3072, 1024)

            def load_kv_half(h, hf, sl):
                hg, hh = h // 4, h % 4
                for r2 in range(2):
                    r = 2 * hf + r2
                    mk.dma("sync", kbuf[:, 2048 * sl + 1024 * r2:2048 * sl + 1024 * (r2 + 1)],
                           cca_out[hg][1024 * r + 128 * hh:1024 * r + 128 * (hh + 1), :],
                           reads=[t_cca_out[hg]], writes=[t_k[sl]])
                    src = cca_out[hg].rearrange("a b -> (a b)")
                    base = (1024 * r + 512) * TC + 128 * hh
                    sap = bass.AP(src.tensor, src.offset + base,
                                  [[512, 128], [65536, 8], [1, 128]])
                    dap = bass.AP(vbuf.ap().tensor, vbuf.ap().offset + 2048 * sl + 1024 * r2,
                                  [[4096, 128], [128, 8], [1, 128]])
                    mk.dma("sync", dap, sap, reads=[t_cca_out[hg]], writes=[t_v[sl]])

            for h in range(HEADS):
                qsl = h % 2
                mk.dma("sync", qbuf[:, TC * qsl:TC * (qsl + 1)],
                       qa_sp[128 * h:128 * (h + 1), :], reads=[t_qa], writes=[t_q[qsl]])
                if h == 0:
                    load_kv_half(0, 0, 0)
                t_pu = [T(f"pu{u}") for u in range(4)]
                for kc in range(32):
                    hf = kc // 16
                    sl = hf
                    if kc % 16 == 0:
                        if kc == 0:
                            load_kv_half(h, 1, 1)
                        else:
                            if h < HEADS - 1:
                                load_kv_half(h + 1, 0, 0)
                    kk = kc % 16
                    for qt in range(2):
                        u = (2 * kc + qt) % 4
                        base = 512 * u
                        tp = pst(base, 512)
                        mk.mm_group(tp, [(ps[:, base:base + 512],
                                          kbuf[:, 2048 * sl + 128 * kk:2048 * sl + 128 * (kk + 1)],
                                          qbuf[:, TC * qsl + 512 * qt:TC * qsl + 512 * (qt + 1)],
                                          True, True, [t_k[sl], t_q[qsl]])])
                        mk.op("scalar", lambda o=pbuf[:, 512 * u:512 * (u + 1)],
                                         i=ps[:, base:base + 512]:
                              nc.scalar.activation(out=o, in_=i, func=AF.Exp,
                                                   scale=SCALE, bias=-BEXP),
                              reads=tp, writes=[t_pu[u]])
                        mk.mm_group(t_l, [(ps[0:2, 3072 + 512 * qt:3584 + 512 * qt],
                                           cst_sb[:, 0:2],
                                           pbuf[:, 512 * u:512 * (u + 1)],
                                           kc == 0, kc == 31, [t_pu[u], t_consts])])
                        mk.mm_group(t_out, [(ps[:, 2048 + 512 * qt:2048 + 512 * (qt + 1)],
                                             vbuf[:, 2048 * sl + 128 * kk:2048 * sl + 128 * (kk + 1)],
                                             pbuf[:, 512 * u:512 * (u + 1)],
                                             kc == 0, kc == 31, [t_v[sl], t_pu[u]])])
                emit_norm_head(h, t_out, t_l, t_ao[h])

        if _DBG:
            mk.op("scalar", lambda: nc.scalar.copy(out=bc_sb[:, 0:128], in_=ps[:, 3072:3200]),
                  reads=pst(3072, 512), writes=[t_bc])
            mk.dma("gpsimd", dbg["dbg_l"][:, :], bc_sb[:, 0:128], reads=[t_bc], writes=[])
            for di in range(NDC):
                mk.dma("gpsimd", dbg["dbg_ao"][128 * di:128 * (di + 1), :],
                       y_sb[:, TC * di:TC * (di + 1)], reads=[t_ao[di]], writes=[])

        emit_oproj_a(t_ao)

    def emit_norm_head(h, t_out, t_l, t_aoh):
        mk.op("scalar", lambda: nc.scalar.activation(
            out=rows_sb[:, TC:2 * TC], in_=ps[0:1, 3072:4096], func=AF.Ln),
            reads=t_l, writes=[t_stat])
        mk.op("scalar", lambda: nc.scalar.activation(
            out=rows_sb[:, 0:TC], in_=rows_sb[:, TC:2 * TC], func=AF.Exp, scale=-1.0),
            reads=[t_stat], writes=[t_lrow])
        for qt in range(2):
            t_b = pst(512 * qt, 512)
            mk.mm_group(t_b, [(ps[:, 512 * qt:512 * (qt + 1)], row1F_sb[0:1, 0:128],
                               rows_sb[0:1, 512 * qt:512 * (qt + 1)],
                               True, True, [t_lrow, t_consts])])
            mk.op("scalar", lambda qt=qt: nc.scalar.copy(
                out=bc_sb[:, :], in_=ps[:, 512 * qt:512 * (qt + 1)]),
                reads=t_b, writes=[t_bc])
            mk.op("vector",
                  lambda o=y_sb[:, TC * h + 512 * qt:TC * h + 512 * (qt + 1)],
                         i=ps[:, 2048 + 512 * qt:2048 + 512 * (qt + 1)]:
                  nc.vector.tensor_tensor(out=o, in0=i, in1=bc_sb[:, :], op=ALU.mult),
                  reads=pst(2048, 1024) + [t_bc], writes=[t_aoh])

    def emit_oproj_a(t_ao):
        mk.phase_barrier()
        with (
            nc.sbuf_tensor("wbuf2", [128, 2 * D], F32R) as wbuf,
            nc.sbuf_tensor("aTb2", [RANK, D], F32R) as aTb,
            nc.sbuf_tensor("bTb2", [128, NDC * RANK], F32R) as bTb,
            nc.sbuf_tensor("ubuf2", [RANK, TC], F32R) as ubuf,
        ):
            t_w = [T("w20"), T("w21")]
            t_aT = T("aT2")
            t_bT = T("bT2")
            t_u = T("u_o")
            mk.dma("sync", rowbuf_sb[:, :], di_["ao_brow"][:, :], writes=[t_rowbuf])
            mk.dma("sync", bTb[:, :], di_["ao_bT"][:, :], writes=[t_bT])
            mk.dma("sync", aTb[:, :], di_["ao_aT"][:, :], writes=[t_aT])
            mms = []
            for di in range(NDC):
                for qt in range(2):
                    mms.append((ps[0:RANK, 512 * qt:512 * (qt + 1)],
                                bTb[:, RANK * di:RANK * (di + 1)],
                                y_sb[:, TC * di + 512 * qt:TC * di + 512 * (qt + 1)],
                                di == 0, di == NDC - 1, [t_bT, t_ao[di]]))
            tp = pst(0, 1024)
            mk.g_start((0, 512)); mk.g_start((512, 512))
            mk.mm_group(tp, mms)
            mk.g_stop((0, 512)); mk.g_stop((512, 512))
            mk.op("scalar", lambda: nc.scalar.activation(
                out=ubuf[:, :], in_=ps[0:RANK, 0:TC], func=AF.Copy, scale=LORA_SC),
                reads=tp, writes=[t_u])
            for d in range(NDC):
                sl = d % 2
                base = 1024 * sl
                mk.dma("sync", wbuf[:, D * sl:D * (sl + 1)], di_["ao_w"][d, :, :],
                       writes=[t_w[sl]])
                tp = pst(base, 1024)
                mk.g_start((base, 512)); mk.g_start((base + 512, 512))
                mms = []
                for di in range(NDC):
                    for qt in range(2):
                        mms.append((ps[:, base + 512 * qt:base + 512 * (qt + 1)],
                                    wbuf[:, D * sl + 128 * di:D * sl + 128 * (di + 1)],
                                    y_sb[:, TC * di + 512 * qt:TC * di + 512 * (qt + 1)],
                                    di == 0, False, [t_w[sl], t_ao[di]]))
                for qt in range(2):
                    mms.append((ps[:, base + 512 * qt:base + 512 * (qt + 1)],
                                aTb[0:RANK, 128 * d:128 * (d + 1)],
                                ubuf[0:RANK, 512 * qt:512 * (qt + 1)],
                                False, False, [t_aT, t_u]))
                    mms.append((ps[:, base + 512 * qt:base + 512 * (qt + 1)],
                                rowbuf_sb[0:1, 128 * d:128 * (d + 1)],
                                row1_sb[0:1, 0:512],
                                False, True, [t_rowbuf, t_consts]))
                mk.mm_group(tp, mms)
                mk.g_stop((base, 512)); mk.g_stop((base + 512, 512))
                mk.op("vector", lambda o=x_sb[:, TC * d:TC * (d + 1)],
                                 i=ps[:, base:base + TC]:
                      nc.vector.tensor_tensor(out=o, in0=o, in1=i, op=ALU.add),
                      reads=tp + [t_x[d]], writes=[t_x[d]])

    _ffn_n = [0]

    def emit_ffn():
        _ffn_n[0] += 1
        sfx = str(_ffn_n[0])
        mk.phase_barrier()
        with (
            nc.sbuf_tensor("wgb" + sfx, [128, 2 * D], F32R) as wgb,
            nc.sbuf_tensor("wub" + sfx, [128, 2 * D], F32R) as wub,
            nc.sbuf_tensor("ssb" + sfx, [128, 2 * TC], F32R) as ssb,
            nc.sbuf_tensor("hb" + sfx, [128, 2 * TC], F32R) as hb,
        ):
            t_wg = [T("wg0"), T("wg1")]
            t_wu = [T("wu0"), T("wu1")]
            t_ss = [T("ss0"), T("ss1")]
            t_hb = [T("hb0"), T("hb1")]
            for hc in range(NHC):
                sl = hc % 2
                gb = 2048 * sl
                ub = 2048 * sl + 1024
                mk.dma("sync", wgb[:, D * sl:D * (sl + 1)], di_["fg_w"][hc, :, :],
                       writes=[t_wg[sl]])
                mk.dma("sync", wub[:, D * sl:D * (sl + 1)], di_["fu_w"][hc, :, :],
                       writes=[t_wu[sl]])
                tg = pst(gb, 1024)
                tu = pst(ub, 1024)
                for b0, tt, wb, t_wx in ((gb, tg, wgb, t_wg), (ub, tu, wub, t_wu)):
                    mk.g_start((b0, 512)); mk.g_start((b0 + 512, 512))
                    mms = []
                    for di in range(NDC):
                        for qt in range(2):
                            mms.append((ps[:, b0 + 512 * qt:b0 + 512 * (qt + 1)],
                                        wb[:, D * sl + 128 * di:D * sl + 128 * (di + 1)],
                                        y_sb[:, TC * di + 512 * qt:TC * di + 512 * (qt + 1)],
                                        di == 0, di == NDC - 1, [t_wx[sl], t_y[di]]))
                    mk.mm_group(tt, mms)
                    mk.g_stop((b0, 512)); mk.g_stop((b0 + 512, 512))
                mk.op("scalar", lambda o=ssb[:, TC * sl:TC * (sl + 1)],
                                 i=ps[:, gb:gb + TC]:
                      nc.scalar.activation(out=o, in_=i, func=AF.Silu),
                      reads=tg, writes=[t_ss[sl]])
                mk.op("vector", lambda o=hb[:, TC * sl:TC * (sl + 1)],
                                 i0=ssb[:, TC * sl:TC * (sl + 1)], i1=ps[:, ub:ub + TC]:
                      nc.vector.tensor_tensor(out=o, in0=i0, in1=i1, op=ALU.mult),
                      reads=[t_ss[sl]] + tu, writes=[t_hb[sl]])
                mk.dma("gpsimd", h_sp[128 * hc:128 * (hc + 1), :],
                       hb[:, TC * sl:TC * (sl + 1)],
                       reads=[t_hb[sl]], writes=[t_h[hc]])
        mk.phase_barrier()
        with (
            nc.sbuf_tensor("hrb" + sfx, [128, 2 * TC], F32R) as hrb,
            nc.sbuf_tensor("wdb" + sfx, [128, 2 * 512], F32R) as wdb,
        ):
            t_hr = [T("hr0"), T("hr1")]
            t_wd = [T("wd0"), T("wd1")]
            for dq in range(4):
                tps = [pst(1024 * dc, 1024) for dc in range(4)]
                for dc in range(4):
                    mk.g_start((1024 * dc, 512)); mk.g_start((1024 * dc + 512, 512))
                for hc in range(NHC):
                    sl = hc % 2
                    mk.dma("scalar", hrb[:, TC * sl:TC * (sl + 1)],
                           h_sp[128 * hc:128 * (hc + 1), :],
                           reads=[t_h[hc]], writes=[t_hr[sl]])
                    mk.dma("scalar", wdb[:, 512 * sl:512 * (sl + 1)],
                           di_["fd_w"][dq, hc, :, :], writes=[t_wd[sl]])
                    for dc in range(4):
                        mms = []
                        for qt in range(2):
                            mms.append((ps[:, 1024 * dc + 512 * qt:1024 * dc + 512 * (qt + 1)],
                                        wdb[:, 512 * sl + 128 * dc:512 * sl + 128 * (dc + 1)],
                                        hrb[:, TC * sl + 512 * qt:TC * sl + 512 * (qt + 1)],
                                        hc == 0, hc == NHC - 1, [t_wd[sl], t_hr[sl]]))
                        mk.mm_group(tps[dc], mms)
                for dc in range(4):
                    mk.g_stop((1024 * dc, 512)); mk.g_stop((1024 * dc + 512, 512))
                    d = 4 * dq + dc
                    mk.op("vector", lambda o=x_sb[:, TC * d:TC * (d + 1)],
                                     i=ps[:, 1024 * dc:1024 * dc + TC]:
                          nc.vector.tensor_tensor(out=o, in0=o, in1=i, op=ALU.add),
                          reads=tps[dc] + [t_x[d]], writes=[t_x[d]])

    def emit_attn_b():
        mk.phase_barrier()
        with (
            nc.sbuf_tensor("wbuf3", [128, 2 * D], F32R) as wbuf,
            nc.sbuf_tensor("wmb3", [128, 2 * 512], F32R) as wmb,
            nc.sbuf_tensor("cosb", [128, TC], F32) as cosb,
            nc.sbuf_tensor("sinb", [128, TC], F32) as sinb,
            nc.sbuf_tensor("ktmp", [128, 2 * TC], F32R) as ktmp,
            nc.sbuf_tensor("krot", [128, 2 * TC], F32R) as krot,
        ):
            t_w = [T("w30"), T("w31")]
            t_wm = [T("wm30"), T("wm31")]
            t_cos = T("cos")
            t_ktmp = [T("kt0"), T("kt1")]
            t_krot = [T("kr0"), T("kr1")]
            mk.dma("sync", cosb[:, :], di_["rope_cos"][:, :], writes=[t_cos])
            mk.dma("sync", sinb[:, :], di_["rope_sinS"][:, :], writes=[t_cos])

            def rope_chunk(sl, tp_in, base):
                tsh = pst(2048, 1024)
                mk.g_start((2048, 512)); mk.g_start((2560, 512))
                mms = []
                for qt in range(2):
                    mms.append((ps[:, 2048 + 512 * qt:2048 + 512 * (qt + 1)],
                                perm_sb[:, :],
                                ktmp[:, TC * sl + 512 * qt:TC * sl + 512 * (qt + 1)],
                                True, True, [t_consts, t_ktmp[sl]]))
                mk.mm_group(tsh, mms)
                mk.g_stop((2048, 512)); mk.g_stop((2560, 512))
                mk.op("vector", lambda o=krot[:, TC * sl:TC * (sl + 1)],
                                 i=ktmp[:, TC * sl:TC * (sl + 1)]:
                      nc.vector.tensor_tensor(out=o, in0=i, in1=cosb[:, :], op=ALU.mult),
                      reads=[t_ktmp[sl], t_cos], writes=[t_krot[sl]])
                mk.op("vector", lambda o=ktmp[:, TC * sl:TC * (sl + 1)],
                                 i=ps[:, 2048:2048 + TC]:
                      nc.vector.scalar_tensor_tensor(out=o, in0=i, scalar=1.0,
                                                     in1=sinb[:, :],
                                                     op0=ALU.mult, op1=ALU.mult),
                      reads=tsh + [t_cos], writes=[t_ktmp[sl]])
                mk.op("vector", lambda o=krot[:, TC * sl:TC * (sl + 1)],
                                 i=ktmp[:, TC * sl:TC * (sl + 1)]:
                      nc.vector.tensor_tensor(out=o, in0=o, in1=i, op=ALU.add),
                      reads=[t_ktmp[sl], t_krot[sl]], writes=[t_krot[sl]])

            for dkv in range(NKV):
                sl = dkv % 2
                base = 1024 * sl
                mk.dma("sync", wbuf[:, D * sl:D * (sl + 1)], di_["bk_w"][dkv, :, :],
                       writes=[t_w[sl]])
                tp = pst(base, 1024)
                mk.g_start((base, 512)); mk.g_start((base + 512, 512))
                mms = []
                for di in range(NDC):
                    for qt in range(2):
                        mms.append((ps[:, base + 512 * qt:base + 512 * (qt + 1)],
                                    wbuf[:, D * sl + 128 * di:D * sl + 128 * (di + 1)],
                                    y_sb[:, TC * di + 512 * qt:TC * di + 512 * (qt + 1)],
                                    di == 0, di == NDC - 1, [t_w[sl], t_y[di]]))
                mk.mm_group(tp, mms)
                mk.g_stop((base, 512)); mk.g_stop((base + 512, 512))
                mk.op("scalar", lambda o=ktmp[:, TC * sl:TC * (sl + 1)],
                                 i=ps[:, base:base + TC]:
                      nc.scalar.copy(out=o, in_=i), reads=tp, writes=[t_ktmp[sl]])
                rope_chunk(sl, tp, base)
                mk.dma("gpsimd", ccb_in[128 * dkv:128 * (dkv + 1), :],
                       krot[:, TC * sl:TC * (sl + 1)],
                       reads=[t_krot[sl]], writes=[t_ccb_in])
                mk.dma("gpsimd", kloc_sp[128 * dkv:128 * (dkv + 1), :],
                       krot[:, TC * sl:TC * (sl + 1)],
                       reads=[t_krot[sl]], writes=[t_kloc])
            tvs = [pst(512 * t, 512) for t in range(8)]
            for t in range(8):
                mk.g_start((512 * t, 512))
            for di in range(NDC):
                sl = di % 2
                mk.dma("sync", wmb[:, 512 * sl:512 * (sl + 1)],
                       di_["bv_wm"][128 * di:128 * (di + 1), :], writes=[t_wm[sl]])
                mms = []
                for t in range(8):
                    mms.append((ps[:, 512 * t:512 * (t + 1)],
                                y_sb[:, TC * di + 128 * t:TC * di + 128 * (t + 1)],
                                wmb[:, 512 * sl:512 * (sl + 1)],
                                di == 0, di == NDC - 1, [t_wm[sl], t_y[di]]))
                mk.mm_group(pst(0, 4096), mms)
            for t in range(8):
                mk.g_stop((512 * t, 512))
                sl = t % 2
                mk.op("scalar", lambda o=ktmp[:, TC * sl:TC * sl + 512],
                                 i=ps[:, 512 * t:512 * (t + 1)]:
                      nc.scalar.copy(out=o, in_=i),
                      reads=tvs[t], writes=[t_ktmp[sl]])
                dst = ccb_in.rearrange("a b -> (a b)")[
                    512 * TC + t * 65536: 512 * TC + (t + 1) * 65536]
                mk.dma("gpsimd", dst.rearrange("(p c) -> p c", p=128),
                       ktmp[:, TC * sl:TC * sl + 512],
                       reads=[t_ktmp[sl]], writes=[t_ccb_in])
                mk.dma("gpsimd", vloc_sp[128 * t:128 * (t + 1), :],
                       ktmp[:, TC * sl:TC * sl + 512],
                       reads=[t_ktmp[sl]], writes=[t_vloc])
            mk.op("gpsimd", lambda: nc.gpsimd.collective_compute(
                "AllGather", ALU.bypass,
                replica_groups=[[0, 1, 2, 3], [4, 5, 6, 7]],
                ins=[ccb_in.rearrange("a b -> (a b)").opt()],
                outs=[ccb_out.rearrange("a b -> (a b)").opt()]),
                reads=[t_ccb_in], writes=[t_ccb_out], sem="sCC", is_async=True)
            for d in range(NDC):
                sl = d % 2
                base = 1024 * sl
                mk.dma("sync", wbuf[:, D * sl:D * (sl + 1)], di_["bq_w"][d, :, :],
                       writes=[t_w[sl]])
                tp = pst(base, 1024)
                mk.g_start((base, 512)); mk.g_start((base + 512, 512))
                mms = []
                for di in range(NDC):
                    for qt in range(2):
                        mms.append((ps[:, base + 512 * qt:base + 512 * (qt + 1)],
                                    wbuf[:, D * sl + 128 * di:D * sl + 128 * (di + 1)],
                                    y_sb[:, TC * di + 512 * qt:TC * di + 512 * (qt + 1)],
                                    di == 0, di == NDC - 1, [t_w[sl], t_y[di]]))
                mk.mm_group(tp, mms)
                mk.g_stop((base, 512)); mk.g_stop((base + 512, 512))
                mk.op("scalar", lambda o=ktmp[:, TC * sl:TC * (sl + 1)],
                                 i=ps[:, base:base + TC]:
                      nc.scalar.copy(out=o, in_=i), reads=tp, writes=[t_ktmp[sl]])
                rope_chunk(sl, tp, base)
                mk.dma("gpsimd", qb_sp[128 * d:128 * (d + 1), :],
                       krot[:, TC * sl:TC * (sl + 1)],
                       reads=[t_krot[sl]], writes=[t_qb])

        mk.phase_barrier()
        with (
            nc.sbuf_tensor("kbufb", [128, TF], F32R) as kbufb,
            nc.sbuf_tensor("vbufb", [128, TF], F32R) as vbufb,
            nc.sbuf_tensor("qbufb", [128, TC], F32R) as qbufb,
            nc.sbuf_tensor("pbufb", [128, 2 * TC], F32R) as pbufb,
            nc.sbuf_tensor("kband", [128, 512], F32R) as kband,
            nc.sbuf_tensor("vband", [128, 512], F32R) as vband,
            nc.sbuf_tensor("dmb", [128, 2048], F32) as dmb,
        ):
            t_kb = T("kbB")
            t_vb = T("vbB")
            t_q = [T("qB0"), T("qB1")]
            t_p = [T("pB0"), T("pB1")]
            t_kbd = [T("kbd0"), T("kbd1")]
            t_vbd = [T("vbd0"), T("vbd1")]
            t_dm = T("dmask")
            t_out = pst(2048, 1024)
            t_l = pst(3072, 1024)
            t_ao = [T(f"aob{i}") for i in range(NDC)]
            for _j in range(4):
                mk.dma("sync", dmb[:, 512 * _j:512 * (_j + 1)],
                       di_["dmask"][128 * _j:128 * (_j + 1), :], writes=[t_dm])

            def load_group(g):
                for r in range(4):
                    mk.dma("sync", kbufb[:, TC * r:TC * (r + 1)],
                           ccb_out[TC * r + 128 * g:TC * r + 128 * (g + 1), :],
                           reads=[t_ccb_out], writes=[t_kb])
                    src = ccb_out.rearrange("a b -> (a b)")
                    base = (TC * r + 512) * TC + 128 * g
                    sap = bass.AP(src.tensor, src.offset + base,
                                  [[512, 128], [65536, 8], [1, 128]])
                    dap = bass.AP(vbufb.ap().tensor, vbufb.ap().offset + TC * r,
                                  [[TF, 128], [128, 8], [1, 128]])
                    mk.dma("sync", dap, sap, reads=[t_ccb_out], writes=[t_vb])

            for h in range(HEADS):
                g = h // 4
                qsl = 0
                if h % 4 == 0:
                    load_group(g)
                mk.dma("sync", qbufb[:, 0:TC],
                       qb_sp[128 * h:128 * (h + 1), :], reads=[t_qb], writes=[t_q[0]])
                t_pu = [T(f"pub{u}") for u in range(4)]
                for kc in range(32):
                    for qt in range(2):
                        u = (2 * kc + qt) % 4
                        base = 512 * u
                        tp = pst(base, 512)
                        mk.mm_group(tp, [(ps[:, base:base + 512],
                                          kbufb[:, 128 * kc:128 * (kc + 1)],
                                          qbufb[:, 512 * qt:512 * (qt + 1)],
                                          True, True, [t_kb, t_q[qsl]])])
                        mk.op("scalar",
                              lambda o=pbufb[:, 512 * u:512 * (u + 1)],
                                     i=ps[:, base:base + 512],
                                     b=btbl_sb[:, 32 * qt + kc:32 * qt + kc + 1]:
                              nc.scalar.activation(out=o, in_=i, func=AF.Exp,
                                                   scale=SCALE, bias=b),
                              reads=tp + [t_consts], writes=[t_pu[u]])
                        mk.mm_group(t_l, [(ps[0:2, 3072 + 512 * qt:3584 + 512 * qt],
                                           cst_sb[:, 0:2],
                                           pbufb[:, 512 * u:512 * (u + 1)],
                                           kc == 0, False, [t_pu[u], t_consts])])
                        mk.mm_group(t_out, [(ps[:, 2048 + 512 * qt:2048 + 512 * (qt + 1)],
                                             vbufb[:, 128 * kc:128 * (kc + 1)],
                                             pbufb[:, 512 * u:512 * (u + 1)],
                                             kc == 0, False, [t_vb, t_pu[u]])])
                for qt in range(2):
                    bsl = 0
                    mk.dma("sync", kband[:, 0:512],
                           kloc_sp[128 * g:128 * (g + 1), 512 * qt:512 * (qt + 1)],
                           reads=[t_kloc], writes=[t_kbd[bsl]])
                    src = vloc_sp.rearrange("a b -> (a b)")
                    base_v = (512 * qt) * KVD + 128 * g
                    sap = bass.AP(src.tensor, src.offset + base_v,
                                  [[KVD, 128], [128 * KVD, 4], [1, 128]])
                    dap = bass.AP(vband.ap().tensor, vband.ap().offset,
                                  [[512, 128], [128, 4], [1, 128]])
                    mk.dma("sync", dap, sap, reads=[t_vloc], writes=[t_vbd[bsl]])
                    for j in range(4):
                        u = j % 4
                        base = 512 * u
                        tp = pst(base, 512)
                        mk.mm_group(tp, [(ps[:, base:base + 512],
                                          kband[:, 128 * j:128 * (j + 1)],
                                          qbufb[:, 512 * qt:512 * (qt + 1)],
                                          True, True, [t_kbd[bsl], t_q[qsl]])])
                        mk.op("vector", lambda o=ps[:, base:base + 512],
                                         m=dmb[:, 512 * j:512 * (j + 1)]:
                              nc.vector.tensor_tensor(out=o, in0=o, in1=m, op=ALU.add),
                              reads=tp + [t_dm], writes=tp)
                        mk.op("scalar",
                              lambda o=pbufb[:, 512 * u:512 * (u + 1)],
                                     i=ps[:, base:base + 512]:
                              nc.scalar.activation(out=o, in_=i, func=AF.Exp,
                                                   scale=SCALE, bias=-BEXP),
                              reads=tp, writes=[t_pu[u]])
                        mk.mm_group(t_l, [(ps[0:2, 3072 + 512 * qt:3584 + 512 * qt],
                                           cst_sb[:, 0:2],
                                           pbufb[:, 512 * u:512 * (u + 1)],
                                           False, j == 3, [t_pu[u], t_consts])])
                        mk.mm_group(t_out, [(ps[:, 2048 + 512 * qt:2048 + 512 * (qt + 1)],
                                             vband[:, 128 * j:128 * (j + 1)],
                                             pbufb[:, 512 * u:512 * (u + 1)],
                                             False, j == 3, [t_vbd[bsl], t_pu[u]])])
                emit_norm_head(h, t_out, t_l, t_ao[h])

        mk.phase_barrier()
        with nc.sbuf_tensor("wbuf4", [128, 2 * D], F32R) as wbuf:
            t_w = [T("w40"), T("w41")]
            for d in range(NDC):
                sl = d % 2
                base = 1024 * sl
                mk.dma("sync", wbuf[:, D * sl:D * (sl + 1)], di_["bp_w"][d, :, :],
                       writes=[t_w[sl]])
                tp = pst(base, 1024)
                mk.g_start((base, 512)); mk.g_start((base + 512, 512))
                mms = []
                for di in range(NDC):
                    for qt in range(2):
                        mms.append((ps[:, base + 512 * qt:base + 512 * (qt + 1)],
                                    wbuf[:, D * sl + 128 * di:D * sl + 128 * (di + 1)],
                                    y_sb[:, TC * di + 512 * qt:TC * di + 512 * (qt + 1)],
                                    di == 0, di == NDC - 1, [t_w[sl], t_ao[di]]))
                mk.mm_group(tp, mms)
                mk.g_stop((base, 512)); mk.g_stop((base + 512, 512))
                mk.op("vector", lambda o=x_sb[:, TC * d:TC * (d + 1)],
                                 i=ps[:, base:base + TC]:
                      nc.vector.tensor_tensor(out=o, in0=o, in1=i, op=ALU.add),
                      reads=tp + [t_x[d]], writes=[t_x[d]])

    if _STAGES >= 1:
        emit_ln()
        if _DBG:
            for di in range(NDC):
                mk.dma("gpsimd", dbg["dbg_y"][128 * di:128 * (di + 1), :],
                       y_sb[:, TC * di:TC * (di + 1)], reads=[t_y[di]], writes=[])
        emit_attn_a()
    if _STAGES >= 2:
        emit_ln()
        emit_ffn()
    if _STAGES >= 3:
        emit_ln()
        emit_attn_b()
    if _STAGES >= 4:
        emit_ln()
        emit_ffn()

    mk.phase_barrier()
    with nc.sbuf_tensor("stage_out", [128, 2 * D], F32) as stage_out:
        t_st = [T("so0"), T("so1")]
        for t in range(NTC):
            sl = t % 2
            for dset in range(4):
                regs = pst(3072, 512)
                mk.dep_waits("tensor", [t_x[dset * 4 + k] for k in range(4)] + [t_consts],
                             regs)
                mk.cnt["sTE"] += 1
                val = mk.cnt["sTE"]
                s = mk.sems["sTE"]
                for k in range(4):
                    di = dset * 4 + k
                    o = ps[:, 3072 + 128 * k:3072 + 128 * (k + 1)]
                    inp = x_sb[:, TC * di + 128 * t:TC * di + 128 * (t + 1)]
                    if k < 3:
                        def run(o=o, inp=inp):
                            nc.tensor.transpose(o, inp, ident_sb[:, :])
                    else:
                        def run(o=o, inp=inp, s=s):
                            nc.tensor.transpose(o, inp, ident_sb[:, :]).then_inc(s, 1)
                    mk.ops["tensor"].append(run)
                for rg in regs:
                    rg.w = ("tensor", "sTE", val, False, False)
                    rg.readers = []
                for k in range(4):
                    t_x[dset * 4 + k].readers.append(("tensor", "sTE", val, False))
                mk.op("scalar",
                      lambda o=stage_out[:, D * sl + 512 * dset:D * sl + 512 * (dset + 1)],
                             i=ps[:, 3072:3584]:
                      nc.scalar.copy(out=o, in_=i),
                      reads=regs, writes=[t_st[sl]])
            mk.dma("gpsimd", out_tm[128 * t:128 * (t + 1), :],
                   stage_out[:, D * sl:D * (sl + 1)], reads=[t_st[sl]], writes=[])

    for _ch in mk.chans["gpsimd"]:
        if mk.cnt[_ch]:
            mk._wait("gpsimd", _ch, mk.cnt[_ch])

    mk.final_cnt.update(mk.cnt)
    with nc.Block() as block:
        @block.tensor
        def _(te):
            for f in mk.ops["tensor"]:
                f()

        @block.scalar
        def _(sc):
            for f in mk.ops["scalar"]:
                f()

        @block.vector
        def _(ve):
            for f in mk.ops["vector"]:
                f()

        @block.gpsimd
        def _(gp):
            for f in mk.ops["gpsimd"]:
                f()

        @block.sync
        def _(sy):
            for f in mk.ops["sync"]:
                f()

    stack.close()
    return nc


_NC_CACHE = None


def kernel(**inputs):
    global _NC_CACHE
    in_maps = _prep_host(inputs)
    if _NC_CACHE is None:
        _NC_CACHE = build_nc()
    res = run_bass_kernel_spmd(_NC_CACHE, in_maps, core_ids=list(range(8)))
    out = np.empty((2, TF, D), dtype=np.float32)
    for c in range(8):
        b, g = c // 4, c % 4
        out[b, TC * g:TC * (g + 1), :] = res.results[c]["out_tm"]
    return out


# version 05
# speedup vs baseline: 1.3716x; 1.3716x over previous
import numpy as np
from contextlib import ExitStack

import concourse.bass as bass
import concourse.mybir as mybir
from concourse.bass_utils import run_bass_kernel_spmd

F32 = mybir.dt.float32
F32R = mybir.dt.float32r
AF = mybir.ActivationFunctionType
ALU = mybir.AluOpType

D = 2048
NDC = 16
TC = 1024
NTC = 8
TF = 4096
HEADS = 16
HD = 128
RANK = 16
KVD = 512
NKV = 4
HID = 8192
NHC = 64
SCALE = float(HD) ** -0.5
BEXP = 16.0
EPS = 1e-5
LORA_SC = 0.5
NEG = -1.0e30

_STAGES = 4
_DBG = False


class Tile:
    __slots__ = ("name", "w", "readers")

    def __init__(self, name):
        self.name = name
        self.w = None
        self.readers = []


class MK:
    ENGINES = ("tensor", "scalar", "vector", "gpsimd", "sync")

    def __init__(self, nc, stack):
        self.nc = nc
        self.stack = stack
        self.ops = {e: [] for e in self.ENGINES}
        self.sems = {}
        self.cnt = {}
        for name in ("sTE", "sSC", "sVE", "sGP", "sSY", "sCC", "sSCD"):
            self.sems[name] = stack.enter_context(nc.semaphore(name))
            self.cnt[name] = 0
        self.chans = {}
        self.rr = {}
        for eng, n in (("sync", 6), ("scalar", 4), ("gpsimd", 4)):
            self.chans[eng] = []
            self.rr[eng] = 0
            for i in range(n):
                nm = f"ch_{eng}{i}"
                self.sems[nm] = stack.enter_context(nc.semaphore(nm))
                self.cnt[nm] = 0
                self.chans[eng].append(nm)
        self.esem = {"tensor": "sTE", "scalar": "sSC", "vector": "sVE",
                     "gpsimd": "sGP", "sync": "sSY"}
        self.last_wait = {e: {} for e in self.ENGINES}
        self.bank_open = {}
        self.final_cnt = {}

    def _wait(self, eng, sem_name, val):
        if val is None:
            raise RuntimeError(f"un-synced dependency for {eng}")
        if self.last_wait[eng].get(sem_name, 0) >= val:
            return
        self.last_wait[eng][sem_name] = val
        sem = self.sems[sem_name]
        nc = self.nc
        fc = self.final_cnt

        def run(sem=sem, val=val, eng=eng, sn=sem_name):
            getattr(nc, eng).wait_ge(sem, min(val, fc.get(sn, val)))
        self.ops[eng].append(run)

    _SELF_FENCE = ("scalar", "vector", "gpsimd")
    DMA_W = 4

    def _wval(self, sem_name, val, windowed):
        if not windowed:
            return val
        return val + 16 * (self.DMA_W - 1)

    def dep_waits(self, eng, reads, writes):
        for t in reads:
            if t.w is not None:
                we, ws, wv, wasync, wwin = t.w
                if we != eng or wasync or eng in self._SELF_FENCE:
                    self._wait(eng, ws, self._wval(ws, wv, wwin))
        for t in writes:
            if t.w is not None:
                we, ws, wv, wasync, wwin = t.w
                if we != eng or wasync or eng in self._SELF_FENCE:
                    self._wait(eng, ws, self._wval(ws, wv, wwin))
            for (re, rs, rv, rwin) in t.readers:
                if re != eng or eng in self._SELF_FENCE:
                    self._wait(eng, rs, self._wval(rs, rv, rwin))

    def op(self, eng, fn, reads=(), writes=(), amt=1, sem=None, is_async=False,
           windowed=False):
        if sem == "sCC" and self.cnt["sCC"]:
            self._wait(eng, "sCC", self.cnt["sCC"])
        self.dep_waits(eng, reads, writes)
        sem_name = sem or self.esem[eng]
        self.cnt[sem_name] += amt
        val = self.cnt[sem_name]
        s = self.sems[sem_name]

        def run(fn=fn, s=s, amt=amt):
            ins = fn()
            ins.then_inc(s, amt)
        self.ops[eng].append(run)
        for t in reads:
            t.readers.append((eng, sem_name, val, windowed))
        for t in writes:
            t.w = (eng, sem_name, val, is_async, windowed)
            t.readers = []
        return val

    def dma(self, eng, out_ap, in_ap, reads=(), writes=()):
        nc = self.nc
        ch = self.chans[eng][self.rr[eng] % len(self.chans[eng])]
        self.rr[eng] += 1
        if self.cnt[ch]:
            self._wait(eng, ch, self.cnt[ch])
        return self.op(
            eng, lambda o=out_ap, i=in_ap: getattr(nc, eng).dma_start(out=o, in_=i),
            reads=reads, writes=writes, amt=16, sem=ch, is_async=True, windowed=False)

    def phase_barrier(self):
        for q in self.ENGINES:
            for s in list(self.cnt):
                if s == self.esem[q]:
                    continue
                if self.cnt[s]:
                    self._wait(q, s, self.cnt[s])

    def g_start(self, region):
        assert not self.bank_open.get(region), f"region {region} already open"
        self.bank_open[region] = True

    def g_stop(self, region):
        assert self.bank_open.get(region), f"region {region} not open"
        self.bank_open[region] = False

    def mm_group(self, tregs, mms):
        nc = self.nc
        all_reads = []
        for (_, _, _, _, _, rts) in mms:
            all_reads.extend(rts)
        self.dep_waits("tensor", all_reads, tregs)
        sem_name = "sTE"
        self.cnt[sem_name] += 1
        val = self.cnt[sem_name]
        s = self.sems[sem_name]
        n = len(mms)
        for i, (o, lt, rh, st, sp, _) in enumerate(mms):
            if i < n - 1:
                def run(o=o, lt=lt, rh=rh, st=st, sp=sp):
                    nc.tensor.matmul(o, lt, rh, start=st, stop=sp)
            else:
                def run(o=o, lt=lt, rh=rh, st=st, sp=sp, s=s):
                    nc.tensor.matmul(o, lt, rh, start=st, stop=sp).then_inc(s, 1)
            self.ops["tensor"].append(run)
        for t in all_reads:
            t.readers.append(("tensor", sem_name, val, False))
        for t in tregs:
            t.w = ("tensor", sem_name, val, False, False)
            t.readers = []
        return val


def _wstream(W):
    no, ni = W.shape[0] // 128, W.shape[1] // 128
    return np.ascontiguousarray(
        W.T.reshape(ni, 128, no, 128).transpose(2, 1, 0, 3).reshape(no, 128, ni * 128))


def _prep_host(inputs):
    g = lambda k: np.asarray(inputs[k], dtype=np.float32)
    shared = {}
    for n in ("q", "k", "v", "o"):
        shared[f"a{n}_w"] = _wstream(g(f"L0_attn_a_W{n}"))
        shared[f"a{n}_bT"] = np.ascontiguousarray(
            g(f"L0_attn_a_B{n}").T.reshape(NDC, 128, RANK).transpose(1, 0, 2)
            .reshape(128, NDC * RANK))
        shared[f"a{n}_aT"] = np.ascontiguousarray(g(f"L0_attn_a_A{n}").T)
        shared[f"a{n}_bias"] = np.ascontiguousarray(g(f"L0_attn_a_b{n}").reshape(NDC, 128).T)
        shared[f"a{n}_brow"] = g(f"L0_attn_a_b{n}").reshape(1, D)
    shared["av_wm"] = np.ascontiguousarray(g("L0_attn_a_Wv").T)
    shared["bq_w"] = _wstream(g("L0_attn_b_Wq"))
    shared["bk_w"] = _wstream(g("L0_attn_b_Wk"))
    shared["bv_wm"] = np.ascontiguousarray(g("L0_attn_b_Wv").T)
    shared["bp_w"] = _wstream(g("L0_attn_b_Wp"))
    shared["fg_w"] = _wstream(g("L0_ffn_w_gate"))
    shared["fu_w"] = _wstream(g("L0_ffn_w_up"))
    shared["fd_w"] = np.ascontiguousarray(
        g("L0_ffn_w_down").T.reshape(NHC, 128, 4, 4, 128).transpose(2, 0, 1, 3, 4)
        .reshape(4, NHC, 128, 512))
    shared["identF"] = np.eye(128, dtype=np.float32)
    perm = np.zeros((128, 128), dtype=np.float32)
    for d in range(128):
        perm[(d + 64) % 128, d] = 1.0
    shared["permR"] = perm
    shared["cstR"] = np.ones((128, 2), dtype=np.float32)
    shared["cstF"] = np.full((128, 1), 1.0 / D, dtype=np.float32)
    _cv = np.array([1e-5, -16.0, float(2048 ** 0) * (128.0 ** -0.5), 0.5, 1.0, -1.0, 0.0, 2.0],
                   dtype=np.float32)
    shared["actcF"] = np.tile(_cv, (128, 1))
    shared["actcR"] = np.tile(_cv, (128, 1))
    shared["row1"] = np.ones((1, 512), dtype=np.float32)
    shared["row1F"] = np.ones((1, 128), dtype=np.float32)
    dm = np.zeros((4, 128, 512), dtype=np.float32)
    for j in range(4):
        dk = np.arange(128)[:, None]
        dq = np.arange(512)[None, :]
        dm[j] = np.where(dk <= dq - 128 * j, 0.0, NEG)
    shared["dmask"] = dm.reshape(512, 512)

    x = np.asarray(inputs["L0_attn_a_x"], dtype=np.float32)
    in_maps = []
    half = np.arange(64, dtype=np.float64)
    invf = 1.0 / (10000.0 ** (half / 64.0))
    for c in range(8):
        b, gg = c // 4, c % 4
        m = dict(shared)
        m["x_tm"] = np.ascontiguousarray(x[b, TC * gg:TC * (gg + 1), :])
        pos = np.arange(TC * gg, TC * (gg + 1), dtype=np.float64)
        ang = pos[None, :] * invf[:, None]
        cos = np.cos(ang)
        sin = np.sin(ang)
        m["rope_cos"] = np.concatenate([cos, cos], 0).astype(np.float32)
        m["rope_sinS"] = np.concatenate([-sin, sin], 0).astype(np.float32)
        btbl = np.zeros((128, 64), dtype=np.float32)
        for qt in range(2):
            q128 = 8 * gg + 4 * qt
            for kc in range(32):
                btbl[:, 32 * qt + kc] = -BEXP if kc < q128 else NEG
        m["bias_tbl"] = btbl
        in_maps.append(m)
    return in_maps


def build_nc():
    nc = bass.Bass("TRN2", target_bir_lowering=False)
    stack = ExitStack()
    mk = MK(nc, stack)
    T = Tile

    di_ = {}

    def din(name, shape, dt=F32R):
        di_[name] = nc.dram_tensor(name, shape, dt, kind="ExternalInput").ap()
        return di_[name]

    x_tm = din("x_tm", [TC, D], F32)
    out_tm = nc.dram_tensor("out_tm", [TC, D], F32, kind="ExternalOutput").ap()
    for n in ("q", "k", "v", "o"):
        din(f"a{n}_w", [NDC, 128, D])
        din(f"a{n}_bT", [128, NDC * RANK])
        din(f"a{n}_aT", [RANK, D])
        din(f"a{n}_bias", [128, NDC], F32)
        din(f"a{n}_brow", [1, D], F32R)
    din("av_wm", [D, D])
    din("bq_w", [NDC, 128, D])
    din("bk_w", [NKV, 128, D])
    din("bv_wm", [D, KVD])
    din("bp_w", [NDC, 128, D])
    din("fg_w", [NHC, 128, D])
    din("fu_w", [NHC, 128, D])
    din("fd_w", [4, NHC, 128, 512])
    din("identF", [128, 128], F32)
    din("permR", [128, 128])
    din("cstR", [128, 2])
    din("cstF", [128, 1], F32)
    din("actcF", [128, 8], F32)
    din("actcR", [128, 8])
    din("row1", [1, 512])
    din("row1F", [1, 128], F32)
    din("dmask", [512, 512], F32)
    din("rope_cos", [128, TC], F32)
    din("rope_sinS", [128, TC], F32)
    din("bias_tbl", [128, 64], F32)

    dbg = {}
    if _DBG:
        for nm, shp in (("dbg_y", [D, TC]), ("dbg_q", [D, TC]), ("dbg_kv", [TC, TC]),
                        ("dbg_ao", [D, TC]), ("dbg_l", [128, 128])):
            dbg[nm] = nc.dram_tensor(nm, shp, F32R, kind="ExternalOutput").ap()
    qa_sp = nc.dram_tensor("qa_sp", [D, TC], F32R).ap()
    qb_sp = nc.dram_tensor("qb_sp", [D, TC], F32R).ap()
    h_sp = nc.dram_tensor("h_sp", [HID, TC], F32R).ap()
    kloc_sp = nc.dram_tensor("kloc_sp", [KVD, TC], F32R).ap()
    vloc_sp = nc.dram_tensor("vloc_sp", [TC, KVD], F32R).ap()
    cca_in = [nc.dram_tensor(f"cca_in{i}", [TC, TC], F32R).ap() for i in range(4)]
    cca_out = [nc.dram_tensor(f"cca_out{i}", [4 * TC, TC], F32R).ap() for i in range(4)]
    ccb_in = nc.dram_tensor("ccb_in", [TC, TC], F32R).ap()
    ccb_out = nc.dram_tensor("ccb_out", [4 * TC, TC], F32R).ap()

    t_qa = T("qa_sp")
    t_qb = T("qb_sp")
    t_h = [T(f"h_sp{i}") for i in range(NHC)]
    t_kloc = T("kloc")
    t_vloc = T("vloc")
    t_cca_in = [T(f"cca_in{i}") for i in range(4)]
    t_cca_out = [T(f"cca_out{i}") for i in range(4)]
    t_ccb_in = T("ccb_in")
    t_ccb_out = T("ccb_out")

    sb = lambda name, shape, dt=F32R: stack.enter_context(nc.sbuf_tensor(name, shape, dt))
    x_sb = sb("x_sb", [128, NDC * TC], F32)
    y_sb = sb("y_sb", [128, NDC * TC])
    ident_sb = sb("ident_sb", [128, 128], F32)
    perm_sb = sb("perm_sb", [128, 128])
    cst_sb = sb("cst_sb", [128, 2])
    avg_sb = sb("avg_sb", [128, 1], F32)
    actcF_sb = sb("actcF_sb", [128, 8], F32)
    actcR_sb = sb("actcR_sb", [128, 8])
    _CVALS = [EPS, -BEXP, SCALE, LORA_SC, 1.0, -1.0, 0.0, 2.0]
    for _i, _v in enumerate(_CVALS):
        nc.const_aps.register(F32, _v, actcF_sb[:, _i:_i + 1])
        nc.const_aps.register(F32R, _v, actcR_sb[:, _i:_i + 1])
    row1_sb = sb("row1_sb", [1, 512])
    row1F_sb = sb("row1F_sb", [1, 128], F32)
    btbl_sb = sb("btbl_sb", [128, 64], F32)
    bias_sb = {n: sb(f"bias_{n}", [128, NDC], F32) for n in ("q", "k", "v")}
    rowbuf_sb = sb("rowbuf_sb", [1, D])
    rows_sb = sb("rows_sb", [1, 2 * TC], F32)
    bc_sb = sb("bc_sb", [128, 512], F32)

    t_x = [T(f"x{i}") for i in range(NDC)]
    t_y = [T(f"y{i}") for i in range(NDC)]
    t_consts = T("consts")
    t_stat = T("stat")
    t_nmr = T("nmr")
    t_lt8 = T("lt8")
    t_lrow = T("lrow")
    t_lt8 = T("lt8")
    t_bc = T("bc")
    t_rowbuf = T("rowbuf")

    ps = stack.enter_context(nc.psum_tensor("ps", [128, 4096], F32))
    t_psreg = [T(f"psr{r}") for r in range(8)]

    def pst(off, width):
        return t_psreg[off // 512:(off + width + 511) // 512]

    mk.dma("sync", ident_sb[:, :], di_["identF"][:, :], writes=[t_consts])
    mk.dma("sync", perm_sb[:, :], di_["permR"][:, :], writes=[t_consts])
    mk.dma("sync", cst_sb[:, :], di_["cstR"][:, :], writes=[t_consts])
    mk.dma("sync", avg_sb[:, :], di_["cstF"][:, :], writes=[t_consts])
    mk.dma("sync", actcF_sb[:, :], di_["actcF"][:, :], writes=[t_consts])
    mk.dma("sync", actcR_sb[:, :], di_["actcR"][:, :], writes=[t_consts])
    mk.dma("sync", row1_sb[:, :], di_["row1"][:, :], writes=[t_consts])
    mk.dma("sync", row1F_sb[:, :], di_["row1F"][:, :], writes=[t_consts])
    mk.dma("sync", btbl_sb[:, :], di_["bias_tbl"][:, :], writes=[t_consts])
    for n in ("q", "k", "v"):
        mk.dma("sync", bias_sb[n][:, :], di_[f"a{n}_bias"][:, :], writes=[t_consts])


    with nc.sbuf_tensor("stage_in", [128, 2 * D], F32) as stage_in:
        t_st = [T("st0"), T("st1")]
        for t in range(NTC):
            sl = t % 2
            st_ap = stage_in[:, sl * D:(sl + 1) * D]
            mk.dma("sync", st_ap, x_tm[128 * t:128 * (t + 1), :], writes=[t_st[sl]])
            for dset in range(4):
                regs = pst(3072, 512)
                mms = []
                for k in range(4):
                    di = dset * 4 + k
                    o = ps[:, 3072 + 128 * k:3072 + 128 * (k + 1)]
                    mms.append((o, st_ap[:, 128 * di:128 * (di + 1)],
                                ident_sb[:, :], True, True, [t_st[sl], t_consts]))
                mk.dep_waits("tensor", [t_st[sl], t_consts], regs)
                mk.cnt["sTE"] += 1
                val = mk.cnt["sTE"]
                s = mk.sems["sTE"]
                for i, (o, inp, idn, _, _, _) in enumerate(mms):
                    if i < 3:
                        def run(o=o, inp=inp, idn=idn):
                            nc.tensor.transpose(o, inp, idn)
                    else:
                        def run(o=o, inp=inp, idn=idn, s=s):
                            nc.tensor.transpose(o, inp, idn).then_inc(s, 1)
                    mk.ops["tensor"].append(run)
                for rg in regs:
                    rg.w = ("tensor", "sTE", val, False, False)
                    rg.readers = []
                t_st[sl].readers.append(("tensor", "sTE", val, False))
                for k in range(4):
                    di = dset * 4 + k
                    mk.op("scalar",
                          lambda o=x_sb[:, TC * di + 128 * t: TC * di + 128 * (t + 1)],
                                 i=ps[:, 3072 + 128 * k:3072 + 128 * (k + 1)]:
                          nc.scalar.copy(out=o, in_=i),
                          reads=regs, writes=[t_x[di]])

    _sq_state = {"n": 0}

    def emit_ln(_unused=None, _unused2=None):
        _sq_state["n"] += 1
        mk.phase_barrier()
        _ctx = nc.sbuf_tensor("sqb%d" % _sq_state["n"], [128, 2 * TC], F32)
        sq_sb = _ctx.__enter__()
        t_sq = [T("sq0"), T("sq1")]
        t_mean = pst(0, 1024)
        t_ex2 = pst(1024, 1024)
        for reg in ((0, 512), (512, 512), (1024, 512), (1536, 512)):
            mk.g_start(reg)
        mean_mms, sq_mms = [], []
        for di in range(NDC):
            sl = di % 2
            mk.op("scalar",
                  lambda o=sq_sb[:, TC * sl:TC * (sl + 1)],
                         i=x_sb[:, TC * di:TC * (di + 1)]:
                  nc.scalar.activation(out=o, in_=i, func=AF.Square),
                  reads=[t_x[di]], writes=[t_sq[sl]])
            for qt in range(2):
                mean_mms = [(ps[0:1, 512 * qt:512 * (qt + 1)],
                             avg_sb[:, 0:1],
                             x_sb[:, TC * di + 512 * qt:TC * di + 512 * (qt + 1)],
                             di == 0, di == NDC - 1, [t_x[di], t_consts])]
                mk.mm_group(t_mean, mean_mms)
                sq_mms = [(ps[0:1, 1024 + 512 * qt:1024 + 512 * (qt + 1)],
                           avg_sb[:, 0:1],
                           sq_sb[:, TC * sl + 512 * qt:TC * sl + 512 * (qt + 1)],
                           di == 0, di == NDC - 1, [t_sq[sl], t_consts])]
                mk.mm_group(t_ex2, sq_mms)
        for reg in ((0, 512), (512, 512), (1024, 512), (1536, 512)):
            mk.g_stop(reg)
        mk.op("scalar", lambda: nc.scalar.activation(
            out=rows_sb[:, 0:TC], in_=ps[0:1, 0:TC], func=AF.Square),
            reads=t_mean, writes=[t_stat])
        mk.op("vector", lambda: nc.vector.tensor_tensor(
            out=rows_sb[:, TC:2 * TC], in0=ps[0:1, 1024:2048],
            in1=rows_sb[:, 0:TC], op=ALU.subtract),
            reads=t_ex2 + [t_stat], writes=[t_stat])
        mk.op("scalar", lambda: nc.scalar.activation(
            out=rows_sb[:, 0:TC], in_=rows_sb[:, TC:2 * TC],
            func=AF.Sqrt, bias=EPS),
            reads=[t_stat], writes=[t_stat])
        mk.op("vector", lambda: nc.vector.reciprocal(
            out=rows_sb[:, 0:TC], in_=rows_sb[:, 0:TC]),
            reads=[t_stat], writes=[t_stat])
        mk.op("vector", lambda: nc.vector.scalar_tensor_tensor(
            out=rows_sb[:, TC:2 * TC], in0=ps[0:1, 0:TC], scalar=-1.0,
            in1=rows_sb[:, 0:TC], op0=ALU.mult, op1=ALU.mult),
            reads=t_mean + [t_stat], writes=[t_nmr])
        t_rbc = pst(2048, 1024)
        t_nbc = pst(3072, 1024)
        rb, nb = [], []
        for qt in range(2):
            rb.append((ps[:, 2048 + 512 * qt:2048 + 512 * (qt + 1)], row1F_sb[0:1, 0:128],
                       rows_sb[:, 512 * qt:512 * (qt + 1)],
                       True, True, [t_stat, t_consts]))
            nb.append((ps[:, 3072 + 512 * qt:3072 + 512 * (qt + 1)], row1F_sb[0:1, 0:128],
                       rows_sb[:, TC + 512 * qt:TC + 512 * (qt + 1)],
                       True, True, [t_nmr, t_consts]))
        mk.mm_group(t_rbc, rb)
        mk.mm_group(t_nbc, nb)
        for di in range(NDC):
            mk.op("vector", lambda o=y_sb[:, TC * di:TC * (di + 1)],
                             i=x_sb[:, TC * di:TC * (di + 1)]:
                  nc.vector.tensor_tensor(out=o, in0=i, in1=ps[:, 2048:3072], op=ALU.mult),
                  reads=[t_x[di]] + t_rbc, writes=[t_y[di]])
            mk.op("vector", lambda o=y_sb[:, TC * di:TC * (di + 1)]:
                  nc.vector.tensor_tensor(out=o, in0=o, in1=ps[:, 3072:4096], op=ALU.add),
                  reads=t_nbc + [t_y[di]], writes=[t_y[di]])
        _ctx.__exit__(None, None, None)

    def emit_attn_a():
        mk.phase_barrier()
        with (
            nc.sbuf_tensor("wbuf", [128, 2 * D], F32R) as wbuf,
            nc.sbuf_tensor("wmbuf", [128, 2 * 512], F32R) as wmbuf,
            nc.sbuf_tensor("aTb", [RANK, D], F32R) as aTb,
            nc.sbuf_tensor("bTb", [128, NDC * RANK], F32R) as bTb,
            nc.sbuf_tensor("ubuf", [RANK, 3 * TC], F32R) as ubuf,
            nc.sbuf_tensor("kvout", [128, 2 * TC], F32R) as kvout,
        ):
            t_w = [T("wb0"), T("wb1")]
            t_wm = [T("wm0"), T("wm1")]
            t_aT = T("aT")
            t_bT = T("bT")
            t_u = {}
            t_kv = [T("kv0"), T("kv1")]
            mk.dma("sync", rowbuf_sb[:, :], di_["av_brow"][:, :], writes=[t_rowbuf])

            for ui, n in enumerate(("q", "k", "v")):
                t_u[n] = T(f"u_{n}")
                mk.dma("sync", bTb[:, :], di_[f"a{n}_bT"][:, :], writes=[t_bT])
                mms = []
                for di in range(NDC):
                    for qt in range(2):
                        mms.append((ps[0:RANK, 512 * qt:512 * (qt + 1)],
                                    bTb[:, RANK * di:RANK * (di + 1)],
                                    y_sb[:, TC * di + 512 * qt:TC * di + 512 * (qt + 1)],
                                    di == 0, di == NDC - 1, [t_bT, t_y[di]]))
                tp = pst(0, 1024)
                mk.g_start((0, 512)); mk.g_start((512, 512))
                mk.mm_group(tp, mms)
                mk.g_stop((0, 512)); mk.g_stop((512, 512))
                mk.op("scalar", lambda o=ubuf[:, TC * ui:TC * (ui + 1)],
                                 i=ps[0:RANK, 0:TC]:
                      nc.scalar.activation(out=o, in_=i, func=AF.Copy, scale=LORA_SC),
                      reads=tp, writes=[t_u[n]])

            kpos = [0, 0]
            for hg in range(4):
                mk.dma("sync", aTb[:, :], di_["ak_aT"][:, :], writes=[t_aT])
                for j in range(4):
                    d = 4 * hg + j
                    sl = d % 2
                    base = 1024 * sl
                    mk.dma("sync", wbuf[:, D * sl:D * (sl + 1)],
                           di_["ak_w"][d, :, :], writes=[t_w[sl]])
                    tp = pst(base, 1024)
                    mk.g_start((base, 512)); mk.g_start((base + 512, 512))
                    mms = []
                    for di in range(NDC):
                        for qt in range(2):
                            mms.append((ps[:, base + 512 * qt:base + 512 * (qt + 1)],
                                        wbuf[:, D * sl + 128 * di:D * sl + 128 * (di + 1)],
                                        y_sb[:, TC * di + 512 * qt:TC * di + 512 * (qt + 1)],
                                        di == 0, False, [t_w[sl], t_y[di]]))
                    for qt in range(2):
                        mms.append((ps[:, base + 512 * qt:base + 512 * (qt + 1)],
                                    aTb[0:RANK, 128 * d:128 * (d + 1)],
                                    ubuf[0:RANK, TC * 1 + 512 * qt:TC * 1 + 512 * (qt + 1)],
                                    False, True, [t_aT, t_u["k"]]))
                    mk.mm_group(tp, mms)
                    mk.g_stop((base, 512)); mk.g_stop((base + 512, 512))
                    mk.op("scalar", lambda o=kvout[:, TC * sl:TC * (sl + 1)],
                                     i=ps[:, base:base + TC], b=bias_sb["k"][:, d:d + 1]:
                          nc.scalar.activation(out=o, in_=i, func=AF.Identity, bias=b),
                          reads=tp, writes=[t_kv[sl]])
                    mk.dma("gpsimd", cca_in[hg][128 * j:128 * (j + 1), :],
                           kvout[:, TC * sl:TC * (sl + 1)],
                           reads=[t_kv[sl]], writes=[t_cca_in[hg]])
                mk.dma("sync", aTb[:, :], di_["av_aT"][:, :], writes=[t_aT])
                tvs = [pst(512 * t, 512) for t in range(8)]
                for t in range(8):
                    mk.g_start((512 * t, 512))
                for di in range(NDC):
                    sl = di % 2
                    mk.dma("sync", wmbuf[:, 512 * sl:512 * (sl + 1)],
                           di_["av_wm"][128 * di:128 * (di + 1), 512 * hg:512 * (hg + 1)],
                           writes=[t_wm[sl]])
                    mms = []
                    for t in range(8):
                        mms.append((ps[:, 512 * t:512 * (t + 1)],
                                    y_sb[:, TC * di + 128 * t:TC * di + 128 * (t + 1)],
                                    wmbuf[:, 512 * sl:512 * (sl + 1)],
                                    di == 0, False, [t_wm[sl], t_y[di]]))
                    mk.mm_group(pst(0, 4096), mms)
                for t in range(8):
                    mms = [(ps[:, 512 * t:512 * (t + 1)],
                            ubuf[0:RANK, TC * 2 + 128 * t:TC * 2 + 128 * (t + 1)],
                            aTb[0:RANK, 512 * hg:512 * (hg + 1)],
                            False, False, [t_aT, t_u["v"]]),
                           (ps[:, 512 * t:512 * (t + 1)],
                            row1_sb[0:1, 0:128],
                            rowbuf_sb[0:1, 512 * hg:512 * (hg + 1)],
                            False, True, [t_consts, t_rowbuf])]
                    mk.mm_group(tvs[t] if t > 0 else tvs[0], mms)
                for t in range(8):
                    mk.g_stop((512 * t, 512))
                    sl = t % 2
                    mk.op("scalar", lambda o=kvout[:, TC * sl + 512 * 0:TC * sl + 512],
                                     i=ps[:, 512 * t:512 * (t + 1)]:
                          nc.scalar.copy(out=o, in_=i),
                          reads=tvs[t], writes=[t_kv[sl]])
                    dst = cca_in[hg].rearrange("a b -> (a b)")[
                        512 * TC + t * 65536: 512 * TC + (t + 1) * 65536]
                    mk.dma("gpsimd", dst.rearrange("(p c) -> p c", p=128),
                           kvout[:, TC * sl:TC * sl + 512],
                           reads=[t_kv[sl]], writes=[t_cca_in[hg]])
                mk.op("gpsimd", lambda hg=hg: nc.gpsimd.collective_compute(
                    "AllGather", ALU.bypass,
                    replica_groups=[[0, 1, 2, 3], [4, 5, 6, 7]],
                    ins=[cca_in[hg].rearrange("a b -> (a b)").opt()],
                    outs=[cca_out[hg].rearrange("a b -> (a b)").opt()]),
                    reads=[t_cca_in[hg]], writes=[t_cca_out[hg]],
                    sem="sCC", is_async=True)

            mk.dma("sync", aTb[:, :], di_["aq_aT"][:, :], writes=[t_aT])
            for d in range(NDC):
                sl = d % 2
                base = 1024 * sl
                mk.dma("sync", wbuf[:, D * sl:D * (sl + 1)], di_["aq_w"][d, :, :],
                       writes=[t_w[sl]])
                tp = pst(base, 1024)
                mk.g_start((base, 512)); mk.g_start((base + 512, 512))
                mms = []
                for di in range(NDC):
                    for qt in range(2):
                        mms.append((ps[:, base + 512 * qt:base + 512 * (qt + 1)],
                                    wbuf[:, D * sl + 128 * di:D * sl + 128 * (di + 1)],
                                    y_sb[:, TC * di + 512 * qt:TC * di + 512 * (qt + 1)],
                                    di == 0, False, [t_w[sl], t_y[di]]))
                for qt in range(2):
                    mms.append((ps[:, base + 512 * qt:base + 512 * (qt + 1)],
                                aTb[0:RANK, 128 * d:128 * (d + 1)],
                                ubuf[0:RANK, 512 * qt:512 * (qt + 1)],
                                False, True, [t_aT, t_u["q"]]))
                mk.mm_group(tp, mms)
                mk.g_stop((base, 512)); mk.g_stop((base + 512, 512))
                mk.op("scalar", lambda o=kvout[:, TC * sl:TC * (sl + 1)],
                                 i=ps[:, base:base + TC], b=bias_sb["q"][:, d:d + 1]:
                      nc.scalar.activation(out=o, in_=i, func=AF.Identity, bias=b),
                      reads=tp, writes=[t_kv[sl]])
                mk.dma("gpsimd", qa_sp[128 * d:128 * (d + 1), :],
                       kvout[:, TC * sl:TC * (sl + 1)],
                       reads=[t_kv[sl]], writes=[t_qa])

        if _DBG:
            mk.dma("gpsimd", dbg["dbg_q"][:, :], qa_sp[:, :], reads=[t_qa], writes=[])
            mk.dma("gpsimd", dbg["dbg_kv"][:, :], cca_in[0][:, :],
                   reads=[t_cca_in[0]], writes=[])

        t_ao = [T(f"ao{i}") for i in range(NDC)]
        mk.phase_barrier()
        with (
            nc.sbuf_tensor("kbuf", [128, 2 * 2048], F32R) as kbuf,
            nc.sbuf_tensor("vbuf", [128, 2 * 2048], F32R) as vbuf,
            nc.sbuf_tensor("qbuf", [128, 2 * TC], F32R) as qbuf,
            nc.sbuf_tensor("pbuf", [128, 2 * TC], F32R) as pbuf,
        ):
            t_k = [T("k0"), T("k1")]
            t_v = [T("v0"), T("v1")]
            t_q = [T("q0"), T("q1")]
            t_p = [T("p0"), T("p1")]
            t_out = pst(2048, 1024)
            t_l = pst(3072, 1024)

            def load_kv_half(h, hf, sl):
                hg, hh = h // 4, h % 4
                for r2 in range(2):
                    r = 2 * hf + r2
                    mk.dma("sync", kbuf[:, 2048 * sl + 1024 * r2:2048 * sl + 1024 * (r2 + 1)],
                           cca_out[hg][1024 * r + 128 * hh:1024 * r + 128 * (hh + 1), :],
                           reads=[t_cca_out[hg]], writes=[t_k[sl]])
                    src = cca_out[hg].rearrange("a b -> (a b)")
                    base = (1024 * r + 512) * TC + 128 * hh
                    sap = bass.AP(src.tensor, src.offset + base,
                                  [[512, 128], [65536, 8], [1, 128]])
                    dap = bass.AP(vbuf.ap().tensor, vbuf.ap().offset + 2048 * sl + 1024 * r2,
                                  [[4096, 128], [128, 8], [1, 128]])
                    mk.dma("sync", dap, sap, reads=[t_cca_out[hg]], writes=[t_v[sl]])

            for h in range(HEADS):
                qsl = h % 2
                mk.dma("sync", qbuf[:, TC * qsl:TC * (qsl + 1)],
                       qa_sp[128 * h:128 * (h + 1), :], reads=[t_qa], writes=[t_q[qsl]])
                if h == 0:
                    load_kv_half(0, 0, 0)
                mk.g_start((2048, 512)); mk.g_start((2560, 512))

                def lav_a(kc, psl, sl, kk):
                    lm = []
                    for qt in range(2):
                        lm.append((ps[0:2, 3072 + 512 * qt:3584 + 512 * qt],
                                   cst_sb[:, 0:2],
                                   pbuf[:, TC * psl + 512 * qt:TC * psl + 512 * (qt + 1)],
                                   kc == 0, kc == 31, [t_p[psl], t_consts]))
                    mk.mm_group(t_l, lm)
                    av = []
                    for qt in range(2):
                        av.append((ps[:, 2048 + 512 * qt:2048 + 512 * (qt + 1)],
                                   vbuf[:, 2048 * sl + 128 * kk:2048 * sl + 128 * (kk + 1)],
                                   pbuf[:, TC * psl + 512 * qt:TC * psl + 512 * (qt + 1)],
                                   kc == 0, kc == 31, [t_v[sl], t_p[psl]]))
                    mk.mm_group(t_out, av)

                pend = None
                for kc in range(32):
                    hf = kc // 16
                    sl = hf
                    kk = kc % 16
                    psl = kc % 2
                    base = 1024 * psl
                    tp = pst(base, 1024)
                    mk.g_start((base, 512)); mk.g_start((base + 512, 512))
                    mms = []
                    for qt in range(2):
                        mms.append((ps[:, base + 512 * qt:base + 512 * (qt + 1)],
                                    kbuf[:, 2048 * sl + 128 * kk:2048 * sl + 128 * (kk + 1)],
                                    qbuf[:, TC * qsl + 512 * qt:TC * qsl + 512 * (qt + 1)],
                                    True, True, [t_k[sl], t_q[qsl]]))
                    mk.mm_group(tp, mms)
                    mk.g_stop((base, 512)); mk.g_stop((base + 512, 512))
                    mk.op("scalar", lambda o=pbuf[:, TC * psl:TC * (psl + 1)],
                                     i=ps[:, base:base + TC]:
                          nc.scalar.activation(out=o, in_=i, func=AF.Exp,
                                               scale=SCALE, bias=-BEXP),
                          reads=tp, writes=[t_p[psl]])
                    if pend is not None:
                        lav_a(*pend)
                    pend = (kc, psl, sl, kk)
                    if kc % 16 == 0:
                        if kc == 0:
                            load_kv_half(h, 1, 1)
                        else:
                            if h < HEADS - 1:
                                load_kv_half(h + 1, 0, 0)
                lav_a(*pend)
                mk.g_stop((2048, 512)); mk.g_stop((2560, 512))
                emit_norm_head(h, t_out, t_l, t_ao[h])

        if _DBG:
            mk.op("scalar", lambda: nc.scalar.copy(out=bc_sb[:, 0:128], in_=ps[:, 3072:3200]),
                  reads=pst(3072, 512), writes=[t_bc])
            mk.dma("gpsimd", dbg["dbg_l"][:, :], bc_sb[:, 0:128], reads=[t_bc], writes=[])
            for di in range(NDC):
                mk.dma("gpsimd", dbg["dbg_ao"][128 * di:128 * (di + 1), :],
                       y_sb[:, TC * di:TC * (di + 1)], reads=[t_ao[di]], writes=[])

        emit_oproj_a(t_ao)

    def emit_norm_head(h, t_out, t_l, t_aoh):
        mk.op("scalar", lambda: nc.scalar.activation(
            out=rows_sb[:, TC:2 * TC], in_=ps[0:1, 3072:4096], func=AF.Ln),
            reads=t_l, writes=[t_stat])
        mk.op("scalar", lambda: nc.scalar.activation(
            out=rows_sb[:, 0:TC], in_=rows_sb[:, TC:2 * TC], func=AF.Exp, scale=-1.0),
            reads=[t_stat], writes=[t_lrow])
        for qt in range(2):
            t_b = pst(512 * qt, 512)
            mk.mm_group(t_b, [(ps[:, 512 * qt:512 * (qt + 1)], row1F_sb[0:1, 0:128],
                               rows_sb[0:1, 512 * qt:512 * (qt + 1)],
                               True, True, [t_lrow, t_consts])])
            mk.op("scalar", lambda qt=qt: nc.scalar.copy(
                out=bc_sb[:, :], in_=ps[:, 512 * qt:512 * (qt + 1)]),
                reads=t_b, writes=[t_bc])
            mk.op("vector",
                  lambda o=y_sb[:, TC * h + 512 * qt:TC * h + 512 * (qt + 1)],
                         i=ps[:, 2048 + 512 * qt:2048 + 512 * (qt + 1)]:
                  nc.vector.tensor_tensor(out=o, in0=i, in1=bc_sb[:, :], op=ALU.mult),
                  reads=pst(2048, 1024) + [t_bc], writes=[t_aoh])

    def emit_oproj_a(t_ao):
        mk.phase_barrier()
        with (
            nc.sbuf_tensor("wbuf2", [128, 2 * D], F32R) as wbuf,
            nc.sbuf_tensor("aTb2", [RANK, D], F32R) as aTb,
            nc.sbuf_tensor("bTb2", [128, NDC * RANK], F32R) as bTb,
            nc.sbuf_tensor("ubuf2", [RANK, TC], F32R) as ubuf,
        ):
            t_w = [T("w20"), T("w21")]
            t_aT = T("aT2")
            t_bT = T("bT2")
            t_u = T("u_o")
            mk.dma("sync", rowbuf_sb[:, :], di_["ao_brow"][:, :], writes=[t_rowbuf])
            mk.dma("sync", bTb[:, :], di_["ao_bT"][:, :], writes=[t_bT])
            mk.dma("sync", aTb[:, :], di_["ao_aT"][:, :], writes=[t_aT])
            mms = []
            for di in range(NDC):
                for qt in range(2):
                    mms.append((ps[0:RANK, 512 * qt:512 * (qt + 1)],
                                bTb[:, RANK * di:RANK * (di + 1)],
                                y_sb[:, TC * di + 512 * qt:TC * di + 512 * (qt + 1)],
                                di == 0, di == NDC - 1, [t_bT, t_ao[di]]))
            tp = pst(0, 1024)
            mk.g_start((0, 512)); mk.g_start((512, 512))
            mk.mm_group(tp, mms)
            mk.g_stop((0, 512)); mk.g_stop((512, 512))
            mk.op("scalar", lambda: nc.scalar.activation(
                out=ubuf[:, :], in_=ps[0:RANK, 0:TC], func=AF.Copy, scale=LORA_SC),
                reads=tp, writes=[t_u])
            for d in range(NDC):
                sl = d % 2
                base = 1024 * sl
                mk.dma("sync", wbuf[:, D * sl:D * (sl + 1)], di_["ao_w"][d, :, :],
                       writes=[t_w[sl]])
                tp = pst(base, 1024)
                mk.g_start((base, 512)); mk.g_start((base + 512, 512))
                mms = []
                for di in range(NDC):
                    for qt in range(2):
                        mms.append((ps[:, base + 512 * qt:base + 512 * (qt + 1)],
                                    wbuf[:, D * sl + 128 * di:D * sl + 128 * (di + 1)],
                                    y_sb[:, TC * di + 512 * qt:TC * di + 512 * (qt + 1)],
                                    di == 0, False, [t_w[sl], t_ao[di]]))
                for qt in range(2):
                    mms.append((ps[:, base + 512 * qt:base + 512 * (qt + 1)],
                                aTb[0:RANK, 128 * d:128 * (d + 1)],
                                ubuf[0:RANK, 512 * qt:512 * (qt + 1)],
                                False, False, [t_aT, t_u]))
                    mms.append((ps[:, base + 512 * qt:base + 512 * (qt + 1)],
                                rowbuf_sb[0:1, 128 * d:128 * (d + 1)],
                                row1_sb[0:1, 0:512],
                                False, True, [t_rowbuf, t_consts]))
                mk.mm_group(tp, mms)
                mk.g_stop((base, 512)); mk.g_stop((base + 512, 512))
                mk.op("vector", lambda o=x_sb[:, TC * d:TC * (d + 1)],
                                 i=ps[:, base:base + TC]:
                      nc.vector.tensor_tensor(out=o, in0=o, in1=i, op=ALU.add),
                      reads=tp + [t_x[d]], writes=[t_x[d]])

    _ffn_n = [0]

    def emit_ffn():
        _ffn_n[0] += 1
        sfx = str(_ffn_n[0])
        mk.phase_barrier()
        with (
            nc.sbuf_tensor("wgb" + sfx, [128, 2 * D], F32R) as wgb,
            nc.sbuf_tensor("wub" + sfx, [128, 2 * D], F32R) as wub,
            nc.sbuf_tensor("ssb" + sfx, [128, 2 * TC], F32R) as ssb,
            nc.sbuf_tensor("hb" + sfx, [128, 2 * TC], F32R) as hb,
        ):
            t_wg = [T("wg0"), T("wg1")]
            t_wu = [T("wu0"), T("wu1")]
            t_ss = [T("ss0"), T("ss1")]
            t_hb = [T("hb0"), T("hb1")]
            for hc in range(NHC):
                sl = hc % 2
                gb = 2048 * sl
                ub = 2048 * sl + 1024
                mk.dma("sync", wgb[:, D * sl:D * (sl + 1)], di_["fg_w"][hc, :, :],
                       writes=[t_wg[sl]])
                mk.dma("sync", wub[:, D * sl:D * (sl + 1)], di_["fu_w"][hc, :, :],
                       writes=[t_wu[sl]])
                tg = pst(gb, 1024)
                tu = pst(ub, 1024)
                for b0, tt, wb, t_wx in ((gb, tg, wgb, t_wg), (ub, tu, wub, t_wu)):
                    mk.g_start((b0, 512)); mk.g_start((b0 + 512, 512))
                    mms = []
                    for di in range(NDC):
                        for qt in range(2):
                            mms.append((ps[:, b0 + 512 * qt:b0 + 512 * (qt + 1)],
                                        wb[:, D * sl + 128 * di:D * sl + 128 * (di + 1)],
                                        y_sb[:, TC * di + 512 * qt:TC * di + 512 * (qt + 1)],
                                        di == 0, di == NDC - 1, [t_wx[sl], t_y[di]]))
                    mk.mm_group(tt, mms)
                    mk.g_stop((b0, 512)); mk.g_stop((b0 + 512, 512))
                mk.op("scalar", lambda o=ssb[:, TC * sl:TC * (sl + 1)],
                                 i=ps[:, gb:gb + TC]:
                      nc.scalar.activation(out=o, in_=i, func=AF.Silu),
                      reads=tg, writes=[t_ss[sl]])
                mk.op("vector", lambda o=hb[:, TC * sl:TC * (sl + 1)],
                                 i0=ssb[:, TC * sl:TC * (sl + 1)], i1=ps[:, ub:ub + TC]:
                      nc.vector.tensor_tensor(out=o, in0=i0, in1=i1, op=ALU.mult),
                      reads=[t_ss[sl]] + tu, writes=[t_hb[sl]])
                mk.dma("gpsimd", h_sp[128 * hc:128 * (hc + 1), :],
                       hb[:, TC * sl:TC * (sl + 1)],
                       reads=[t_hb[sl]], writes=[t_h[hc]])
        mk.phase_barrier()
        with (
            nc.sbuf_tensor("hrb" + sfx, [128, 2 * TC], F32R) as hrb,
            nc.sbuf_tensor("wdb" + sfx, [128, 2 * 512], F32R) as wdb,
        ):
            t_hr = [T("hr0"), T("hr1")]
            t_wd = [T("wd0"), T("wd1")]
            for dq in range(4):
                tps = [pst(1024 * dc, 1024) for dc in range(4)]
                for dc in range(4):
                    mk.g_start((1024 * dc, 512)); mk.g_start((1024 * dc + 512, 512))
                for hc in range(NHC):
                    sl = hc % 2
                    mk.dma("scalar", hrb[:, TC * sl:TC * (sl + 1)],
                           h_sp[128 * hc:128 * (hc + 1), :],
                           reads=[t_h[hc]], writes=[t_hr[sl]])
                    mk.dma("scalar", wdb[:, 512 * sl:512 * (sl + 1)],
                           di_["fd_w"][dq, hc, :, :], writes=[t_wd[sl]])
                    for dc in range(4):
                        mms = []
                        for qt in range(2):
                            mms.append((ps[:, 1024 * dc + 512 * qt:1024 * dc + 512 * (qt + 1)],
                                        wdb[:, 512 * sl + 128 * dc:512 * sl + 128 * (dc + 1)],
                                        hrb[:, TC * sl + 512 * qt:TC * sl + 512 * (qt + 1)],
                                        hc == 0, hc == NHC - 1, [t_wd[sl], t_hr[sl]]))
                        mk.mm_group(tps[dc], mms)
                for dc in range(4):
                    mk.g_stop((1024 * dc, 512)); mk.g_stop((1024 * dc + 512, 512))
                    d = 4 * dq + dc
                    mk.op("vector", lambda o=x_sb[:, TC * d:TC * (d + 1)],
                                     i=ps[:, 1024 * dc:1024 * dc + TC]:
                          nc.vector.tensor_tensor(out=o, in0=o, in1=i, op=ALU.add),
                          reads=tps[dc] + [t_x[d]], writes=[t_x[d]])

    def emit_attn_b():
        mk.phase_barrier()
        with (
            nc.sbuf_tensor("wbuf3", [128, 2 * D], F32R) as wbuf,
            nc.sbuf_tensor("wmb3", [128, 2 * 512], F32R) as wmb,
            nc.sbuf_tensor("cosb", [128, TC], F32) as cosb,
            nc.sbuf_tensor("sinb", [128, TC], F32) as sinb,
            nc.sbuf_tensor("ktmp", [128, 2 * TC], F32R) as ktmp,
            nc.sbuf_tensor("krot", [128, 2 * TC], F32R) as krot,
        ):
            t_w = [T("w30"), T("w31")]
            t_wm = [T("wm30"), T("wm31")]
            t_cos = T("cos")
            t_ktmp = [T("kt0"), T("kt1")]
            t_krot = [T("kr0"), T("kr1")]
            mk.dma("sync", cosb[:, :], di_["rope_cos"][:, :], writes=[t_cos])
            mk.dma("sync", sinb[:, :], di_["rope_sinS"][:, :], writes=[t_cos])

            def rope_chunk(sl, tp_in, base):
                tsh = pst(2048, 1024)
                mk.g_start((2048, 512)); mk.g_start((2560, 512))
                mms = []
                for qt in range(2):
                    mms.append((ps[:, 2048 + 512 * qt:2048 + 512 * (qt + 1)],
                                perm_sb[:, :],
                                ktmp[:, TC * sl + 512 * qt:TC * sl + 512 * (qt + 1)],
                                True, True, [t_consts, t_ktmp[sl]]))
                mk.mm_group(tsh, mms)
                mk.g_stop((2048, 512)); mk.g_stop((2560, 512))
                mk.op("vector", lambda o=krot[:, TC * sl:TC * (sl + 1)],
                                 i=ktmp[:, TC * sl:TC * (sl + 1)]:
                      nc.vector.tensor_tensor(out=o, in0=i, in1=cosb[:, :], op=ALU.mult),
                      reads=[t_ktmp[sl], t_cos], writes=[t_krot[sl]])
                mk.op("vector", lambda o=ktmp[:, TC * sl:TC * (sl + 1)],
                                 i=ps[:, 2048:2048 + TC]:
                      nc.vector.scalar_tensor_tensor(out=o, in0=i, scalar=1.0,
                                                     in1=sinb[:, :],
                                                     op0=ALU.mult, op1=ALU.mult),
                      reads=tsh + [t_cos], writes=[t_ktmp[sl]])
                mk.op("vector", lambda o=krot[:, TC * sl:TC * (sl + 1)],
                                 i=ktmp[:, TC * sl:TC * (sl + 1)]:
                      nc.vector.tensor_tensor(out=o, in0=o, in1=i, op=ALU.add),
                      reads=[t_ktmp[sl], t_krot[sl]], writes=[t_krot[sl]])

            for dkv in range(NKV):
                sl = dkv % 2
                base = 1024 * sl
                mk.dma("sync", wbuf[:, D * sl:D * (sl + 1)], di_["bk_w"][dkv, :, :],
                       writes=[t_w[sl]])
                tp = pst(base, 1024)
                mk.g_start((base, 512)); mk.g_start((base + 512, 512))
                mms = []
                for di in range(NDC):
                    for qt in range(2):
                        mms.append((ps[:, base + 512 * qt:base + 512 * (qt + 1)],
                                    wbuf[:, D * sl + 128 * di:D * sl + 128 * (di + 1)],
                                    y_sb[:, TC * di + 512 * qt:TC * di + 512 * (qt + 1)],
                                    di == 0, di == NDC - 1, [t_w[sl], t_y[di]]))
                mk.mm_group(tp, mms)
                mk.g_stop((base, 512)); mk.g_stop((base + 512, 512))
                mk.op("scalar", lambda o=ktmp[:, TC * sl:TC * (sl + 1)],
                                 i=ps[:, base:base + TC]:
                      nc.scalar.copy(out=o, in_=i), reads=tp, writes=[t_ktmp[sl]])
                rope_chunk(sl, tp, base)
                mk.dma("gpsimd", ccb_in[128 * dkv:128 * (dkv + 1), :],
                       krot[:, TC * sl:TC * (sl + 1)],
                       reads=[t_krot[sl]], writes=[t_ccb_in])
                mk.dma("gpsimd", kloc_sp[128 * dkv:128 * (dkv + 1), :],
                       krot[:, TC * sl:TC * (sl + 1)],
                       reads=[t_krot[sl]], writes=[t_kloc])
            tvs = [pst(512 * t, 512) for t in range(8)]
            for t in range(8):
                mk.g_start((512 * t, 512))
            for di in range(NDC):
                sl = di % 2
                mk.dma("sync", wmb[:, 512 * sl:512 * (sl + 1)],
                       di_["bv_wm"][128 * di:128 * (di + 1), :], writes=[t_wm[sl]])
                mms = []
                for t in range(8):
                    mms.append((ps[:, 512 * t:512 * (t + 1)],
                                y_sb[:, TC * di + 128 * t:TC * di + 128 * (t + 1)],
                                wmb[:, 512 * sl:512 * (sl + 1)],
                                di == 0, di == NDC - 1, [t_wm[sl], t_y[di]]))
                mk.mm_group(pst(0, 4096), mms)
            for t in range(8):
                mk.g_stop((512 * t, 512))
                sl = t % 2
                mk.op("scalar", lambda o=ktmp[:, TC * sl:TC * sl + 512],
                                 i=ps[:, 512 * t:512 * (t + 1)]:
                      nc.scalar.copy(out=o, in_=i),
                      reads=tvs[t], writes=[t_ktmp[sl]])
                dst = ccb_in.rearrange("a b -> (a b)")[
                    512 * TC + t * 65536: 512 * TC + (t + 1) * 65536]
                mk.dma("gpsimd", dst.rearrange("(p c) -> p c", p=128),
                       ktmp[:, TC * sl:TC * sl + 512],
                       reads=[t_ktmp[sl]], writes=[t_ccb_in])
                mk.dma("gpsimd", vloc_sp[128 * t:128 * (t + 1), :],
                       ktmp[:, TC * sl:TC * sl + 512],
                       reads=[t_ktmp[sl]], writes=[t_vloc])
            mk.op("gpsimd", lambda: nc.gpsimd.collective_compute(
                "AllGather", ALU.bypass,
                replica_groups=[[0, 1, 2, 3], [4, 5, 6, 7]],
                ins=[ccb_in.rearrange("a b -> (a b)").opt()],
                outs=[ccb_out.rearrange("a b -> (a b)").opt()]),
                reads=[t_ccb_in], writes=[t_ccb_out], sem="sCC", is_async=True)
            for d in range(NDC):
                sl = d % 2
                base = 1024 * sl
                mk.dma("sync", wbuf[:, D * sl:D * (sl + 1)], di_["bq_w"][d, :, :],
                       writes=[t_w[sl]])
                tp = pst(base, 1024)
                mk.g_start((base, 512)); mk.g_start((base + 512, 512))
                mms = []
                for di in range(NDC):
                    for qt in range(2):
                        mms.append((ps[:, base + 512 * qt:base + 512 * (qt + 1)],
                                    wbuf[:, D * sl + 128 * di:D * sl + 128 * (di + 1)],
                                    y_sb[:, TC * di + 512 * qt:TC * di + 512 * (qt + 1)],
                                    di == 0, di == NDC - 1, [t_w[sl], t_y[di]]))
                mk.mm_group(tp, mms)
                mk.g_stop((base, 512)); mk.g_stop((base + 512, 512))
                mk.op("scalar", lambda o=ktmp[:, TC * sl:TC * (sl + 1)],
                                 i=ps[:, base:base + TC]:
                      nc.scalar.copy(out=o, in_=i), reads=tp, writes=[t_ktmp[sl]])
                rope_chunk(sl, tp, base)
                mk.dma("gpsimd", qb_sp[128 * d:128 * (d + 1), :],
                       krot[:, TC * sl:TC * (sl + 1)],
                       reads=[t_krot[sl]], writes=[t_qb])

        mk.phase_barrier()
        with (
            nc.sbuf_tensor("kbufb", [128, TF], F32R) as kbufb,
            nc.sbuf_tensor("vbufb", [128, TF], F32R) as vbufb,
            nc.sbuf_tensor("qbufb", [128, TC], F32R) as qbufb,
            nc.sbuf_tensor("pbufb", [128, 2 * TC], F32R) as pbufb,
            nc.sbuf_tensor("kband", [128, 512], F32R) as kband,
            nc.sbuf_tensor("vband", [128, 512], F32R) as vband,
            nc.sbuf_tensor("dmb", [128, 2048], F32) as dmb,
        ):
            t_kb = T("kbB")
            t_vb = T("vbB")
            t_q = [T("qB0"), T("qB1")]
            t_pb = [T(f"pBr{i}") for i in range(4)]
            t_kbd = [T("kbd0"), T("kbd1")]
            t_vbd = [T("vbd0"), T("vbd1")]
            t_dm = T("dmask")
            t_out = pst(2048, 1024)
            t_l = pst(3072, 1024)
            t_ao = [T(f"aob{i}") for i in range(NDC)]
            for _j in range(4):
                mk.dma("sync", dmb[:, 512 * _j:512 * (_j + 1)],
                       di_["dmask"][128 * _j:128 * (_j + 1), :], writes=[t_dm])

            def load_group(g):
                for r in range(4):
                    mk.dma("sync", kbufb[:, TC * r:TC * (r + 1)],
                           ccb_out[TC * r + 128 * g:TC * r + 128 * (g + 1), :],
                           reads=[t_ccb_out], writes=[t_kb])
                    src = ccb_out.rearrange("a b -> (a b)")
                    base = (TC * r + 512) * TC + 128 * g
                    sap = bass.AP(src.tensor, src.offset + base,
                                  [[512, 128], [65536, 8], [1, 128]])
                    dap = bass.AP(vbufb.ap().tensor, vbufb.ap().offset + TC * r,
                                  [[TF, 128], [128, 8], [1, 128]])
                    mk.dma("sync", dap, sap, reads=[t_ccb_out], writes=[t_vb])

            for h in range(HEADS):
                g = h // 4
                qsl = 0
                if h % 4 == 0:
                    load_group(g)
                mk.dma("sync", qbufb[:, 0:TC],
                       qb_sp[128 * h:128 * (h + 1), :], reads=[t_qb], writes=[t_q[0]])
                mk.g_start((2048, 512)); mk.g_start((2560, 512))

                def lav_b(kc, psl):
                    lm = []
                    for qt in range(2):
                        lm.append((ps[0:2, 3072 + 512 * qt:3584 + 512 * qt],
                                   cst_sb[:, 0:2],
                                   pbufb[:, TC * psl + 512 * qt:TC * psl + 512 * (qt + 1)],
                                   kc == 0, False, [t_pb[2 * psl + qt], t_consts]))
                    mk.mm_group(t_l, lm)
                    av = []
                    for qt in range(2):
                        av.append((ps[:, 2048 + 512 * qt:2048 + 512 * (qt + 1)],
                                   vbufb[:, 128 * kc:128 * (kc + 1)],
                                   pbufb[:, TC * psl + 512 * qt:TC * psl + 512 * (qt + 1)],
                                   kc == 0, False, [t_vb, t_pb[2 * psl + qt]]))
                    mk.mm_group(t_out, av)

                pend = None
                for kc in range(32):
                    psl = kc % 2
                    base = 1024 * psl
                    tp = pst(base, 1024)
                    mk.g_start((base, 512)); mk.g_start((base + 512, 512))
                    mms = []
                    for qt in range(2):
                        mms.append((ps[:, base + 512 * qt:base + 512 * (qt + 1)],
                                    kbufb[:, 128 * kc:128 * (kc + 1)],
                                    qbufb[:, 512 * qt:512 * (qt + 1)],
                                    True, True, [t_kb, t_q[qsl]]))
                    mk.mm_group(tp, mms)
                    mk.g_stop((base, 512)); mk.g_stop((base + 512, 512))
                    for qt in range(2):
                        mk.op("scalar",
                              lambda o=pbufb[:, TC * psl + 512 * qt:TC * psl + 512 * (qt + 1)],
                                     i=ps[:, base + 512 * qt:base + 512 * (qt + 1)],
                                     b=btbl_sb[:, 32 * qt + kc:32 * qt + kc + 1]:
                              nc.scalar.activation(out=o, in_=i, func=AF.Exp,
                                                   scale=SCALE, bias=b),
                              reads=tp + [t_consts], writes=[t_pb[2 * psl + qt]])
                    if pend is not None:
                        lav_b(*pend)
                    pend = (kc, psl)
                lav_b(*pend)
                for qt in range(2):
                    bsl = 0
                    mk.dma("sync", kband[:, 0:512],
                           kloc_sp[128 * g:128 * (g + 1), 512 * qt:512 * (qt + 1)],
                           reads=[t_kloc], writes=[t_kbd[bsl]])
                    src = vloc_sp.rearrange("a b -> (a b)")
                    base_v = (512 * qt) * KVD + 128 * g
                    sap = bass.AP(src.tensor, src.offset + base_v,
                                  [[KVD, 128], [128 * KVD, 4], [1, 128]])
                    dap = bass.AP(vband.ap().tensor, vband.ap().offset,
                                  [[512, 128], [128, 4], [1, 128]])
                    mk.dma("sync", dap, sap, reads=[t_vloc], writes=[t_vbd[bsl]])

                    def lav_band(j, r, qt=qt, bsl=bsl):
                        mk.mm_group(t_l, [(ps[0:2, 3072 + 512 * qt:3584 + 512 * qt],
                                           cst_sb[:, 0:2],
                                           pbufb[:, 512 * r:512 * (r + 1)],
                                           False, j == 3, [t_pb[r], t_consts])])
                        mk.mm_group(t_out, [(ps[:, 2048 + 512 * qt:2048 + 512 * (qt + 1)],
                                             vband[:, 128 * j:128 * (j + 1)],
                                             pbufb[:, 512 * r:512 * (r + 1)],
                                             False, j == 3, [t_vbd[bsl], t_pb[r]])])

                    pendb = None
                    for j in range(4):
                        psl = j % 2
                        base = 1024 * psl
                        tp = pst(base, 512)
                        mk.g_start((base, 512))
                        mk.mm_group(tp, [(ps[:, base:base + 512],
                                          kband[:, 128 * j:128 * (j + 1)],
                                          qbufb[:, 512 * qt:512 * (qt + 1)],
                                          True, True, [t_kbd[bsl], t_q[qsl]])])
                        mk.g_stop((base, 512))
                        mk.op("vector", lambda o=ps[:, base:base + 512],
                                         m=dmb[:, 512 * j:512 * (j + 1)]:
                              nc.vector.tensor_tensor(out=o, in0=o, in1=m, op=ALU.add),
                              reads=tp + [t_dm], writes=tp)
                        mk.op("scalar",
                              lambda o=pbufb[:, 512 * psl:512 * (psl + 1)],
                                     i=ps[:, base:base + 512]:
                              nc.scalar.activation(out=o, in_=i, func=AF.Exp,
                                                   scale=SCALE, bias=-BEXP),
                              reads=tp, writes=[t_pb[psl]])
                        if pendb is not None:
                            lav_band(*pendb)
                        pendb = (j, j % 2)
                    lav_band(*pendb)
                mk.g_stop((2048, 512)); mk.g_stop((2560, 512))
                emit_norm_head(h, t_out, t_l, t_ao[h])

        mk.phase_barrier()
        with nc.sbuf_tensor("wbuf4", [128, 2 * D], F32R) as wbuf:
            t_w = [T("w40"), T("w41")]
            for d in range(NDC):
                sl = d % 2
                base = 1024 * sl
                mk.dma("sync", wbuf[:, D * sl:D * (sl + 1)], di_["bp_w"][d, :, :],
                       writes=[t_w[sl]])
                tp = pst(base, 1024)
                mk.g_start((base, 512)); mk.g_start((base + 512, 512))
                mms = []
                for di in range(NDC):
                    for qt in range(2):
                        mms.append((ps[:, base + 512 * qt:base + 512 * (qt + 1)],
                                    wbuf[:, D * sl + 128 * di:D * sl + 128 * (di + 1)],
                                    y_sb[:, TC * di + 512 * qt:TC * di + 512 * (qt + 1)],
                                    di == 0, di == NDC - 1, [t_w[sl], t_ao[di]]))
                mk.mm_group(tp, mms)
                mk.g_stop((base, 512)); mk.g_stop((base + 512, 512))
                mk.op("vector", lambda o=x_sb[:, TC * d:TC * (d + 1)],
                                 i=ps[:, base:base + TC]:
                      nc.vector.tensor_tensor(out=o, in0=o, in1=i, op=ALU.add),
                      reads=tp + [t_x[d]], writes=[t_x[d]])

    if _STAGES >= 1:
        emit_ln()
        if _DBG:
            for di in range(NDC):
                mk.dma("gpsimd", dbg["dbg_y"][128 * di:128 * (di + 1), :],
                       y_sb[:, TC * di:TC * (di + 1)], reads=[t_y[di]], writes=[])
        emit_attn_a()
    if _STAGES >= 2:
        emit_ln()
        emit_ffn()
    if _STAGES >= 3:
        emit_ln()
        emit_attn_b()
    if _STAGES >= 4:
        emit_ln()
        emit_ffn()

    mk.phase_barrier()
    with nc.sbuf_tensor("stage_out", [128, 2 * D], F32) as stage_out:
        t_st = [T("so0"), T("so1")]
        for t in range(NTC):
            sl = t % 2
            for dset in range(4):
                regs = pst(3072, 512)
                mk.dep_waits("tensor", [t_x[dset * 4 + k] for k in range(4)] + [t_consts],
                             regs)
                mk.cnt["sTE"] += 1
                val = mk.cnt["sTE"]
                s = mk.sems["sTE"]
                for k in range(4):
                    di = dset * 4 + k
                    o = ps[:, 3072 + 128 * k:3072 + 128 * (k + 1)]
                    inp = x_sb[:, TC * di + 128 * t:TC * di + 128 * (t + 1)]
                    if k < 3:
                        def run(o=o, inp=inp):
                            nc.tensor.transpose(o, inp, ident_sb[:, :])
                    else:
                        def run(o=o, inp=inp, s=s):
                            nc.tensor.transpose(o, inp, ident_sb[:, :]).then_inc(s, 1)
                    mk.ops["tensor"].append(run)
                for rg in regs:
                    rg.w = ("tensor", "sTE", val, False, False)
                    rg.readers = []
                for k in range(4):
                    t_x[dset * 4 + k].readers.append(("tensor", "sTE", val, False))
                mk.op("scalar",
                      lambda o=stage_out[:, D * sl + 512 * dset:D * sl + 512 * (dset + 1)],
                             i=ps[:, 3072:3584]:
                      nc.scalar.copy(out=o, in_=i),
                      reads=regs, writes=[t_st[sl]])
            mk.dma("gpsimd", out_tm[128 * t:128 * (t + 1), :],
                   stage_out[:, D * sl:D * (sl + 1)], reads=[t_st[sl]], writes=[])

    for _ch in mk.chans["gpsimd"]:
        if mk.cnt[_ch]:
            mk._wait("gpsimd", _ch, mk.cnt[_ch])

    mk.final_cnt.update(mk.cnt)
    with nc.Block() as block:
        @block.tensor
        def _(te):
            for f in mk.ops["tensor"]:
                f()

        @block.scalar
        def _(sc):
            for f in mk.ops["scalar"]:
                f()

        @block.vector
        def _(ve):
            for f in mk.ops["vector"]:
                f()

        @block.gpsimd
        def _(gp):
            for f in mk.ops["gpsimd"]:
                f()

        @block.sync
        def _(sy):
            for f in mk.ops["sync"]:
                f()

    stack.close()
    return nc


_NC_CACHE = None


def kernel(**inputs):
    global _NC_CACHE
    in_maps = _prep_host(inputs)
    if _NC_CACHE is None:
        _NC_CACHE = build_nc()
    res = run_bass_kernel_spmd(_NC_CACHE, in_maps, core_ids=list(range(8)))
    out = np.empty((2, TF, D), dtype=np.float32)
    for c in range(8):
        b, g = c // 4, c % 4
        out[b, TC * g:TC * (g + 1), :] = res.results[c]["out_tm"]
    return out


# version 06
# speedup vs baseline: 1.3845x; 1.0094x over previous
import numpy as np
from contextlib import ExitStack

import concourse.bass as bass
import concourse.mybir as mybir
from concourse.bass_utils import run_bass_kernel_spmd

F32 = mybir.dt.float32
F32R = mybir.dt.float32r
AF = mybir.ActivationFunctionType
ALU = mybir.AluOpType

D = 2048
NDC = 16
TC = 1024
NTC = 8
TF = 4096
HEADS = 16
HD = 128
RANK = 16
KVD = 512
NKV = 4
HID = 8192
NHC = 64
SCALE = float(HD) ** -0.5
BEXP = 16.0
EPS = 1e-5
LORA_SC = 0.5
NEG = -1.0e30

_STAGES = 4
_DBG = False


class Tile:
    __slots__ = ("name", "w", "readers")

    def __init__(self, name):
        self.name = name
        self.w = None
        self.readers = []


class MK:
    ENGINES = ("tensor", "scalar", "vector", "gpsimd", "sync")

    def __init__(self, nc, stack):
        self.nc = nc
        self.stack = stack
        self.ops = {e: [] for e in self.ENGINES}
        self.sems = {}
        self.cnt = {}
        for name in ("sTE", "sSC", "sVE", "sGP", "sSY", "sCC", "sSCD"):
            self.sems[name] = stack.enter_context(nc.semaphore(name))
            self.cnt[name] = 0
        self.chans = {}
        self.rr = {}
        for eng, n in (("sync", 6), ("scalar", 4), ("gpsimd", 4)):
            self.chans[eng] = []
            self.rr[eng] = 0
            for i in range(n):
                nm = f"ch_{eng}{i}"
                self.sems[nm] = stack.enter_context(nc.semaphore(nm))
                self.cnt[nm] = 0
                self.chans[eng].append(nm)
        self.esem = {"tensor": "sTE", "scalar": "sSC", "vector": "sVE",
                     "gpsimd": "sGP", "sync": "sSY"}
        self.last_wait = {e: {} for e in self.ENGINES}
        self.bank_open = {}
        self.final_cnt = {}

    def _wait(self, eng, sem_name, val):
        if val is None:
            raise RuntimeError(f"un-synced dependency for {eng}")
        if self.last_wait[eng].get(sem_name, 0) >= val:
            return
        self.last_wait[eng][sem_name] = val
        sem = self.sems[sem_name]
        nc = self.nc
        fc = self.final_cnt

        def run(sem=sem, val=val, eng=eng, sn=sem_name):
            getattr(nc, eng).wait_ge(sem, min(val, fc.get(sn, val)))
        self.ops[eng].append(run)

    _SELF_FENCE = ("scalar", "vector", "gpsimd")
    DMA_W = 4

    def _wval(self, sem_name, val, windowed):
        if not windowed:
            return val
        return val + 16 * (self.DMA_W - 1)

    def dep_waits(self, eng, reads, writes):
        for t in reads:
            if t.w is not None:
                we, ws, wv, wasync, wwin = t.w
                if we != eng or wasync or eng in self._SELF_FENCE:
                    self._wait(eng, ws, self._wval(ws, wv, wwin))
        for t in writes:
            if t.w is not None:
                we, ws, wv, wasync, wwin = t.w
                if we != eng or wasync or eng in self._SELF_FENCE:
                    self._wait(eng, ws, self._wval(ws, wv, wwin))
            for (re, rs, rv, rwin) in t.readers:
                if re != eng or eng in self._SELF_FENCE:
                    self._wait(eng, rs, self._wval(rs, rv, rwin))

    def op(self, eng, fn, reads=(), writes=(), amt=1, sem=None, is_async=False,
           windowed=False):
        if sem == "sCC" and self.cnt["sCC"]:
            self._wait(eng, "sCC", self.cnt["sCC"])
        self.dep_waits(eng, reads, writes)
        sem_name = sem or self.esem[eng]
        self.cnt[sem_name] += amt
        val = self.cnt[sem_name]
        s = self.sems[sem_name]

        def run(fn=fn, s=s, amt=amt):
            ins = fn()
            ins.then_inc(s, amt)
        self.ops[eng].append(run)
        for t in reads:
            t.readers.append((eng, sem_name, val, windowed))
        for t in writes:
            t.w = (eng, sem_name, val, is_async, windowed)
            t.readers = []
        return val

    def dma(self, eng, out_ap, in_ap, reads=(), writes=()):
        nc = self.nc
        ch = self.chans[eng][self.rr[eng] % len(self.chans[eng])]
        self.rr[eng] += 1
        if self.cnt[ch]:
            self._wait(eng, ch, self.cnt[ch])
        return self.op(
            eng, lambda o=out_ap, i=in_ap: getattr(nc, eng).dma_start(out=o, in_=i),
            reads=reads, writes=writes, amt=16, sem=ch, is_async=True, windowed=False)

    def phase_barrier(self):
        for q in self.ENGINES:
            for s in list(self.cnt):
                if s == self.esem[q]:
                    continue
                if self.cnt[s]:
                    self._wait(q, s, self.cnt[s])

    def g_start(self, region):
        assert not self.bank_open.get(region), f"region {region} already open"
        self.bank_open[region] = True

    def g_stop(self, region):
        assert self.bank_open.get(region), f"region {region} not open"
        self.bank_open[region] = False

    def mm_group(self, tregs, mms):
        nc = self.nc
        all_reads = []
        for (_, _, _, _, _, rts) in mms:
            all_reads.extend(rts)
        self.dep_waits("tensor", all_reads, tregs)
        sem_name = "sTE"
        self.cnt[sem_name] += 1
        val = self.cnt[sem_name]
        s = self.sems[sem_name]
        n = len(mms)
        for i, (o, lt, rh, st, sp, _) in enumerate(mms):
            if i < n - 1:
                def run(o=o, lt=lt, rh=rh, st=st, sp=sp):
                    nc.tensor.matmul(o, lt, rh, start=st, stop=sp)
            else:
                def run(o=o, lt=lt, rh=rh, st=st, sp=sp, s=s):
                    nc.tensor.matmul(o, lt, rh, start=st, stop=sp).then_inc(s, 1)
            self.ops["tensor"].append(run)
        for t in all_reads:
            t.readers.append(("tensor", sem_name, val, False))
        for t in tregs:
            t.w = ("tensor", sem_name, val, False, False)
            t.readers = []
        return val


def _wstream(W):
    no, ni = W.shape[0] // 128, W.shape[1] // 128
    return np.ascontiguousarray(
        W.T.reshape(ni, 128, no, 128).transpose(2, 1, 0, 3).reshape(no, 128, ni * 128))


def _prep_host(inputs):
    g = lambda k: np.asarray(inputs[k], dtype=np.float32)
    shared = {}
    for n in ("q", "k", "v", "o"):
        shared[f"a{n}_w"] = _wstream(g(f"L0_attn_a_W{n}"))
        shared[f"a{n}_bT"] = np.ascontiguousarray(
            g(f"L0_attn_a_B{n}").T.reshape(NDC, 128, RANK).transpose(1, 0, 2)
            .reshape(128, NDC * RANK))
        shared[f"a{n}_aT"] = np.ascontiguousarray(g(f"L0_attn_a_A{n}").T)
        shared[f"a{n}_bias"] = np.ascontiguousarray(g(f"L0_attn_a_b{n}").reshape(NDC, 128).T)
        shared[f"a{n}_brow"] = g(f"L0_attn_a_b{n}").reshape(1, D)
    shared["av_wm"] = np.ascontiguousarray(g("L0_attn_a_Wv").T)
    shared["bq_w"] = _wstream(g("L0_attn_b_Wq"))
    shared["bk_w"] = _wstream(g("L0_attn_b_Wk"))
    shared["bv_wm"] = np.ascontiguousarray(g("L0_attn_b_Wv").T)
    shared["bp_w"] = _wstream(g("L0_attn_b_Wp"))
    shared["fg_w"] = _wstream(g("L0_ffn_w_gate"))
    shared["fu_w"] = _wstream(g("L0_ffn_w_up"))
    shared["fd_w"] = np.ascontiguousarray(
        g("L0_ffn_w_down").T.reshape(NHC, 128, 4, 4, 128).transpose(2, 0, 1, 3, 4)
        .reshape(4, NHC, 128, 512))
    shared["identF"] = np.eye(128, dtype=np.float32)
    perm = np.zeros((128, 128), dtype=np.float32)
    for d in range(128):
        perm[(d + 64) % 128, d] = 1.0
    shared["permR"] = perm
    shared["cstR"] = np.ones((128, 2), dtype=np.float32)
    shared["cstF"] = np.full((128, 1), 1.0 / D, dtype=np.float32)
    _cv = np.array([1e-5, -16.0, float(2048 ** 0) * (128.0 ** -0.5), 0.5, 1.0, -1.0, 0.0, 2.0],
                   dtype=np.float32)
    shared["actcF"] = np.tile(_cv, (128, 1))
    shared["actcR"] = np.tile(_cv, (128, 1))
    shared["row1"] = np.ones((1, 512), dtype=np.float32)
    shared["row1F"] = np.ones((1, 128), dtype=np.float32)
    dm = np.zeros((4, 128, 512), dtype=np.float32)
    for j in range(4):
        dk = np.arange(128)[:, None]
        dq = np.arange(512)[None, :]
        dm[j] = np.where(dk <= dq - 128 * j, 0.0, NEG)
    shared["dmask"] = dm.reshape(512, 512)

    x = np.asarray(inputs["L0_attn_a_x"], dtype=np.float32)
    in_maps = []
    half = np.arange(64, dtype=np.float64)
    invf = 1.0 / (10000.0 ** (half / 64.0))
    for c in range(8):
        b, gg = c // 4, c % 4
        m = dict(shared)
        m["x_tm"] = np.ascontiguousarray(x[b, TC * gg:TC * (gg + 1), :])
        pos = np.arange(TC * gg, TC * (gg + 1), dtype=np.float64)
        ang = pos[None, :] * invf[:, None]
        cos = np.cos(ang)
        sin = np.sin(ang)
        m["rope_cos"] = np.concatenate([cos, cos], 0).astype(np.float32)
        m["rope_sinS"] = np.concatenate([-sin, sin], 0).astype(np.float32)
        btbl = np.zeros((128, 64), dtype=np.float32)
        for qt in range(2):
            q128 = 8 * gg + 4 * qt
            for kc in range(32):
                btbl[:, 32 * qt + kc] = -BEXP if kc < q128 else NEG
        m["bias_tbl"] = btbl
        in_maps.append(m)
    return in_maps


def build_nc():
    nc = bass.Bass("TRN2", target_bir_lowering=False)
    stack = ExitStack()
    mk = MK(nc, stack)
    T = Tile

    di_ = {}

    def din(name, shape, dt=F32R):
        di_[name] = nc.dram_tensor(name, shape, dt, kind="ExternalInput").ap()
        return di_[name]

    x_tm = din("x_tm", [TC, D], F32)
    out_fm = nc.dram_tensor("out_fm", [D, TC], F32, kind="ExternalOutput").ap()
    for n in ("q", "k", "v", "o"):
        din(f"a{n}_w", [NDC, 128, D])
        din(f"a{n}_bT", [128, NDC * RANK])
        din(f"a{n}_aT", [RANK, D])
        din(f"a{n}_bias", [128, NDC], F32)
        din(f"a{n}_brow", [1, D], F32R)
    din("av_wm", [D, D])
    din("bq_w", [NDC, 128, D])
    din("bk_w", [NKV, 128, D])
    din("bv_wm", [D, KVD])
    din("bp_w", [NDC, 128, D])
    din("fg_w", [NHC, 128, D])
    din("fu_w", [NHC, 128, D])
    din("fd_w", [4, NHC, 128, 512])
    din("identF", [128, 128], F32)
    din("permR", [128, 128])
    din("cstR", [128, 2])
    din("cstF", [128, 1], F32)
    din("actcF", [128, 8], F32)
    din("actcR", [128, 8])
    din("row1", [1, 512])
    din("row1F", [1, 128], F32)
    din("dmask", [512, 512], F32)
    din("rope_cos", [128, TC], F32)
    din("rope_sinS", [128, TC], F32)
    din("bias_tbl", [128, 64], F32)

    dbg = {}
    if _DBG:
        for nm, shp in (("dbg_y", [D, TC]), ("dbg_q", [D, TC]), ("dbg_kv", [TC, TC]),
                        ("dbg_ao", [D, TC]), ("dbg_l", [128, 128])):
            dbg[nm] = nc.dram_tensor(nm, shp, F32R, kind="ExternalOutput").ap()
    qa_sp = nc.dram_tensor("qa_sp", [D, TC], F32R).ap()
    qb_sp = nc.dram_tensor("qb_sp", [D, TC], F32R).ap()
    h_sp = nc.dram_tensor("h_sp", [HID, TC], F32R).ap()
    kloc_sp = nc.dram_tensor("kloc_sp", [KVD, TC], F32R).ap()
    vloc_sp = nc.dram_tensor("vloc_sp", [TC, KVD], F32R).ap()
    cca_in = [nc.dram_tensor(f"cca_in{i}", [TC, TC], F32R).ap() for i in range(4)]
    cca_out = [nc.dram_tensor(f"cca_out{i}", [4 * TC, TC], F32R).ap() for i in range(4)]
    ccb_in = nc.dram_tensor("ccb_in", [TC, TC], F32R).ap()
    ccb_out = nc.dram_tensor("ccb_out", [4 * TC, TC], F32R).ap()

    t_qa = T("qa_sp")
    t_qb = T("qb_sp")
    t_h = [T(f"h_sp{i}") for i in range(NHC)]
    t_kloc = T("kloc")
    t_vloc = T("vloc")
    t_cca_in = [T(f"cca_in{i}") for i in range(4)]
    t_cca_out = [T(f"cca_out{i}") for i in range(4)]
    t_ccb_in = T("ccb_in")
    t_ccb_out = T("ccb_out")

    sb = lambda name, shape, dt=F32R: stack.enter_context(nc.sbuf_tensor(name, shape, dt))
    x_sb = sb("x_sb", [128, NDC * TC], F32)
    y_sb = sb("y_sb", [128, NDC * TC])
    ident_sb = sb("ident_sb", [128, 128], F32)
    perm_sb = sb("perm_sb", [128, 128])
    cst_sb = sb("cst_sb", [128, 2])
    avg_sb = sb("avg_sb", [128, 1], F32)
    actcF_sb = sb("actcF_sb", [128, 8], F32)
    actcR_sb = sb("actcR_sb", [128, 8])
    _CVALS = [EPS, -BEXP, SCALE, LORA_SC, 1.0, -1.0, 0.0, 2.0]
    for _i, _v in enumerate(_CVALS):
        nc.const_aps.register(F32, _v, actcF_sb[:, _i:_i + 1])
        nc.const_aps.register(F32R, _v, actcR_sb[:, _i:_i + 1])
    row1_sb = sb("row1_sb", [1, 512])
    row1F_sb = sb("row1F_sb", [1, 128], F32)
    btbl_sb = sb("btbl_sb", [128, 64], F32)
    bias_sb = {n: sb(f"bias_{n}", [128, NDC], F32) for n in ("q", "k", "v")}
    rowbuf_sb = sb("rowbuf_sb", [1, D])
    rows_sb = sb("rows_sb", [1, 2 * TC], F32)
    bc_sb = sb("bc_sb", [128, 512], F32)

    t_x = [T(f"x{i}") for i in range(NDC)]
    t_y = [T(f"y{i}") for i in range(NDC)]
    t_consts = T("consts")
    t_stat = T("stat")
    t_nmr = T("nmr")
    t_lt8 = T("lt8")
    t_lrow = T("lrow")
    t_lt8 = T("lt8")
    t_bc = T("bc")
    t_rowbuf = T("rowbuf")

    ps = stack.enter_context(nc.psum_tensor("ps", [128, 4096], F32))
    t_psreg = [T(f"psr{r}") for r in range(8)]

    def pst(off, width):
        return t_psreg[off // 512:(off + width + 511) // 512]

    mk.dma("sync", ident_sb[:, :], di_["identF"][:, :], writes=[t_consts])
    mk.dma("sync", perm_sb[:, :], di_["permR"][:, :], writes=[t_consts])
    mk.dma("sync", cst_sb[:, :], di_["cstR"][:, :], writes=[t_consts])
    mk.dma("sync", avg_sb[:, :], di_["cstF"][:, :], writes=[t_consts])
    mk.dma("sync", actcF_sb[:, :], di_["actcF"][:, :], writes=[t_consts])
    mk.dma("sync", actcR_sb[:, :], di_["actcR"][:, :], writes=[t_consts])
    mk.dma("sync", row1_sb[:, :], di_["row1"][:, :], writes=[t_consts])
    mk.dma("sync", row1F_sb[:, :], di_["row1F"][:, :], writes=[t_consts])
    mk.dma("sync", btbl_sb[:, :], di_["bias_tbl"][:, :], writes=[t_consts])
    for n in ("q", "k", "v"):
        mk.dma("sync", bias_sb[n][:, :], di_[f"a{n}_bias"][:, :], writes=[t_consts])


    with nc.sbuf_tensor("stage_in", [128, 2 * D], F32) as stage_in:
        t_st = [T("st0"), T("st1")]
        for t in range(NTC):
            sl = t % 2
            st_ap = stage_in[:, sl * D:(sl + 1) * D]
            mk.dma("sync", st_ap, x_tm[128 * t:128 * (t + 1), :], writes=[t_st[sl]])
            for dset in range(4):
                regs = pst(3072, 512)
                mms = []
                for k in range(4):
                    di = dset * 4 + k
                    o = ps[:, 3072 + 128 * k:3072 + 128 * (k + 1)]
                    mms.append((o, st_ap[:, 128 * di:128 * (di + 1)],
                                ident_sb[:, :], True, True, [t_st[sl], t_consts]))
                mk.dep_waits("tensor", [t_st[sl], t_consts], regs)
                mk.cnt["sTE"] += 1
                val = mk.cnt["sTE"]
                s = mk.sems["sTE"]
                for i, (o, inp, idn, _, _, _) in enumerate(mms):
                    if i < 3:
                        def run(o=o, inp=inp, idn=idn):
                            nc.tensor.transpose(o, inp, idn)
                    else:
                        def run(o=o, inp=inp, idn=idn, s=s):
                            nc.tensor.transpose(o, inp, idn).then_inc(s, 1)
                    mk.ops["tensor"].append(run)
                for rg in regs:
                    rg.w = ("tensor", "sTE", val, False, False)
                    rg.readers = []
                t_st[sl].readers.append(("tensor", "sTE", val, False))
                for k in range(4):
                    di = dset * 4 + k
                    mk.op("scalar",
                          lambda o=x_sb[:, TC * di + 128 * t: TC * di + 128 * (t + 1)],
                                 i=ps[:, 3072 + 128 * k:3072 + 128 * (k + 1)]:
                          nc.scalar.copy(out=o, in_=i),
                          reads=regs, writes=[t_x[di]])

    _sq_state = {"n": 0}

    def emit_ln(_unused=None, _unused2=None):
        _sq_state["n"] += 1
        mk.phase_barrier()
        _ctx = nc.sbuf_tensor("sqb%d" % _sq_state["n"], [128, 2 * TC], F32)
        sq_sb = _ctx.__enter__()
        t_sq = [T("sq0"), T("sq1")]
        t_mean = pst(0, 1024)
        t_ex2 = pst(1024, 1024)
        for reg in ((0, 512), (512, 512), (1024, 512), (1536, 512)):
            mk.g_start(reg)
        mean_mms, sq_mms = [], []
        for di in range(NDC):
            sl = di % 2
            mk.op("scalar",
                  lambda o=sq_sb[:, TC * sl:TC * (sl + 1)],
                         i=x_sb[:, TC * di:TC * (di + 1)]:
                  nc.scalar.activation(out=o, in_=i, func=AF.Square),
                  reads=[t_x[di]], writes=[t_sq[sl]])
            for qt in range(2):
                mean_mms = [(ps[0:1, 512 * qt:512 * (qt + 1)],
                             avg_sb[:, 0:1],
                             x_sb[:, TC * di + 512 * qt:TC * di + 512 * (qt + 1)],
                             di == 0, di == NDC - 1, [t_x[di], t_consts])]
                mk.mm_group(t_mean, mean_mms)
                sq_mms = [(ps[0:1, 1024 + 512 * qt:1024 + 512 * (qt + 1)],
                           avg_sb[:, 0:1],
                           sq_sb[:, TC * sl + 512 * qt:TC * sl + 512 * (qt + 1)],
                           di == 0, di == NDC - 1, [t_sq[sl], t_consts])]
                mk.mm_group(t_ex2, sq_mms)
        for reg in ((0, 512), (512, 512), (1024, 512), (1536, 512)):
            mk.g_stop(reg)
        mk.op("scalar", lambda: nc.scalar.activation(
            out=rows_sb[:, 0:TC], in_=ps[0:1, 0:TC], func=AF.Square),
            reads=t_mean, writes=[t_stat])
        mk.op("vector", lambda: nc.vector.tensor_tensor(
            out=rows_sb[:, TC:2 * TC], in0=ps[0:1, 1024:2048],
            in1=rows_sb[:, 0:TC], op=ALU.subtract),
            reads=t_ex2 + [t_stat], writes=[t_stat])
        mk.op("scalar", lambda: nc.scalar.activation(
            out=rows_sb[:, 0:TC], in_=rows_sb[:, TC:2 * TC],
            func=AF.Sqrt, bias=EPS),
            reads=[t_stat], writes=[t_stat])
        mk.op("vector", lambda: nc.vector.reciprocal(
            out=rows_sb[:, 0:TC], in_=rows_sb[:, 0:TC]),
            reads=[t_stat], writes=[t_stat])
        mk.op("vector", lambda: nc.vector.scalar_tensor_tensor(
            out=rows_sb[:, TC:2 * TC], in0=ps[0:1, 0:TC], scalar=-1.0,
            in1=rows_sb[:, 0:TC], op0=ALU.mult, op1=ALU.mult),
            reads=t_mean + [t_stat], writes=[t_nmr])
        t_rbc = pst(2048, 1024)
        t_nbc = pst(3072, 1024)
        rb, nb = [], []
        for qt in range(2):
            rb.append((ps[:, 2048 + 512 * qt:2048 + 512 * (qt + 1)], row1F_sb[0:1, 0:128],
                       rows_sb[:, 512 * qt:512 * (qt + 1)],
                       True, True, [t_stat, t_consts]))
            nb.append((ps[:, 3072 + 512 * qt:3072 + 512 * (qt + 1)], row1F_sb[0:1, 0:128],
                       rows_sb[:, TC + 512 * qt:TC + 512 * (qt + 1)],
                       True, True, [t_nmr, t_consts]))
        mk.mm_group(t_rbc, rb)
        mk.mm_group(t_nbc, nb)
        for di in range(NDC):
            mk.op("vector", lambda o=y_sb[:, TC * di:TC * (di + 1)],
                             i=x_sb[:, TC * di:TC * (di + 1)]:
                  nc.vector.tensor_tensor(out=o, in0=i, in1=ps[:, 2048:3072], op=ALU.mult),
                  reads=[t_x[di]] + t_rbc, writes=[t_y[di]])
            mk.op("vector", lambda o=y_sb[:, TC * di:TC * (di + 1)]:
                  nc.vector.tensor_tensor(out=o, in0=o, in1=ps[:, 3072:4096], op=ALU.add),
                  reads=t_nbc + [t_y[di]], writes=[t_y[di]])
        _ctx.__exit__(None, None, None)

    def emit_attn_a():
        mk.phase_barrier()
        with (
            nc.sbuf_tensor("wbuf", [128, 2 * D], F32R) as wbuf,
            nc.sbuf_tensor("wmbuf", [128, 2 * 512], F32R) as wmbuf,
            nc.sbuf_tensor("aTb", [RANK, D], F32R) as aTb,
            nc.sbuf_tensor("bTb", [128, NDC * RANK], F32R) as bTb,
            nc.sbuf_tensor("ubuf", [RANK, 3 * TC], F32R) as ubuf,
            nc.sbuf_tensor("kvout", [128, 2 * TC], F32R) as kvout,
        ):
            t_w = [T("wb0"), T("wb1")]
            t_wm = [T("wm0"), T("wm1")]
            t_aT = T("aT")
            t_bT = T("bT")
            t_u = {}
            t_kv = [T("kv0"), T("kv1")]
            mk.dma("sync", rowbuf_sb[:, :], di_["av_brow"][:, :], writes=[t_rowbuf])

            for ui, n in enumerate(("q", "k", "v")):
                t_u[n] = T(f"u_{n}")
                mk.dma("sync", bTb[:, :], di_[f"a{n}_bT"][:, :], writes=[t_bT])
                mms = []
                for di in range(NDC):
                    for qt in range(2):
                        mms.append((ps[0:RANK, 512 * qt:512 * (qt + 1)],
                                    bTb[:, RANK * di:RANK * (di + 1)],
                                    y_sb[:, TC * di + 512 * qt:TC * di + 512 * (qt + 1)],
                                    di == 0, di == NDC - 1, [t_bT, t_y[di]]))
                tp = pst(0, 1024)
                mk.g_start((0, 512)); mk.g_start((512, 512))
                mk.mm_group(tp, mms)
                mk.g_stop((0, 512)); mk.g_stop((512, 512))
                mk.op("scalar", lambda o=ubuf[:, TC * ui:TC * (ui + 1)],
                                 i=ps[0:RANK, 0:TC]:
                      nc.scalar.activation(out=o, in_=i, func=AF.Copy, scale=LORA_SC),
                      reads=tp, writes=[t_u[n]])

            kpos = [0, 0]
            for hg in range(4):
                mk.dma("sync", aTb[:, :], di_["ak_aT"][:, :], writes=[t_aT])
                for j in range(4):
                    d = 4 * hg + j
                    sl = d % 2
                    base = 1024 * sl
                    mk.dma("sync", wbuf[:, D * sl:D * (sl + 1)],
                           di_["ak_w"][d, :, :], writes=[t_w[sl]])
                    tp = pst(base, 1024)
                    mk.g_start((base, 512)); mk.g_start((base + 512, 512))
                    mms = []
                    for di in range(NDC):
                        for qt in range(2):
                            mms.append((ps[:, base + 512 * qt:base + 512 * (qt + 1)],
                                        wbuf[:, D * sl + 128 * di:D * sl + 128 * (di + 1)],
                                        y_sb[:, TC * di + 512 * qt:TC * di + 512 * (qt + 1)],
                                        di == 0, False, [t_w[sl], t_y[di]]))
                    for qt in range(2):
                        mms.append((ps[:, base + 512 * qt:base + 512 * (qt + 1)],
                                    aTb[0:RANK, 128 * d:128 * (d + 1)],
                                    ubuf[0:RANK, TC * 1 + 512 * qt:TC * 1 + 512 * (qt + 1)],
                                    False, True, [t_aT, t_u["k"]]))
                    mk.mm_group(tp, mms)
                    mk.g_stop((base, 512)); mk.g_stop((base + 512, 512))
                    mk.op("scalar", lambda o=kvout[:, TC * sl:TC * (sl + 1)],
                                     i=ps[:, base:base + TC], b=bias_sb["k"][:, d:d + 1]:
                          nc.scalar.activation(out=o, in_=i, func=AF.Identity, bias=b),
                          reads=tp, writes=[t_kv[sl]])
                    mk.dma("gpsimd", cca_in[hg][128 * j:128 * (j + 1), :],
                           kvout[:, TC * sl:TC * (sl + 1)],
                           reads=[t_kv[sl]], writes=[t_cca_in[hg]])
                mk.dma("sync", aTb[:, :], di_["av_aT"][:, :], writes=[t_aT])
                tvs = [pst(512 * t, 512) for t in range(8)]
                for t in range(8):
                    mk.g_start((512 * t, 512))
                for di in range(NDC):
                    sl = di % 2
                    mk.dma("sync", wmbuf[:, 512 * sl:512 * (sl + 1)],
                           di_["av_wm"][128 * di:128 * (di + 1), 512 * hg:512 * (hg + 1)],
                           writes=[t_wm[sl]])
                    mms = []
                    for t in range(8):
                        mms.append((ps[:, 512 * t:512 * (t + 1)],
                                    y_sb[:, TC * di + 128 * t:TC * di + 128 * (t + 1)],
                                    wmbuf[:, 512 * sl:512 * (sl + 1)],
                                    di == 0, False, [t_wm[sl], t_y[di]]))
                    mk.mm_group(pst(0, 4096), mms)
                for t in range(8):
                    mms = [(ps[:, 512 * t:512 * (t + 1)],
                            ubuf[0:RANK, TC * 2 + 128 * t:TC * 2 + 128 * (t + 1)],
                            aTb[0:RANK, 512 * hg:512 * (hg + 1)],
                            False, False, [t_aT, t_u["v"]]),
                           (ps[:, 512 * t:512 * (t + 1)],
                            row1_sb[0:1, 0:128],
                            rowbuf_sb[0:1, 512 * hg:512 * (hg + 1)],
                            False, True, [t_consts, t_rowbuf])]
                    mk.mm_group(tvs[t] if t > 0 else tvs[0], mms)
                for t in range(8):
                    mk.g_stop((512 * t, 512))
                    sl = t % 2
                    mk.op("scalar", lambda o=kvout[:, TC * sl + 512 * 0:TC * sl + 512],
                                     i=ps[:, 512 * t:512 * (t + 1)]:
                          nc.scalar.copy(out=o, in_=i),
                          reads=tvs[t], writes=[t_kv[sl]])
                    dst = cca_in[hg].rearrange("a b -> (a b)")[
                        512 * TC + t * 65536: 512 * TC + (t + 1) * 65536]
                    mk.dma("gpsimd", dst.rearrange("(p c) -> p c", p=128),
                           kvout[:, TC * sl:TC * sl + 512],
                           reads=[t_kv[sl]], writes=[t_cca_in[hg]])
                mk.op("gpsimd", lambda hg=hg: nc.gpsimd.collective_compute(
                    "AllGather", ALU.bypass,
                    replica_groups=[[0, 1, 2, 3], [4, 5, 6, 7]],
                    ins=[cca_in[hg].rearrange("a b -> (a b)").opt()],
                    outs=[cca_out[hg].rearrange("a b -> (a b)").opt()]),
                    reads=[t_cca_in[hg]], writes=[t_cca_out[hg]],
                    sem="sCC", is_async=True)

            mk.dma("sync", aTb[:, :], di_["aq_aT"][:, :], writes=[t_aT])
            for d in range(NDC):
                sl = d % 2
                base = 1024 * sl
                mk.dma("sync", wbuf[:, D * sl:D * (sl + 1)], di_["aq_w"][d, :, :],
                       writes=[t_w[sl]])
                tp = pst(base, 1024)
                mk.g_start((base, 512)); mk.g_start((base + 512, 512))
                mms = []
                for di in range(NDC):
                    for qt in range(2):
                        mms.append((ps[:, base + 512 * qt:base + 512 * (qt + 1)],
                                    wbuf[:, D * sl + 128 * di:D * sl + 128 * (di + 1)],
                                    y_sb[:, TC * di + 512 * qt:TC * di + 512 * (qt + 1)],
                                    di == 0, False, [t_w[sl], t_y[di]]))
                for qt in range(2):
                    mms.append((ps[:, base + 512 * qt:base + 512 * (qt + 1)],
                                aTb[0:RANK, 128 * d:128 * (d + 1)],
                                ubuf[0:RANK, 512 * qt:512 * (qt + 1)],
                                False, True, [t_aT, t_u["q"]]))
                mk.mm_group(tp, mms)
                mk.g_stop((base, 512)); mk.g_stop((base + 512, 512))
                mk.op("scalar", lambda o=kvout[:, TC * sl:TC * (sl + 1)],
                                 i=ps[:, base:base + TC], b=bias_sb["q"][:, d:d + 1]:
                      nc.scalar.activation(out=o, in_=i, func=AF.Identity, bias=b),
                      reads=tp, writes=[t_kv[sl]])
                mk.dma("gpsimd", qa_sp[128 * d:128 * (d + 1), :],
                       kvout[:, TC * sl:TC * (sl + 1)],
                       reads=[t_kv[sl]], writes=[t_qa])

        if _DBG:
            mk.dma("gpsimd", dbg["dbg_q"][:, :], qa_sp[:, :], reads=[t_qa], writes=[])
            mk.dma("gpsimd", dbg["dbg_kv"][:, :], cca_in[0][:, :],
                   reads=[t_cca_in[0]], writes=[])

        t_ao = [T(f"ao{i}") for i in range(NDC)]
        mk.phase_barrier()
        with (
            nc.sbuf_tensor("kbuf", [128, 2 * 2048], F32R) as kbuf,
            nc.sbuf_tensor("vbuf", [128, 2 * 2048], F32R) as vbuf,
            nc.sbuf_tensor("qbuf", [128, 2 * TC], F32R) as qbuf,
            nc.sbuf_tensor("pbuf", [128, 2 * TC], F32R) as pbuf,
        ):
            t_k = [T("k0"), T("k1")]
            t_v = [T("v0"), T("v1")]
            t_q = [T("q0"), T("q1")]
            t_p = [T("p0"), T("p1")]
            t_out = pst(2048, 1024)
            t_l = pst(3072, 1024)

            def load_kv_half(h, hf, sl):
                hg, hh = h // 4, h % 4
                for r2 in range(2):
                    r = 2 * hf + r2
                    mk.dma("sync", kbuf[:, 2048 * sl + 1024 * r2:2048 * sl + 1024 * (r2 + 1)],
                           cca_out[hg][1024 * r + 128 * hh:1024 * r + 128 * (hh + 1), :],
                           reads=[t_cca_out[hg]], writes=[t_k[sl]])
                    src = cca_out[hg].rearrange("a b -> (a b)")
                    base = (1024 * r + 512) * TC + 128 * hh
                    sap = bass.AP(src.tensor, src.offset + base,
                                  [[512, 128], [65536, 8], [1, 128]])
                    dap = bass.AP(vbuf.ap().tensor, vbuf.ap().offset + 2048 * sl + 1024 * r2,
                                  [[4096, 128], [128, 8], [1, 128]])
                    mk.dma("sync", dap, sap, reads=[t_cca_out[hg]], writes=[t_v[sl]])

            for h in range(HEADS):
                qsl = h % 2
                mk.dma("sync", qbuf[:, TC * qsl:TC * (qsl + 1)],
                       qa_sp[128 * h:128 * (h + 1), :], reads=[t_qa], writes=[t_q[qsl]])
                if h == 0:
                    load_kv_half(0, 0, 0)
                mk.g_start((2048, 512)); mk.g_start((2560, 512))

                def lav_a(kc, psl, sl, kk):
                    lm = []
                    for qt in range(2):
                        lm.append((ps[0:2, 3072 + 512 * qt:3584 + 512 * qt],
                                   cst_sb[:, 0:2],
                                   pbuf[:, TC * psl + 512 * qt:TC * psl + 512 * (qt + 1)],
                                   kc == 0, kc == 31, [t_p[psl], t_consts]))
                    mk.mm_group(t_l, lm)
                    av = []
                    for qt in range(2):
                        av.append((ps[:, 2048 + 512 * qt:2048 + 512 * (qt + 1)],
                                   vbuf[:, 2048 * sl + 128 * kk:2048 * sl + 128 * (kk + 1)],
                                   pbuf[:, TC * psl + 512 * qt:TC * psl + 512 * (qt + 1)],
                                   kc == 0, kc == 31, [t_v[sl], t_p[psl]]))
                    mk.mm_group(t_out, av)

                pend = None
                for kc in range(32):
                    hf = kc // 16
                    sl = hf
                    kk = kc % 16
                    psl = kc % 2
                    base = 1024 * psl
                    tp = pst(base, 1024)
                    mk.g_start((base, 512)); mk.g_start((base + 512, 512))
                    mms = []
                    for qt in range(2):
                        mms.append((ps[:, base + 512 * qt:base + 512 * (qt + 1)],
                                    kbuf[:, 2048 * sl + 128 * kk:2048 * sl + 128 * (kk + 1)],
                                    qbuf[:, TC * qsl + 512 * qt:TC * qsl + 512 * (qt + 1)],
                                    True, True, [t_k[sl], t_q[qsl]]))
                    mk.mm_group(tp, mms)
                    mk.g_stop((base, 512)); mk.g_stop((base + 512, 512))
                    mk.op("scalar", lambda o=pbuf[:, TC * psl:TC * (psl + 1)],
                                     i=ps[:, base:base + TC]:
                          nc.scalar.activation(out=o, in_=i, func=AF.Exp,
                                               scale=SCALE, bias=-BEXP),
                          reads=tp, writes=[t_p[psl]])
                    if pend is not None:
                        lav_a(*pend)
                    pend = (kc, psl, sl, kk)
                    if kc % 16 == 0:
                        if kc == 0:
                            load_kv_half(h, 1, 1)
                        else:
                            if h < HEADS - 1:
                                load_kv_half(h + 1, 0, 0)
                lav_a(*pend)
                mk.g_stop((2048, 512)); mk.g_stop((2560, 512))
                emit_norm_head(h, t_out, t_l, t_ao[h])

        if _DBG:
            mk.op("scalar", lambda: nc.scalar.copy(out=bc_sb[:, 0:128], in_=ps[:, 3072:3200]),
                  reads=pst(3072, 512), writes=[t_bc])
            mk.dma("gpsimd", dbg["dbg_l"][:, :], bc_sb[:, 0:128], reads=[t_bc], writes=[])
            for di in range(NDC):
                mk.dma("gpsimd", dbg["dbg_ao"][128 * di:128 * (di + 1), :],
                       y_sb[:, TC * di:TC * (di + 1)], reads=[t_ao[di]], writes=[])

        emit_oproj_a(t_ao)

    def emit_norm_head(h, t_out, t_l, t_aoh):
        mk.op("scalar", lambda: nc.scalar.activation(
            out=rows_sb[:, TC:2 * TC], in_=ps[0:1, 3072:4096], func=AF.Ln),
            reads=t_l, writes=[t_stat])
        mk.op("scalar", lambda: nc.scalar.activation(
            out=rows_sb[:, 0:TC], in_=rows_sb[:, TC:2 * TC], func=AF.Exp, scale=-1.0),
            reads=[t_stat], writes=[t_lrow])
        for qt in range(2):
            t_b = pst(512 * qt, 512)
            mk.mm_group(t_b, [(ps[:, 512 * qt:512 * (qt + 1)], row1F_sb[0:1, 0:128],
                               rows_sb[0:1, 512 * qt:512 * (qt + 1)],
                               True, True, [t_lrow, t_consts])])
            mk.op("scalar", lambda qt=qt: nc.scalar.copy(
                out=bc_sb[:, :], in_=ps[:, 512 * qt:512 * (qt + 1)]),
                reads=t_b, writes=[t_bc])
            mk.op("vector",
                  lambda o=y_sb[:, TC * h + 512 * qt:TC * h + 512 * (qt + 1)],
                         i=ps[:, 2048 + 512 * qt:2048 + 512 * (qt + 1)]:
                  nc.vector.tensor_tensor(out=o, in0=i, in1=bc_sb[:, :], op=ALU.mult),
                  reads=pst(2048, 1024) + [t_bc], writes=[t_aoh])

    def emit_oproj_a(t_ao):
        mk.phase_barrier()
        with (
            nc.sbuf_tensor("wbuf2", [128, 2 * D], F32R) as wbuf,
            nc.sbuf_tensor("aTb2", [RANK, D], F32R) as aTb,
            nc.sbuf_tensor("bTb2", [128, NDC * RANK], F32R) as bTb,
            nc.sbuf_tensor("ubuf2", [RANK, TC], F32R) as ubuf,
        ):
            t_w = [T("w20"), T("w21")]
            t_aT = T("aT2")
            t_bT = T("bT2")
            t_u = T("u_o")
            mk.dma("sync", rowbuf_sb[:, :], di_["ao_brow"][:, :], writes=[t_rowbuf])
            mk.dma("sync", bTb[:, :], di_["ao_bT"][:, :], writes=[t_bT])
            mk.dma("sync", aTb[:, :], di_["ao_aT"][:, :], writes=[t_aT])
            mms = []
            for di in range(NDC):
                for qt in range(2):
                    mms.append((ps[0:RANK, 512 * qt:512 * (qt + 1)],
                                bTb[:, RANK * di:RANK * (di + 1)],
                                y_sb[:, TC * di + 512 * qt:TC * di + 512 * (qt + 1)],
                                di == 0, di == NDC - 1, [t_bT, t_ao[di]]))
            tp = pst(0, 1024)
            mk.g_start((0, 512)); mk.g_start((512, 512))
            mk.mm_group(tp, mms)
            mk.g_stop((0, 512)); mk.g_stop((512, 512))
            mk.op("scalar", lambda: nc.scalar.activation(
                out=ubuf[:, :], in_=ps[0:RANK, 0:TC], func=AF.Copy, scale=LORA_SC),
                reads=tp, writes=[t_u])
            for d in range(NDC):
                sl = d % 2
                base = 1024 * sl
                mk.dma("sync", wbuf[:, D * sl:D * (sl + 1)], di_["ao_w"][d, :, :],
                       writes=[t_w[sl]])
                tp = pst(base, 1024)
                mk.g_start((base, 512)); mk.g_start((base + 512, 512))
                mms = []
                for di in range(NDC):
                    for qt in range(2):
                        mms.append((ps[:, base + 512 * qt:base + 512 * (qt + 1)],
                                    wbuf[:, D * sl + 128 * di:D * sl + 128 * (di + 1)],
                                    y_sb[:, TC * di + 512 * qt:TC * di + 512 * (qt + 1)],
                                    di == 0, False, [t_w[sl], t_ao[di]]))
                for qt in range(2):
                    mms.append((ps[:, base + 512 * qt:base + 512 * (qt + 1)],
                                aTb[0:RANK, 128 * d:128 * (d + 1)],
                                ubuf[0:RANK, 512 * qt:512 * (qt + 1)],
                                False, False, [t_aT, t_u]))
                    mms.append((ps[:, base + 512 * qt:base + 512 * (qt + 1)],
                                rowbuf_sb[0:1, 128 * d:128 * (d + 1)],
                                row1_sb[0:1, 0:512],
                                False, True, [t_rowbuf, t_consts]))
                mk.mm_group(tp, mms)
                mk.g_stop((base, 512)); mk.g_stop((base + 512, 512))
                mk.op("vector", lambda o=x_sb[:, TC * d:TC * (d + 1)],
                                 i=ps[:, base:base + TC]:
                      nc.vector.tensor_tensor(out=o, in0=o, in1=i, op=ALU.add),
                      reads=tp + [t_x[d]], writes=[t_x[d]])

    _ffn_n = [0]

    def emit_ffn(final=False):
        _ffn_n[0] += 1
        sfx = str(_ffn_n[0])
        mk.phase_barrier()
        with (
            nc.sbuf_tensor("wgb" + sfx, [128, 2 * D], F32R) as wgb,
            nc.sbuf_tensor("wub" + sfx, [128, 2 * D], F32R) as wub,
            nc.sbuf_tensor("ssb" + sfx, [128, 2 * TC], F32R) as ssb,
            nc.sbuf_tensor("hb" + sfx, [128, 2 * TC], F32R) as hb,
        ):
            t_wg = [T("wg0"), T("wg1")]
            t_wu = [T("wu0"), T("wu1")]
            t_ss = [T("ss0"), T("ss1")]
            t_hb = [T("hb0"), T("hb1")]
            for hc in range(NHC):
                sl = hc % 2
                gb = 2048 * sl
                ub = 2048 * sl + 1024
                mk.dma("sync", wgb[:, D * sl:D * (sl + 1)], di_["fg_w"][hc, :, :],
                       writes=[t_wg[sl]])
                mk.dma("sync", wub[:, D * sl:D * (sl + 1)], di_["fu_w"][hc, :, :],
                       writes=[t_wu[sl]])
                tg = pst(gb, 1024)
                tu = pst(ub, 1024)
                for b0, tt, wb, t_wx in ((gb, tg, wgb, t_wg), (ub, tu, wub, t_wu)):
                    mk.g_start((b0, 512)); mk.g_start((b0 + 512, 512))
                    mms = []
                    for di in range(NDC):
                        for qt in range(2):
                            mms.append((ps[:, b0 + 512 * qt:b0 + 512 * (qt + 1)],
                                        wb[:, D * sl + 128 * di:D * sl + 128 * (di + 1)],
                                        y_sb[:, TC * di + 512 * qt:TC * di + 512 * (qt + 1)],
                                        di == 0, di == NDC - 1, [t_wx[sl], t_y[di]]))
                    mk.mm_group(tt, mms)
                    mk.g_stop((b0, 512)); mk.g_stop((b0 + 512, 512))
                mk.op("scalar", lambda o=ssb[:, TC * sl:TC * (sl + 1)],
                                 i=ps[:, gb:gb + TC]:
                      nc.scalar.activation(out=o, in_=i, func=AF.Silu),
                      reads=tg, writes=[t_ss[sl]])
                mk.op("vector", lambda o=hb[:, TC * sl:TC * (sl + 1)],
                                 i0=ssb[:, TC * sl:TC * (sl + 1)], i1=ps[:, ub:ub + TC]:
                      nc.vector.tensor_tensor(out=o, in0=i0, in1=i1, op=ALU.mult),
                      reads=[t_ss[sl]] + tu, writes=[t_hb[sl]])
                mk.dma("gpsimd", h_sp[128 * hc:128 * (hc + 1), :],
                       hb[:, TC * sl:TC * (sl + 1)],
                       reads=[t_hb[sl]], writes=[t_h[hc]])
        mk.phase_barrier()
        with (
            nc.sbuf_tensor("hrb" + sfx, [128, 2 * TC], F32R) as hrb,
            nc.sbuf_tensor("wdb" + sfx, [128, 2 * 512], F32R) as wdb,
        ):
            t_hr = [T("hr0"), T("hr1")]
            t_wd = [T("wd0"), T("wd1")]
            for dq in range(4):
                tps = [pst(1024 * dc, 1024) for dc in range(4)]
                for dc in range(4):
                    mk.g_start((1024 * dc, 512)); mk.g_start((1024 * dc + 512, 512))
                for hc in range(NHC):
                    sl = hc % 2
                    eng = "sync" if hc % 2 == 0 else "scalar"
                    mk.dma(eng, hrb[:, TC * sl:TC * (sl + 1)],
                           h_sp[128 * hc:128 * (hc + 1), :],
                           reads=[t_h[hc]], writes=[t_hr[sl]])
                    mk.dma(eng, wdb[:, 512 * sl:512 * (sl + 1)],
                           di_["fd_w"][dq, hc, :, :], writes=[t_wd[sl]])
                    for dc in range(4):
                        mms = []
                        for qt in range(2):
                            mms.append((ps[:, 1024 * dc + 512 * qt:1024 * dc + 512 * (qt + 1)],
                                        wdb[:, 512 * sl + 128 * dc:512 * sl + 128 * (dc + 1)],
                                        hrb[:, TC * sl + 512 * qt:TC * sl + 512 * (qt + 1)],
                                        hc == 0, hc == NHC - 1, [t_wd[sl], t_hr[sl]]))
                        mk.mm_group(tps[dc], mms)
                for dc in range(4):
                    mk.g_stop((1024 * dc, 512)); mk.g_stop((1024 * dc + 512, 512))
                    d = 4 * dq + dc
                    mk.op("vector", lambda o=x_sb[:, TC * d:TC * (d + 1)],
                                     i=ps[:, 1024 * dc:1024 * dc + TC]:
                          nc.vector.tensor_tensor(out=o, in0=o, in1=i, op=ALU.add),
                          reads=tps[dc] + [t_x[d]], writes=[t_x[d]])
                    if final:
                        mk.dma("gpsimd", out_fm[128 * d:128 * (d + 1), :],
                               x_sb[:, TC * d:TC * (d + 1)],
                               reads=[t_x[d]], writes=[])

    def emit_attn_b():
        mk.phase_barrier()
        with (
            nc.sbuf_tensor("wbuf3", [128, 2 * D], F32R) as wbuf,
            nc.sbuf_tensor("wmb3", [128, 2 * 512], F32R) as wmb,
            nc.sbuf_tensor("cosb", [128, TC], F32) as cosb,
            nc.sbuf_tensor("sinb", [128, TC], F32) as sinb,
            nc.sbuf_tensor("ktmp", [128, 2 * TC], F32R) as ktmp,
            nc.sbuf_tensor("krot", [128, 2 * TC], F32R) as krot,
        ):
            t_w = [T("w30"), T("w31")]
            t_wm = [T("wm30"), T("wm31")]
            t_cos = T("cos")
            t_ktmp = [T("kt0"), T("kt1")]
            t_krot = [T("kr0"), T("kr1")]
            mk.dma("sync", cosb[:, :], di_["rope_cos"][:, :], writes=[t_cos])
            mk.dma("sync", sinb[:, :], di_["rope_sinS"][:, :], writes=[t_cos])

            def rope_chunk(sl, tp_in, base):
                tsh = pst(2048, 1024)
                mk.g_start((2048, 512)); mk.g_start((2560, 512))
                mms = []
                for qt in range(2):
                    mms.append((ps[:, 2048 + 512 * qt:2048 + 512 * (qt + 1)],
                                perm_sb[:, :],
                                ktmp[:, TC * sl + 512 * qt:TC * sl + 512 * (qt + 1)],
                                True, True, [t_consts, t_ktmp[sl]]))
                mk.mm_group(tsh, mms)
                mk.g_stop((2048, 512)); mk.g_stop((2560, 512))
                mk.op("vector", lambda o=krot[:, TC * sl:TC * (sl + 1)],
                                 i=ktmp[:, TC * sl:TC * (sl + 1)]:
                      nc.vector.tensor_tensor(out=o, in0=i, in1=cosb[:, :], op=ALU.mult),
                      reads=[t_ktmp[sl], t_cos], writes=[t_krot[sl]])
                mk.op("vector", lambda o=ktmp[:, TC * sl:TC * (sl + 1)],
                                 i=ps[:, 2048:2048 + TC]:
                      nc.vector.scalar_tensor_tensor(out=o, in0=i, scalar=1.0,
                                                     in1=sinb[:, :],
                                                     op0=ALU.mult, op1=ALU.mult),
                      reads=tsh + [t_cos], writes=[t_ktmp[sl]])
                mk.op("vector", lambda o=krot[:, TC * sl:TC * (sl + 1)],
                                 i=ktmp[:, TC * sl:TC * (sl + 1)]:
                      nc.vector.tensor_tensor(out=o, in0=o, in1=i, op=ALU.add),
                      reads=[t_ktmp[sl], t_krot[sl]], writes=[t_krot[sl]])

            for dkv in range(NKV):
                sl = dkv % 2
                base = 1024 * sl
                mk.dma("sync", wbuf[:, D * sl:D * (sl + 1)], di_["bk_w"][dkv, :, :],
                       writes=[t_w[sl]])
                tp = pst(base, 1024)
                mk.g_start((base, 512)); mk.g_start((base + 512, 512))
                mms = []
                for di in range(NDC):
                    for qt in range(2):
                        mms.append((ps[:, base + 512 * qt:base + 512 * (qt + 1)],
                                    wbuf[:, D * sl + 128 * di:D * sl + 128 * (di + 1)],
                                    y_sb[:, TC * di + 512 * qt:TC * di + 512 * (qt + 1)],
                                    di == 0, di == NDC - 1, [t_w[sl], t_y[di]]))
                mk.mm_group(tp, mms)
                mk.g_stop((base, 512)); mk.g_stop((base + 512, 512))
                mk.op("scalar", lambda o=ktmp[:, TC * sl:TC * (sl + 1)],
                                 i=ps[:, base:base + TC]:
                      nc.scalar.copy(out=o, in_=i), reads=tp, writes=[t_ktmp[sl]])
                rope_chunk(sl, tp, base)
                mk.dma("gpsimd", ccb_in[128 * dkv:128 * (dkv + 1), :],
                       krot[:, TC * sl:TC * (sl + 1)],
                       reads=[t_krot[sl]], writes=[t_ccb_in])
                mk.dma("gpsimd", kloc_sp[128 * dkv:128 * (dkv + 1), :],
                       krot[:, TC * sl:TC * (sl + 1)],
                       reads=[t_krot[sl]], writes=[t_kloc])
            tvs = [pst(512 * t, 512) for t in range(8)]
            for t in range(8):
                mk.g_start((512 * t, 512))
            for di in range(NDC):
                sl = di % 2
                mk.dma("sync", wmb[:, 512 * sl:512 * (sl + 1)],
                       di_["bv_wm"][128 * di:128 * (di + 1), :], writes=[t_wm[sl]])
                mms = []
                for t in range(8):
                    mms.append((ps[:, 512 * t:512 * (t + 1)],
                                y_sb[:, TC * di + 128 * t:TC * di + 128 * (t + 1)],
                                wmb[:, 512 * sl:512 * (sl + 1)],
                                di == 0, di == NDC - 1, [t_wm[sl], t_y[di]]))
                mk.mm_group(pst(0, 4096), mms)
            for t in range(8):
                mk.g_stop((512 * t, 512))
                sl = t % 2
                mk.op("scalar", lambda o=ktmp[:, TC * sl:TC * sl + 512],
                                 i=ps[:, 512 * t:512 * (t + 1)]:
                      nc.scalar.copy(out=o, in_=i),
                      reads=tvs[t], writes=[t_ktmp[sl]])
                dst = ccb_in.rearrange("a b -> (a b)")[
                    512 * TC + t * 65536: 512 * TC + (t + 1) * 65536]
                mk.dma("gpsimd", dst.rearrange("(p c) -> p c", p=128),
                       ktmp[:, TC * sl:TC * sl + 512],
                       reads=[t_ktmp[sl]], writes=[t_ccb_in])
                mk.dma("gpsimd", vloc_sp[128 * t:128 * (t + 1), :],
                       ktmp[:, TC * sl:TC * sl + 512],
                       reads=[t_ktmp[sl]], writes=[t_vloc])
            mk.op("gpsimd", lambda: nc.gpsimd.collective_compute(
                "AllGather", ALU.bypass,
                replica_groups=[[0, 1, 2, 3], [4, 5, 6, 7]],
                ins=[ccb_in.rearrange("a b -> (a b)").opt()],
                outs=[ccb_out.rearrange("a b -> (a b)").opt()]),
                reads=[t_ccb_in], writes=[t_ccb_out], sem="sCC", is_async=True)
            for d in range(NDC):
                sl = d % 2
                base = 1024 * sl
                mk.dma("sync", wbuf[:, D * sl:D * (sl + 1)], di_["bq_w"][d, :, :],
                       writes=[t_w[sl]])
                tp = pst(base, 1024)
                mk.g_start((base, 512)); mk.g_start((base + 512, 512))
                mms = []
                for di in range(NDC):
                    for qt in range(2):
                        mms.append((ps[:, base + 512 * qt:base + 512 * (qt + 1)],
                                    wbuf[:, D * sl + 128 * di:D * sl + 128 * (di + 1)],
                                    y_sb[:, TC * di + 512 * qt:TC * di + 512 * (qt + 1)],
                                    di == 0, di == NDC - 1, [t_w[sl], t_y[di]]))
                mk.mm_group(tp, mms)
                mk.g_stop((base, 512)); mk.g_stop((base + 512, 512))
                mk.op("scalar", lambda o=ktmp[:, TC * sl:TC * (sl + 1)],
                                 i=ps[:, base:base + TC]:
                      nc.scalar.copy(out=o, in_=i), reads=tp, writes=[t_ktmp[sl]])
                rope_chunk(sl, tp, base)
                mk.dma("gpsimd", qb_sp[128 * d:128 * (d + 1), :],
                       krot[:, TC * sl:TC * (sl + 1)],
                       reads=[t_krot[sl]], writes=[t_qb])

        mk.phase_barrier()
        with (
            nc.sbuf_tensor("kbufb", [128, TF], F32R) as kbufb,
            nc.sbuf_tensor("vbufb", [128, TF], F32R) as vbufb,
            nc.sbuf_tensor("qbufb", [128, TC], F32R) as qbufb,
            nc.sbuf_tensor("pbufb", [128, 2 * TC], F32R) as pbufb,
            nc.sbuf_tensor("kband", [128, 512], F32R) as kband,
            nc.sbuf_tensor("vband", [128, 512], F32R) as vband,
            nc.sbuf_tensor("dmb", [128, 2048], F32) as dmb,
        ):
            t_kb = T("kbB")
            t_vb = T("vbB")
            t_q = [T("qB0"), T("qB1")]
            t_pb = [T(f"pBr{i}") for i in range(4)]
            t_kbd = [T("kbd0"), T("kbd1")]
            t_vbd = [T("vbd0"), T("vbd1")]
            t_dm = T("dmask")
            t_out = pst(2048, 1024)
            t_l = pst(3072, 1024)
            t_ao = [T(f"aob{i}") for i in range(NDC)]
            for _j in range(4):
                mk.dma("sync", dmb[:, 512 * _j:512 * (_j + 1)],
                       di_["dmask"][128 * _j:128 * (_j + 1), :], writes=[t_dm])

            def load_group(g):
                for r in range(4):
                    mk.dma("sync", kbufb[:, TC * r:TC * (r + 1)],
                           ccb_out[TC * r + 128 * g:TC * r + 128 * (g + 1), :],
                           reads=[t_ccb_out], writes=[t_kb])
                    src = ccb_out.rearrange("a b -> (a b)")
                    base = (TC * r + 512) * TC + 128 * g
                    sap = bass.AP(src.tensor, src.offset + base,
                                  [[512, 128], [65536, 8], [1, 128]])
                    dap = bass.AP(vbufb.ap().tensor, vbufb.ap().offset + TC * r,
                                  [[TF, 128], [128, 8], [1, 128]])
                    mk.dma("sync", dap, sap, reads=[t_ccb_out], writes=[t_vb])

            for h in range(HEADS):
                g = h // 4
                qsl = 0
                if h % 4 == 0:
                    load_group(g)
                mk.dma("sync", qbufb[:, 0:TC],
                       qb_sp[128 * h:128 * (h + 1), :], reads=[t_qb], writes=[t_q[0]])
                mk.g_start((2048, 512)); mk.g_start((2560, 512))

                def lav_b(kc, psl):
                    lm = []
                    for qt in range(2):
                        lm.append((ps[0:2, 3072 + 512 * qt:3584 + 512 * qt],
                                   cst_sb[:, 0:2],
                                   pbufb[:, TC * psl + 512 * qt:TC * psl + 512 * (qt + 1)],
                                   kc == 0, False, [t_pb[2 * psl + qt], t_consts]))
                    mk.mm_group(t_l, lm)
                    av = []
                    for qt in range(2):
                        av.append((ps[:, 2048 + 512 * qt:2048 + 512 * (qt + 1)],
                                   vbufb[:, 128 * kc:128 * (kc + 1)],
                                   pbufb[:, TC * psl + 512 * qt:TC * psl + 512 * (qt + 1)],
                                   kc == 0, False, [t_vb, t_pb[2 * psl + qt]]))
                    mk.mm_group(t_out, av)

                pend = None
                for kc in range(32):
                    psl = kc % 2
                    base = 1024 * psl
                    tp = pst(base, 1024)
                    mk.g_start((base, 512)); mk.g_start((base + 512, 512))
                    mms = []
                    for qt in range(2):
                        mms.append((ps[:, base + 512 * qt:base + 512 * (qt + 1)],
                                    kbufb[:, 128 * kc:128 * (kc + 1)],
                                    qbufb[:, 512 * qt:512 * (qt + 1)],
                                    True, True, [t_kb, t_q[qsl]]))
                    mk.mm_group(tp, mms)
                    mk.g_stop((base, 512)); mk.g_stop((base + 512, 512))
                    for qt in range(2):
                        mk.op("scalar",
                              lambda o=pbufb[:, TC * psl + 512 * qt:TC * psl + 512 * (qt + 1)],
                                     i=ps[:, base + 512 * qt:base + 512 * (qt + 1)],
                                     b=btbl_sb[:, 32 * qt + kc:32 * qt + kc + 1]:
                              nc.scalar.activation(out=o, in_=i, func=AF.Exp,
                                                   scale=SCALE, bias=b),
                              reads=tp + [t_consts], writes=[t_pb[2 * psl + qt]])
                    if pend is not None:
                        lav_b(*pend)
                    pend = (kc, psl)
                lav_b(*pend)
                for qt in range(2):
                    bsl = 0
                    mk.dma("sync", kband[:, 0:512],
                           kloc_sp[128 * g:128 * (g + 1), 512 * qt:512 * (qt + 1)],
                           reads=[t_kloc], writes=[t_kbd[bsl]])
                    src = vloc_sp.rearrange("a b -> (a b)")
                    base_v = (512 * qt) * KVD + 128 * g
                    sap = bass.AP(src.tensor, src.offset + base_v,
                                  [[KVD, 128], [128 * KVD, 4], [1, 128]])
                    dap = bass.AP(vband.ap().tensor, vband.ap().offset,
                                  [[512, 128], [128, 4], [1, 128]])
                    mk.dma("sync", dap, sap, reads=[t_vloc], writes=[t_vbd[bsl]])

                    def lav_band(j, r, qt=qt, bsl=bsl):
                        mk.mm_group(t_l, [(ps[0:2, 3072 + 512 * qt:3584 + 512 * qt],
                                           cst_sb[:, 0:2],
                                           pbufb[:, 512 * r:512 * (r + 1)],
                                           False, j == 3, [t_pb[r], t_consts])])
                        mk.mm_group(t_out, [(ps[:, 2048 + 512 * qt:2048 + 512 * (qt + 1)],
                                             vband[:, 128 * j:128 * (j + 1)],
                                             pbufb[:, 512 * r:512 * (r + 1)],
                                             False, j == 3, [t_vbd[bsl], t_pb[r]])])

                    pendb = None
                    for j in range(4):
                        psl = j % 2
                        base = 1024 * psl
                        tp = pst(base, 512)
                        mk.g_start((base, 512))
                        mk.mm_group(tp, [(ps[:, base:base + 512],
                                          kband[:, 128 * j:128 * (j + 1)],
                                          qbufb[:, 512 * qt:512 * (qt + 1)],
                                          True, True, [t_kbd[bsl], t_q[qsl]])])
                        mk.g_stop((base, 512))
                        mk.op("vector", lambda o=ps[:, base:base + 512],
                                         m=dmb[:, 512 * j:512 * (j + 1)]:
                              nc.vector.tensor_tensor(out=o, in0=o, in1=m, op=ALU.add),
                              reads=tp + [t_dm], writes=tp)
                        mk.op("scalar",
                              lambda o=pbufb[:, 512 * psl:512 * (psl + 1)],
                                     i=ps[:, base:base + 512]:
                              nc.scalar.activation(out=o, in_=i, func=AF.Exp,
                                                   scale=SCALE, bias=-BEXP),
                              reads=tp, writes=[t_pb[psl]])
                        if pendb is not None:
                            lav_band(*pendb)
                        pendb = (j, j % 2)
                    lav_band(*pendb)
                mk.g_stop((2048, 512)); mk.g_stop((2560, 512))
                emit_norm_head(h, t_out, t_l, t_ao[h])

        mk.phase_barrier()
        with nc.sbuf_tensor("wbuf4", [128, 2 * D], F32R) as wbuf:
            t_w = [T("w40"), T("w41")]
            for d in range(NDC):
                sl = d % 2
                base = 1024 * sl
                mk.dma("sync", wbuf[:, D * sl:D * (sl + 1)], di_["bp_w"][d, :, :],
                       writes=[t_w[sl]])
                tp = pst(base, 1024)
                mk.g_start((base, 512)); mk.g_start((base + 512, 512))
                mms = []
                for di in range(NDC):
                    for qt in range(2):
                        mms.append((ps[:, base + 512 * qt:base + 512 * (qt + 1)],
                                    wbuf[:, D * sl + 128 * di:D * sl + 128 * (di + 1)],
                                    y_sb[:, TC * di + 512 * qt:TC * di + 512 * (qt + 1)],
                                    di == 0, di == NDC - 1, [t_w[sl], t_ao[di]]))
                mk.mm_group(tp, mms)
                mk.g_stop((base, 512)); mk.g_stop((base + 512, 512))
                mk.op("vector", lambda o=x_sb[:, TC * d:TC * (d + 1)],
                                 i=ps[:, base:base + TC]:
                      nc.vector.tensor_tensor(out=o, in0=o, in1=i, op=ALU.add),
                      reads=tp + [t_x[d]], writes=[t_x[d]])

    if _STAGES >= 1:
        emit_ln()
        if _DBG:
            for di in range(NDC):
                mk.dma("gpsimd", dbg["dbg_y"][128 * di:128 * (di + 1), :],
                       y_sb[:, TC * di:TC * (di + 1)], reads=[t_y[di]], writes=[])
        emit_attn_a()
    if _STAGES >= 2:
        emit_ln()
        emit_ffn()
    if _STAGES >= 3:
        emit_ln()
        emit_attn_b()
    if _STAGES >= 4:
        emit_ln()
        emit_ffn(final=True)

    for _ch in mk.chans["gpsimd"]:
        if mk.cnt[_ch]:
            mk._wait("gpsimd", _ch, mk.cnt[_ch])

    mk.final_cnt.update(mk.cnt)
    with nc.Block() as block:
        @block.tensor
        def _(te):
            for f in mk.ops["tensor"]:
                f()

        @block.scalar
        def _(sc):
            for f in mk.ops["scalar"]:
                f()

        @block.vector
        def _(ve):
            for f in mk.ops["vector"]:
                f()

        @block.gpsimd
        def _(gp):
            for f in mk.ops["gpsimd"]:
                f()

        @block.sync
        def _(sy):
            for f in mk.ops["sync"]:
                f()

    stack.close()
    return nc


_NC_CACHE = None


def kernel(**inputs):
    global _NC_CACHE
    in_maps = _prep_host(inputs)
    if _NC_CACHE is None:
        _NC_CACHE = build_nc()
    res = run_bass_kernel_spmd(_NC_CACHE, in_maps, core_ids=list(range(8)))
    out = np.empty((2, TF, D), dtype=np.float32)
    for c in range(8):
        b, g = c // 4, c % 4
        out[b, TC * g:TC * (g + 1), :] = res.results[c]["out_fm"].T
    return out


# version 07 (final)
# speedup vs baseline: 1.5207x; 1.0984x over previous
import numpy as np
from contextlib import ExitStack

import concourse.bass as bass
import concourse.mybir as mybir
from concourse.bass_utils import run_bass_kernel_spmd

F32 = mybir.dt.float32
F32R = mybir.dt.float32r
AF = mybir.ActivationFunctionType
ALU = mybir.AluOpType

D = 2048
NDC = 16
TC = 1024
NTC = 8
TF = 4096
HEADS = 16
HD = 128
RANK = 16
KVD = 512
NKV = 4
HID = 8192
NHC = 64
SCALE = float(HD) ** -0.5
BEXP = 16.0
EPS = 1e-5
LORA_SC = 0.5
NEG = -1.0e30

_STAGES = 4
_DBG = False


class Tile:
    __slots__ = ("name", "w", "readers")

    def __init__(self, name):
        self.name = name
        self.w = None
        self.readers = []


class MK:
    ENGINES = ("tensor", "scalar", "vector", "gpsimd", "sync")

    def __init__(self, nc, stack):
        self.nc = nc
        self.stack = stack
        self.ops = {e: [] for e in self.ENGINES}
        self.sems = {}
        self.cnt = {}
        for name in ("sTE", "sSC", "sVE", "sGP", "sSY", "sCC", "sSCD"):
            self.sems[name] = stack.enter_context(nc.semaphore(name))
            self.cnt[name] = 0
        self.chans = {}
        self.rr = {}
        for eng, n in (("sync", 6), ("scalar", 4), ("gpsimd", 4)):
            self.chans[eng] = []
            self.rr[eng] = 0
            for i in range(n):
                nm = f"ch_{eng}{i}"
                self.sems[nm] = stack.enter_context(nc.semaphore(nm))
                self.cnt[nm] = 0
                self.chans[eng].append(nm)
        self.esem = {"tensor": "sTE", "scalar": "sSC", "vector": "sVE",
                     "gpsimd": "sGP", "sync": "sSY"}
        self.last_wait = {e: {} for e in self.ENGINES}
        self.bank_open = {}
        self.final_cnt = {}

    def _wait(self, eng, sem_name, val):
        if val is None:
            raise RuntimeError(f"un-synced dependency for {eng}")
        if self.last_wait[eng].get(sem_name, 0) >= val:
            return
        self.last_wait[eng][sem_name] = val
        sem = self.sems[sem_name]
        nc = self.nc
        fc = self.final_cnt

        def run(sem=sem, val=val, eng=eng, sn=sem_name):
            getattr(nc, eng).wait_ge(sem, min(val, fc.get(sn, val)))
        self.ops[eng].append(run)

    _SELF_FENCE = ("scalar", "vector", "gpsimd")
    DMA_W = 4

    def _wval(self, sem_name, val, windowed):
        if not windowed:
            return val
        return val + 16 * (self.DMA_W - 1)

    def dep_waits(self, eng, reads, writes):
        for t in reads:
            if t.w is not None:
                we, ws, wv, wasync, wwin = t.w
                if we != eng or wasync or eng in self._SELF_FENCE:
                    self._wait(eng, ws, self._wval(ws, wv, wwin))
        for t in writes:
            if t.w is not None:
                we, ws, wv, wasync, wwin = t.w
                if we != eng or wasync or eng in self._SELF_FENCE:
                    self._wait(eng, ws, self._wval(ws, wv, wwin))
            for (re, rs, rv, rwin) in t.readers:
                if re != eng or eng in self._SELF_FENCE:
                    self._wait(eng, rs, self._wval(rs, rv, rwin))

    def op(self, eng, fn, reads=(), writes=(), amt=1, sem=None, is_async=False,
           windowed=False):
        if sem == "sCC" and self.cnt["sCC"]:
            self._wait(eng, "sCC", self.cnt["sCC"])
        self.dep_waits(eng, reads, writes)
        sem_name = sem or self.esem[eng]
        self.cnt[sem_name] += amt
        val = self.cnt[sem_name]
        s = self.sems[sem_name]

        def run(fn=fn, s=s, amt=amt):
            ins = fn()
            ins.then_inc(s, amt)
        self.ops[eng].append(run)
        for t in reads:
            t.readers.append((eng, sem_name, val, windowed))
        for t in writes:
            t.w = (eng, sem_name, val, is_async, windowed)
            t.readers = []
        return val

    def dma(self, eng, out_ap, in_ap, reads=(), writes=()):
        nc = self.nc
        ch = self.chans[eng][self.rr[eng] % len(self.chans[eng])]
        self.rr[eng] += 1
        if self.cnt[ch]:
            self._wait(eng, ch, self.cnt[ch])
        return self.op(
            eng, lambda o=out_ap, i=in_ap: getattr(nc, eng).dma_start(out=o, in_=i),
            reads=reads, writes=writes, amt=16, sem=ch, is_async=True, windowed=False)

    def phase_barrier(self):
        for q in self.ENGINES:
            for s in list(self.cnt):
                if s == self.esem[q]:
                    continue
                if self.cnt[s]:
                    self._wait(q, s, self.cnt[s])

    def g_start(self, region):
        assert not self.bank_open.get(region), f"region {region} already open"
        self.bank_open[region] = True

    def g_stop(self, region):
        assert self.bank_open.get(region), f"region {region} not open"
        self.bank_open[region] = False

    def mm_group(self, tregs, mms):
        nc = self.nc
        all_reads = []
        for (_, _, _, _, _, rts) in mms:
            all_reads.extend(rts)
        self.dep_waits("tensor", all_reads, tregs)
        sem_name = "sTE"
        self.cnt[sem_name] += 1
        val = self.cnt[sem_name]
        s = self.sems[sem_name]
        n = len(mms)
        for i, (o, lt, rh, st, sp, _) in enumerate(mms):
            if i < n - 1:
                def run(o=o, lt=lt, rh=rh, st=st, sp=sp):
                    nc.tensor.matmul(o, lt, rh, start=st, stop=sp)
            else:
                def run(o=o, lt=lt, rh=rh, st=st, sp=sp, s=s):
                    nc.tensor.matmul(o, lt, rh, start=st, stop=sp).then_inc(s, 1)
            self.ops["tensor"].append(run)
        for t in all_reads:
            t.readers.append(("tensor", sem_name, val, False))
        for t in tregs:
            t.w = ("tensor", sem_name, val, False, False)
            t.readers = []
        return val


def _wstream(W):
    no, ni = W.shape[0] // 128, W.shape[1] // 128
    return np.ascontiguousarray(
        W.T.reshape(ni, 128, no, 128).transpose(2, 1, 0, 3).reshape(no, 128, ni * 128))


def _prep_host(inputs):
    g = lambda k: np.asarray(inputs[k], dtype=np.float32)
    shared = {}
    for n in ("q", "k", "v", "o"):
        shared[f"a{n}_w"] = _wstream(g(f"L0_attn_a_W{n}"))
        shared[f"a{n}_bT"] = np.ascontiguousarray(
            g(f"L0_attn_a_B{n}").T.reshape(NDC, 128, RANK).transpose(1, 0, 2)
            .reshape(128, NDC * RANK))
        shared[f"a{n}_aT"] = np.ascontiguousarray(g(f"L0_attn_a_A{n}").T)
        shared[f"a{n}_bias"] = np.ascontiguousarray(g(f"L0_attn_a_b{n}").reshape(NDC, 128).T)
        shared[f"a{n}_brow"] = g(f"L0_attn_a_b{n}").reshape(1, D)
    shared["av_wm"] = np.ascontiguousarray(g("L0_attn_a_Wv").T)
    shared["bq_w"] = _wstream(g("L0_attn_b_Wq"))
    shared["bk_w"] = _wstream(g("L0_attn_b_Wk"))
    shared["bv_wm"] = np.ascontiguousarray(g("L0_attn_b_Wv").T)
    shared["bp_w"] = _wstream(g("L0_attn_b_Wp"))
    shared["fg_w"] = _wstream(g("L0_ffn_w_gate"))
    shared["fu_w"] = _wstream(g("L0_ffn_w_up"))
    shared["fd_w"] = np.ascontiguousarray(g("L0_ffn_w_down").T.reshape(NHC, 128, D))
    shared["identF"] = np.eye(128, dtype=np.float32)
    perm = np.zeros((128, 128), dtype=np.float32)
    for d in range(128):
        perm[(d + 64) % 128, d] = 1.0
    shared["permR"] = perm
    shared["cstR"] = np.ones((128, 2), dtype=np.float32)
    shared["cstF"] = np.full((128, 1), 1.0 / D, dtype=np.float32)
    _cv = np.array([1e-5, -16.0, float(2048 ** 0) * (128.0 ** -0.5), 0.5, 1.0, -1.0, 0.0, 2.0],
                   dtype=np.float32)
    shared["actcF"] = np.tile(_cv, (128, 1))
    shared["actcR"] = np.tile(_cv, (128, 1))
    shared["row1"] = np.ones((1, 512), dtype=np.float32)
    shared["row1F"] = np.ones((1, 128), dtype=np.float32)
    dm = np.zeros((4, 128, 512), dtype=np.float32)
    for j in range(4):
        dk = np.arange(128)[:, None]
        dq = np.arange(512)[None, :]
        dm[j] = np.where(dk <= dq - 128 * j, 0.0, NEG)
    shared["dmask"] = dm.reshape(512, 512)

    x = np.asarray(inputs["L0_attn_a_x"], dtype=np.float32)
    in_maps = []
    half = np.arange(64, dtype=np.float64)
    invf = 1.0 / (10000.0 ** (half / 64.0))
    for c in range(8):
        b, gg = c // 4, c % 4
        m = dict(shared)
        m["x_tm"] = np.ascontiguousarray(x[b, TC * gg:TC * (gg + 1), :])
        pos = np.arange(TC * gg, TC * (gg + 1), dtype=np.float64)
        ang = pos[None, :] * invf[:, None]
        cos = np.cos(ang)
        sin = np.sin(ang)
        m["rope_cos"] = np.concatenate([cos, cos], 0).astype(np.float32)
        m["rope_sinS"] = np.concatenate([-sin, sin], 0).astype(np.float32)
        btbl = np.zeros((128, 64), dtype=np.float32)
        for qt in range(2):
            q128 = 8 * gg + 4 * qt
            for kc in range(32):
                btbl[:, 32 * qt + kc] = -BEXP if kc < q128 else NEG
        m["bias_tbl"] = btbl
        in_maps.append(m)
    return in_maps


def build_nc():
    nc = bass.Bass("TRN2", target_bir_lowering=False)
    stack = ExitStack()
    mk = MK(nc, stack)
    T = Tile

    di_ = {}

    def din(name, shape, dt=F32R):
        di_[name] = nc.dram_tensor(name, shape, dt, kind="ExternalInput").ap()
        return di_[name]

    x_tm = din("x_tm", [TC, D], F32)
    out_fm = nc.dram_tensor("out_fm", [D, TC], F32, kind="ExternalOutput").ap()
    for n in ("q", "k", "v", "o"):
        din(f"a{n}_w", [NDC, 128, D])
        din(f"a{n}_bT", [128, NDC * RANK])
        din(f"a{n}_aT", [RANK, D])
        din(f"a{n}_bias", [128, NDC], F32)
        din(f"a{n}_brow", [1, D], F32R)
    din("av_wm", [D, D])
    din("bq_w", [NDC, 128, D])
    din("bk_w", [NKV, 128, D])
    din("bv_wm", [D, KVD])
    din("bp_w", [NDC, 128, D])
    din("fg_w", [NHC, 128, D])
    din("fu_w", [NHC, 128, D])
    din("fd_w", [NHC, 128, D])
    din("identF", [128, 128], F32)
    din("permR", [128, 128])
    din("cstR", [128, 2])
    din("cstF", [128, 1], F32)
    din("actcF", [128, 8], F32)
    din("actcR", [128, 8])
    din("row1", [1, 512])
    din("row1F", [1, 128], F32)
    din("dmask", [512, 512], F32)
    din("rope_cos", [128, TC], F32)
    din("rope_sinS", [128, TC], F32)
    din("bias_tbl", [128, 64], F32)

    dbg = {}
    if _DBG:
        for nm, shp in (("dbg_y", [D, TC]), ("dbg_q", [D, TC]), ("dbg_kv", [TC, TC]),
                        ("dbg_ao", [D, TC]), ("dbg_l", [128, 128])):
            dbg[nm] = nc.dram_tensor(nm, shp, F32R, kind="ExternalOutput").ap()
    qa_sp = nc.dram_tensor("qa_sp", [D, TC], F32R).ap()
    qb_sp = nc.dram_tensor("qb_sp", [D, TC], F32R).ap()
    h_sp = nc.dram_tensor("h_sp", [HID, TC], F32R).ap()
    kloc_sp = nc.dram_tensor("kloc_sp", [KVD, TC], F32R).ap()
    vloc_sp = nc.dram_tensor("vloc_sp", [TC, KVD], F32R).ap()
    cca_in = [nc.dram_tensor(f"cca_in{i}", [TC, TC], F32R).ap() for i in range(4)]
    cca_out = [nc.dram_tensor(f"cca_out{i}", [4 * TC, TC], F32R).ap() for i in range(4)]
    ccb_in = nc.dram_tensor("ccb_in", [TC, TC], F32R).ap()
    ccb_out = nc.dram_tensor("ccb_out", [4 * TC, TC], F32R).ap()

    t_qa = T("qa_sp")
    t_qb = T("qb_sp")
    t_h = [T(f"h_sp{i}") for i in range(NHC)]
    t_kloc = T("kloc")
    t_vloc = T("vloc")
    t_cca_in = [T(f"cca_in{i}") for i in range(4)]
    t_cca_out = [T(f"cca_out{i}") for i in range(4)]
    t_ccb_in = T("ccb_in")
    t_ccb_out = T("ccb_out")

    sb = lambda name, shape, dt=F32R: stack.enter_context(nc.sbuf_tensor(name, shape, dt))
    x_sb = sb("x_sb", [128, NDC * TC], F32)
    y_sb = sb("y_sb", [128, NDC * TC])
    ident_sb = sb("ident_sb", [128, 128], F32)
    perm_sb = sb("perm_sb", [128, 128])
    cst_sb = sb("cst_sb", [128, 2])
    avg_sb = sb("avg_sb", [128, 1], F32)
    actcF_sb = sb("actcF_sb", [128, 8], F32)
    actcR_sb = sb("actcR_sb", [128, 8])
    _CVALS = [EPS, -BEXP, SCALE, LORA_SC, 1.0, -1.0, 0.0, 2.0]
    for _i, _v in enumerate(_CVALS):
        nc.const_aps.register(F32, _v, actcF_sb[:, _i:_i + 1])
        nc.const_aps.register(F32R, _v, actcR_sb[:, _i:_i + 1])
    row1_sb = sb("row1_sb", [1, 512])
    row1F_sb = sb("row1F_sb", [1, 128], F32)
    btbl_sb = sb("btbl_sb", [128, 64], F32)
    bias_sb = {n: sb(f"bias_{n}", [128, NDC], F32) for n in ("q", "k", "v")}
    rowbuf_sb = sb("rowbuf_sb", [1, D])
    rows_sb = sb("rows_sb", [1, 2 * TC], F32)
    bc_sb = sb("bc_sb", [128, 512], F32)

    t_x = [T(f"x{i}") for i in range(NDC)]
    t_y = [T(f"y{i}") for i in range(NDC)]
    t_consts = T("consts")
    t_stat = T("stat")
    t_nmr = T("nmr")
    t_lt8 = T("lt8")
    t_lrow = T("lrow")
    t_lt8 = T("lt8")
    t_bc = T("bc")
    t_rowbuf = T("rowbuf")

    ps = stack.enter_context(nc.psum_tensor("ps", [128, 4096], F32))
    t_psreg = [T(f"psr{r}") for r in range(8)]

    def pst(off, width):
        return t_psreg[off // 512:(off + width + 511) // 512]

    mk.dma("sync", ident_sb[:, :], di_["identF"][:, :], writes=[t_consts])
    mk.dma("sync", perm_sb[:, :], di_["permR"][:, :], writes=[t_consts])
    mk.dma("sync", cst_sb[:, :], di_["cstR"][:, :], writes=[t_consts])
    mk.dma("sync", avg_sb[:, :], di_["cstF"][:, :], writes=[t_consts])
    mk.dma("sync", actcF_sb[:, :], di_["actcF"][:, :], writes=[t_consts])
    mk.dma("sync", actcR_sb[:, :], di_["actcR"][:, :], writes=[t_consts])
    mk.dma("sync", row1_sb[:, :], di_["row1"][:, :], writes=[t_consts])
    mk.dma("sync", row1F_sb[:, :], di_["row1F"][:, :], writes=[t_consts])
    mk.dma("sync", btbl_sb[:, :], di_["bias_tbl"][:, :], writes=[t_consts])
    for n in ("q", "k", "v"):
        mk.dma("sync", bias_sb[n][:, :], di_[f"a{n}_bias"][:, :], writes=[t_consts])


    with nc.sbuf_tensor("stage_in", [128, 2 * D], F32) as stage_in:
        t_st = [T("st0"), T("st1")]
        for t in range(NTC):
            sl = t % 2
            st_ap = stage_in[:, sl * D:(sl + 1) * D]
            mk.dma("sync", st_ap, x_tm[128 * t:128 * (t + 1), :], writes=[t_st[sl]])
            for dset in range(4):
                regs = pst(3072, 512)
                mms = []
                for k in range(4):
                    di = dset * 4 + k
                    o = ps[:, 3072 + 128 * k:3072 + 128 * (k + 1)]
                    mms.append((o, st_ap[:, 128 * di:128 * (di + 1)],
                                ident_sb[:, :], True, True, [t_st[sl], t_consts]))
                mk.dep_waits("tensor", [t_st[sl], t_consts], regs)
                mk.cnt["sTE"] += 1
                val = mk.cnt["sTE"]
                s = mk.sems["sTE"]
                for i, (o, inp, idn, _, _, _) in enumerate(mms):
                    if i < 3:
                        def run(o=o, inp=inp, idn=idn):
                            nc.tensor.transpose(o, inp, idn)
                    else:
                        def run(o=o, inp=inp, idn=idn, s=s):
                            nc.tensor.transpose(o, inp, idn).then_inc(s, 1)
                    mk.ops["tensor"].append(run)
                for rg in regs:
                    rg.w = ("tensor", "sTE", val, False, False)
                    rg.readers = []
                t_st[sl].readers.append(("tensor", "sTE", val, False))
                for k in range(4):
                    di = dset * 4 + k
                    mk.op("scalar",
                          lambda o=x_sb[:, TC * di + 128 * t: TC * di + 128 * (t + 1)],
                                 i=ps[:, 3072 + 128 * k:3072 + 128 * (k + 1)]:
                          nc.scalar.copy(out=o, in_=i),
                          reads=regs, writes=[t_x[di]])

    _sq_state = {"n": 0}

    def emit_ln(_unused=None, _unused2=None):
        _sq_state["n"] += 1
        mk.phase_barrier()
        _ctx = nc.sbuf_tensor("sqb%d" % _sq_state["n"], [128, 2 * TC], F32)
        sq_sb = _ctx.__enter__()
        t_sq = [T("sq0"), T("sq1")]
        t_mean = pst(0, 1024)
        t_ex2 = pst(1024, 1024)
        for reg in ((0, 512), (512, 512), (1024, 512), (1536, 512)):
            mk.g_start(reg)
        mean_mms, sq_mms = [], []
        for di in range(NDC):
            sl = di % 2
            mk.op("scalar",
                  lambda o=sq_sb[:, TC * sl:TC * (sl + 1)],
                         i=x_sb[:, TC * di:TC * (di + 1)]:
                  nc.scalar.activation(out=o, in_=i, func=AF.Square),
                  reads=[t_x[di]], writes=[t_sq[sl]])
            for qt in range(2):
                mean_mms = [(ps[0:1, 512 * qt:512 * (qt + 1)],
                             avg_sb[:, 0:1],
                             x_sb[:, TC * di + 512 * qt:TC * di + 512 * (qt + 1)],
                             di == 0, di == NDC - 1, [t_x[di], t_consts])]
                mk.mm_group(t_mean, mean_mms)
                sq_mms = [(ps[0:1, 1024 + 512 * qt:1024 + 512 * (qt + 1)],
                           avg_sb[:, 0:1],
                           sq_sb[:, TC * sl + 512 * qt:TC * sl + 512 * (qt + 1)],
                           di == 0, di == NDC - 1, [t_sq[sl], t_consts])]
                mk.mm_group(t_ex2, sq_mms)
        for reg in ((0, 512), (512, 512), (1024, 512), (1536, 512)):
            mk.g_stop(reg)
        mk.op("scalar", lambda: nc.scalar.activation(
            out=rows_sb[:, 0:TC], in_=ps[0:1, 0:TC], func=AF.Square),
            reads=t_mean, writes=[t_stat])
        mk.op("vector", lambda: nc.vector.tensor_tensor(
            out=rows_sb[:, TC:2 * TC], in0=ps[0:1, 1024:2048],
            in1=rows_sb[:, 0:TC], op=ALU.subtract),
            reads=t_ex2 + [t_stat], writes=[t_stat])
        mk.op("scalar", lambda: nc.scalar.activation(
            out=rows_sb[:, 0:TC], in_=rows_sb[:, TC:2 * TC],
            func=AF.Sqrt, bias=EPS),
            reads=[t_stat], writes=[t_stat])
        mk.op("vector", lambda: nc.vector.reciprocal(
            out=rows_sb[:, 0:TC], in_=rows_sb[:, 0:TC]),
            reads=[t_stat], writes=[t_stat])
        mk.op("vector", lambda: nc.vector.scalar_tensor_tensor(
            out=rows_sb[:, TC:2 * TC], in0=ps[0:1, 0:TC], scalar=-1.0,
            in1=rows_sb[:, 0:TC], op0=ALU.mult, op1=ALU.mult),
            reads=t_mean + [t_stat], writes=[t_nmr])
        t_rbc = pst(2048, 1024)
        t_nbc = pst(3072, 1024)
        rb, nb = [], []
        for qt in range(2):
            rb.append((ps[:, 2048 + 512 * qt:2048 + 512 * (qt + 1)], row1F_sb[0:1, 0:128],
                       rows_sb[:, 512 * qt:512 * (qt + 1)],
                       True, True, [t_stat, t_consts]))
            nb.append((ps[:, 3072 + 512 * qt:3072 + 512 * (qt + 1)], row1F_sb[0:1, 0:128],
                       rows_sb[:, TC + 512 * qt:TC + 512 * (qt + 1)],
                       True, True, [t_nmr, t_consts]))
        mk.mm_group(t_rbc, rb)
        mk.mm_group(t_nbc, nb)
        for di in range(NDC):
            mk.op("vector", lambda o=y_sb[:, TC * di:TC * (di + 1)],
                             i=x_sb[:, TC * di:TC * (di + 1)]:
                  nc.vector.tensor_tensor(out=o, in0=i, in1=ps[:, 2048:3072], op=ALU.mult),
                  reads=[t_x[di]] + t_rbc, writes=[t_y[di]])
            mk.op("vector", lambda o=y_sb[:, TC * di:TC * (di + 1)]:
                  nc.vector.tensor_tensor(out=o, in0=o, in1=ps[:, 3072:4096], op=ALU.add),
                  reads=t_nbc + [t_y[di]], writes=[t_y[di]])
        _ctx.__exit__(None, None, None)

    def emit_attn_a():
        mk.phase_barrier()
        with (
            nc.sbuf_tensor("wbuf", [128, 2 * D], F32R) as wbuf,
            nc.sbuf_tensor("wmbuf", [128, 2 * 512], F32R) as wmbuf,
            nc.sbuf_tensor("aTb", [RANK, D], F32R) as aTb,
            nc.sbuf_tensor("bTb", [128, NDC * RANK], F32R) as bTb,
            nc.sbuf_tensor("ubuf", [RANK, 3 * TC], F32R) as ubuf,
            nc.sbuf_tensor("kvout", [128, 2 * TC], F32R) as kvout,
        ):
            t_w = [T("wb0"), T("wb1")]
            t_wm = [T("wm0"), T("wm1")]
            t_aT = T("aT")
            t_bT = T("bT")
            t_u = {}
            t_kv = [T("kv0"), T("kv1")]
            mk.dma("sync", rowbuf_sb[:, :], di_["av_brow"][:, :], writes=[t_rowbuf])

            for ui, n in enumerate(("q", "k", "v")):
                t_u[n] = T(f"u_{n}")
                mk.dma("sync", bTb[:, :], di_[f"a{n}_bT"][:, :], writes=[t_bT])
                mms = []
                for di in range(NDC):
                    for qt in range(2):
                        mms.append((ps[0:RANK, 512 * qt:512 * (qt + 1)],
                                    bTb[:, RANK * di:RANK * (di + 1)],
                                    y_sb[:, TC * di + 512 * qt:TC * di + 512 * (qt + 1)],
                                    di == 0, di == NDC - 1, [t_bT, t_y[di]]))
                tp = pst(0, 1024)
                mk.g_start((0, 512)); mk.g_start((512, 512))
                mk.mm_group(tp, mms)
                mk.g_stop((0, 512)); mk.g_stop((512, 512))
                mk.op("scalar", lambda o=ubuf[:, TC * ui:TC * (ui + 1)],
                                 i=ps[0:RANK, 0:TC]:
                      nc.scalar.activation(out=o, in_=i, func=AF.Copy, scale=LORA_SC),
                      reads=tp, writes=[t_u[n]])

            kpos = [0, 0]
            for hg in range(4):
                mk.dma("sync", aTb[:, :], di_["ak_aT"][:, :], writes=[t_aT])
                for j in range(4):
                    d = 4 * hg + j
                    sl = d % 2
                    base = 1024 * sl
                    mk.dma("sync", wbuf[:, D * sl:D * (sl + 1)],
                           di_["ak_w"][d, :, :], writes=[t_w[sl]])
                    tp = pst(base, 1024)
                    mk.g_start((base, 512)); mk.g_start((base + 512, 512))
                    mms = []
                    for di in range(NDC):
                        for qt in range(2):
                            mms.append((ps[:, base + 512 * qt:base + 512 * (qt + 1)],
                                        wbuf[:, D * sl + 128 * di:D * sl + 128 * (di + 1)],
                                        y_sb[:, TC * di + 512 * qt:TC * di + 512 * (qt + 1)],
                                        di == 0, False, [t_w[sl], t_y[di]]))
                    for qt in range(2):
                        mms.append((ps[:, base + 512 * qt:base + 512 * (qt + 1)],
                                    aTb[0:RANK, 128 * d:128 * (d + 1)],
                                    ubuf[0:RANK, TC * 1 + 512 * qt:TC * 1 + 512 * (qt + 1)],
                                    False, True, [t_aT, t_u["k"]]))
                    mk.mm_group(tp, mms)
                    mk.g_stop((base, 512)); mk.g_stop((base + 512, 512))
                    mk.op("scalar", lambda o=kvout[:, TC * sl:TC * (sl + 1)],
                                     i=ps[:, base:base + TC], b=bias_sb["k"][:, d:d + 1]:
                          nc.scalar.activation(out=o, in_=i, func=AF.Identity, bias=b),
                          reads=tp, writes=[t_kv[sl]])
                    mk.dma("gpsimd", cca_in[hg][128 * j:128 * (j + 1), :],
                           kvout[:, TC * sl:TC * (sl + 1)],
                           reads=[t_kv[sl]], writes=[t_cca_in[hg]])
                mk.dma("sync", aTb[:, :], di_["av_aT"][:, :], writes=[t_aT])
                tvs = [pst(512 * t, 512) for t in range(8)]
                for t in range(8):
                    mk.g_start((512 * t, 512))
                for di in range(NDC):
                    sl = di % 2
                    mk.dma("sync", wmbuf[:, 512 * sl:512 * (sl + 1)],
                           di_["av_wm"][128 * di:128 * (di + 1), 512 * hg:512 * (hg + 1)],
                           writes=[t_wm[sl]])
                    mms = []
                    for t in range(8):
                        mms.append((ps[:, 512 * t:512 * (t + 1)],
                                    y_sb[:, TC * di + 128 * t:TC * di + 128 * (t + 1)],
                                    wmbuf[:, 512 * sl:512 * (sl + 1)],
                                    di == 0, False, [t_wm[sl], t_y[di]]))
                    mk.mm_group(pst(0, 4096), mms)
                for t in range(8):
                    mms = [(ps[:, 512 * t:512 * (t + 1)],
                            ubuf[0:RANK, TC * 2 + 128 * t:TC * 2 + 128 * (t + 1)],
                            aTb[0:RANK, 512 * hg:512 * (hg + 1)],
                            False, False, [t_aT, t_u["v"]]),
                           (ps[:, 512 * t:512 * (t + 1)],
                            row1_sb[0:1, 0:128],
                            rowbuf_sb[0:1, 512 * hg:512 * (hg + 1)],
                            False, True, [t_consts, t_rowbuf])]
                    mk.mm_group(tvs[t] if t > 0 else tvs[0], mms)
                for t in range(8):
                    mk.g_stop((512 * t, 512))
                    sl = t % 2
                    mk.op("scalar", lambda o=kvout[:, TC * sl + 512 * 0:TC * sl + 512],
                                     i=ps[:, 512 * t:512 * (t + 1)]:
                          nc.scalar.copy(out=o, in_=i),
                          reads=tvs[t], writes=[t_kv[sl]])
                    dst = cca_in[hg].rearrange("a b -> (a b)")[
                        512 * TC + t * 65536: 512 * TC + (t + 1) * 65536]
                    mk.dma("gpsimd", dst.rearrange("(p c) -> p c", p=128),
                           kvout[:, TC * sl:TC * sl + 512],
                           reads=[t_kv[sl]], writes=[t_cca_in[hg]])
                mk.op("gpsimd", lambda hg=hg: nc.gpsimd.collective_compute(
                    "AllGather", ALU.bypass,
                    replica_groups=[[0, 1, 2, 3], [4, 5, 6, 7]],
                    ins=[cca_in[hg].rearrange("a b -> (a b)").opt()],
                    outs=[cca_out[hg].rearrange("a b -> (a b)").opt()]),
                    reads=[t_cca_in[hg]], writes=[t_cca_out[hg]],
                    sem="sCC", is_async=True)

            mk.dma("sync", aTb[:, :], di_["aq_aT"][:, :], writes=[t_aT])
            for d in range(NDC):
                sl = d % 2
                base = 1024 * sl
                mk.dma("sync", wbuf[:, D * sl:D * (sl + 1)], di_["aq_w"][d, :, :],
                       writes=[t_w[sl]])
                tp = pst(base, 1024)
                mk.g_start((base, 512)); mk.g_start((base + 512, 512))
                mms = []
                for di in range(NDC):
                    for qt in range(2):
                        mms.append((ps[:, base + 512 * qt:base + 512 * (qt + 1)],
                                    wbuf[:, D * sl + 128 * di:D * sl + 128 * (di + 1)],
                                    y_sb[:, TC * di + 512 * qt:TC * di + 512 * (qt + 1)],
                                    di == 0, False, [t_w[sl], t_y[di]]))
                for qt in range(2):
                    mms.append((ps[:, base + 512 * qt:base + 512 * (qt + 1)],
                                aTb[0:RANK, 128 * d:128 * (d + 1)],
                                ubuf[0:RANK, 512 * qt:512 * (qt + 1)],
                                False, True, [t_aT, t_u["q"]]))
                mk.mm_group(tp, mms)
                mk.g_stop((base, 512)); mk.g_stop((base + 512, 512))
                mk.op("scalar", lambda o=kvout[:, TC * sl:TC * (sl + 1)],
                                 i=ps[:, base:base + TC], b=bias_sb["q"][:, d:d + 1]:
                      nc.scalar.activation(out=o, in_=i, func=AF.Identity, bias=b),
                      reads=tp, writes=[t_kv[sl]])
                mk.dma("gpsimd", qa_sp[128 * d:128 * (d + 1), :],
                       kvout[:, TC * sl:TC * (sl + 1)],
                       reads=[t_kv[sl]], writes=[t_qa])

        if _DBG:
            mk.dma("gpsimd", dbg["dbg_q"][:, :], qa_sp[:, :], reads=[t_qa], writes=[])
            mk.dma("gpsimd", dbg["dbg_kv"][:, :], cca_in[0][:, :],
                   reads=[t_cca_in[0]], writes=[])

        t_ao = [T(f"ao{i}") for i in range(NDC)]
        mk.phase_barrier()
        with (
            nc.sbuf_tensor("kbuf", [128, 2 * 2048], F32R) as kbuf,
            nc.sbuf_tensor("vbuf", [128, 2 * 2048], F32R) as vbuf,
            nc.sbuf_tensor("qbuf", [128, 2 * TC], F32R) as qbuf,
            nc.sbuf_tensor("pbuf", [128, 2 * TC], F32R) as pbuf,
        ):
            t_k = [T("k0"), T("k1")]
            t_v = [T("v0"), T("v1")]
            t_q = [T("q0"), T("q1")]
            t_p = [T("p0"), T("p1")]
            t_out = pst(2048, 1024)
            t_l = pst(3072, 1024)

            def load_kv_half(h, hf, sl):
                hg, hh = h // 4, h % 4
                for r2 in range(2):
                    r = 2 * hf + r2
                    mk.dma("sync", kbuf[:, 2048 * sl + 1024 * r2:2048 * sl + 1024 * (r2 + 1)],
                           cca_out[hg][1024 * r + 128 * hh:1024 * r + 128 * (hh + 1), :],
                           reads=[t_cca_out[hg]], writes=[t_k[sl]])
                    src = cca_out[hg].rearrange("a b -> (a b)")
                    base = (1024 * r + 512) * TC + 128 * hh
                    sap = bass.AP(src.tensor, src.offset + base,
                                  [[512, 128], [65536, 8], [1, 128]])
                    dap = bass.AP(vbuf.ap().tensor, vbuf.ap().offset + 2048 * sl + 1024 * r2,
                                  [[4096, 128], [128, 8], [1, 128]])
                    mk.dma("sync", dap, sap, reads=[t_cca_out[hg]], writes=[t_v[sl]])

            for h in range(HEADS):
                qsl = h % 2
                mk.dma("sync", qbuf[:, TC * qsl:TC * (qsl + 1)],
                       qa_sp[128 * h:128 * (h + 1), :], reads=[t_qa], writes=[t_q[qsl]])
                if h == 0:
                    load_kv_half(0, 0, 0)
                mk.g_start((2048, 512)); mk.g_start((2560, 512))

                def lav_a(kc, psl, sl, kk):
                    lm = []
                    for qt in range(2):
                        lm.append((ps[0:2, 3072 + 512 * qt:3584 + 512 * qt],
                                   cst_sb[:, 0:2],
                                   pbuf[:, TC * psl + 512 * qt:TC * psl + 512 * (qt + 1)],
                                   kc == 0, kc == 31, [t_p[psl], t_consts]))
                    mk.mm_group(t_l, lm)
                    av = []
                    for qt in range(2):
                        av.append((ps[:, 2048 + 512 * qt:2048 + 512 * (qt + 1)],
                                   vbuf[:, 2048 * sl + 128 * kk:2048 * sl + 128 * (kk + 1)],
                                   pbuf[:, TC * psl + 512 * qt:TC * psl + 512 * (qt + 1)],
                                   kc == 0, kc == 31, [t_v[sl], t_p[psl]]))
                    mk.mm_group(t_out, av)

                pend = None
                for kc in range(32):
                    hf = kc // 16
                    sl = hf
                    kk = kc % 16
                    psl = kc % 2
                    base = 1024 * psl
                    tp = pst(base, 1024)
                    mk.g_start((base, 512)); mk.g_start((base + 512, 512))
                    mms = []
                    for qt in range(2):
                        mms.append((ps[:, base + 512 * qt:base + 512 * (qt + 1)],
                                    kbuf[:, 2048 * sl + 128 * kk:2048 * sl + 128 * (kk + 1)],
                                    qbuf[:, TC * qsl + 512 * qt:TC * qsl + 512 * (qt + 1)],
                                    True, True, [t_k[sl], t_q[qsl]]))
                    mk.mm_group(tp, mms)
                    mk.g_stop((base, 512)); mk.g_stop((base + 512, 512))
                    mk.op("scalar", lambda o=pbuf[:, TC * psl:TC * (psl + 1)],
                                     i=ps[:, base:base + TC]:
                          nc.scalar.activation(out=o, in_=i, func=AF.Exp,
                                               scale=SCALE, bias=-BEXP),
                          reads=tp, writes=[t_p[psl]])
                    if pend is not None:
                        lav_a(*pend)
                    pend = (kc, psl, sl, kk)
                    if kc % 16 == 0:
                        if kc == 0:
                            load_kv_half(h, 1, 1)
                        else:
                            if h < HEADS - 1:
                                load_kv_half(h + 1, 0, 0)
                lav_a(*pend)
                mk.g_stop((2048, 512)); mk.g_stop((2560, 512))
                emit_norm_head(h, t_out, t_l, t_ao[h])

        if _DBG:
            mk.op("scalar", lambda: nc.scalar.copy(out=bc_sb[:, 0:128], in_=ps[:, 3072:3200]),
                  reads=pst(3072, 512), writes=[t_bc])
            mk.dma("gpsimd", dbg["dbg_l"][:, :], bc_sb[:, 0:128], reads=[t_bc], writes=[])
            for di in range(NDC):
                mk.dma("gpsimd", dbg["dbg_ao"][128 * di:128 * (di + 1), :],
                       y_sb[:, TC * di:TC * (di + 1)], reads=[t_ao[di]], writes=[])

        emit_oproj_a(t_ao)

    def emit_norm_head(h, t_out, t_l, t_aoh):
        mk.op("scalar", lambda: nc.scalar.activation(
            out=rows_sb[:, TC:2 * TC], in_=ps[0:1, 3072:4096], func=AF.Ln),
            reads=t_l, writes=[t_stat])
        mk.op("scalar", lambda: nc.scalar.activation(
            out=rows_sb[:, 0:TC], in_=rows_sb[:, TC:2 * TC], func=AF.Exp, scale=-1.0),
            reads=[t_stat], writes=[t_lrow])
        for qt in range(2):
            t_b = pst(512 * qt, 512)
            mk.mm_group(t_b, [(ps[:, 512 * qt:512 * (qt + 1)], row1F_sb[0:1, 0:128],
                               rows_sb[0:1, 512 * qt:512 * (qt + 1)],
                               True, True, [t_lrow, t_consts])])
            mk.op("scalar", lambda qt=qt: nc.scalar.copy(
                out=bc_sb[:, :], in_=ps[:, 512 * qt:512 * (qt + 1)]),
                reads=t_b, writes=[t_bc])
            mk.op("vector",
                  lambda o=y_sb[:, TC * h + 512 * qt:TC * h + 512 * (qt + 1)],
                         i=ps[:, 2048 + 512 * qt:2048 + 512 * (qt + 1)]:
                  nc.vector.tensor_tensor(out=o, in0=i, in1=bc_sb[:, :], op=ALU.mult),
                  reads=pst(2048, 1024) + [t_bc], writes=[t_aoh])

    def emit_oproj_a(t_ao):
        mk.phase_barrier()
        with (
            nc.sbuf_tensor("wbuf2", [128, 2 * D], F32R) as wbuf,
            nc.sbuf_tensor("aTb2", [RANK, D], F32R) as aTb,
            nc.sbuf_tensor("bTb2", [128, NDC * RANK], F32R) as bTb,
            nc.sbuf_tensor("ubuf2", [RANK, TC], F32R) as ubuf,
        ):
            t_w = [T("w20"), T("w21")]
            t_aT = T("aT2")
            t_bT = T("bT2")
            t_u = T("u_o")
            mk.dma("sync", rowbuf_sb[:, :], di_["ao_brow"][:, :], writes=[t_rowbuf])
            mk.dma("sync", bTb[:, :], di_["ao_bT"][:, :], writes=[t_bT])
            mk.dma("sync", aTb[:, :], di_["ao_aT"][:, :], writes=[t_aT])
            mms = []
            for di in range(NDC):
                for qt in range(2):
                    mms.append((ps[0:RANK, 512 * qt:512 * (qt + 1)],
                                bTb[:, RANK * di:RANK * (di + 1)],
                                y_sb[:, TC * di + 512 * qt:TC * di + 512 * (qt + 1)],
                                di == 0, di == NDC - 1, [t_bT, t_ao[di]]))
            tp = pst(0, 1024)
            mk.g_start((0, 512)); mk.g_start((512, 512))
            mk.mm_group(tp, mms)
            mk.g_stop((0, 512)); mk.g_stop((512, 512))
            mk.op("scalar", lambda: nc.scalar.activation(
                out=ubuf[:, :], in_=ps[0:RANK, 0:TC], func=AF.Copy, scale=LORA_SC),
                reads=tp, writes=[t_u])
            for d in range(NDC):
                sl = d % 2
                base = 1024 * sl
                mk.dma("sync", wbuf[:, D * sl:D * (sl + 1)], di_["ao_w"][d, :, :],
                       writes=[t_w[sl]])
                tp = pst(base, 1024)
                mk.g_start((base, 512)); mk.g_start((base + 512, 512))
                mms = []
                for di in range(NDC):
                    for qt in range(2):
                        mms.append((ps[:, base + 512 * qt:base + 512 * (qt + 1)],
                                    wbuf[:, D * sl + 128 * di:D * sl + 128 * (di + 1)],
                                    y_sb[:, TC * di + 512 * qt:TC * di + 512 * (qt + 1)],
                                    di == 0, False, [t_w[sl], t_ao[di]]))
                for qt in range(2):
                    mms.append((ps[:, base + 512 * qt:base + 512 * (qt + 1)],
                                aTb[0:RANK, 128 * d:128 * (d + 1)],
                                ubuf[0:RANK, 512 * qt:512 * (qt + 1)],
                                False, False, [t_aT, t_u]))
                    mms.append((ps[:, base + 512 * qt:base + 512 * (qt + 1)],
                                rowbuf_sb[0:1, 128 * d:128 * (d + 1)],
                                row1_sb[0:1, 0:512],
                                False, True, [t_rowbuf, t_consts]))
                mk.mm_group(tp, mms)
                mk.g_stop((base, 512)); mk.g_stop((base + 512, 512))
                mk.op("vector", lambda o=x_sb[:, TC * d:TC * (d + 1)],
                                 i=ps[:, base:base + TC]:
                      nc.vector.tensor_tensor(out=o, in0=o, in1=i, op=ALU.add),
                      reads=tp + [t_x[d]], writes=[t_x[d]])

    _ffn_n = [0]

    def emit_ffn(final=False):
        _ffn_n[0] += 1
        sfx = str(_ffn_n[0])
        mk.phase_barrier()
        with (
            nc.sbuf_tensor("wgb" + sfx, [128, 2 * D], F32R) as wgb,
            nc.sbuf_tensor("wub" + sfx, [128, 2 * D], F32R) as wub,
            nc.sbuf_tensor("ssb" + sfx, [128, 2 * TC], F32R) as ssb,
            nc.sbuf_tensor("hb" + sfx, [128, 2 * TC], F32R) as hb,
        ):
            t_wg = [T("wg0"), T("wg1")]
            t_wu = [T("wu0"), T("wu1")]
            t_ss = [T("ss0"), T("ss1")]
            t_hb = [T("hb0"), T("hb1")]
            for hc in range(NHC):
                sl = hc % 2
                gb = 2048 * sl
                ub = 2048 * sl + 1024
                mk.dma("sync", wgb[:, D * sl:D * (sl + 1)], di_["fg_w"][hc, :, :],
                       writes=[t_wg[sl]])
                mk.dma("sync", wub[:, D * sl:D * (sl + 1)], di_["fu_w"][hc, :, :],
                       writes=[t_wu[sl]])
                tg = pst(gb, 1024)
                tu = pst(ub, 1024)
                for b0, tt, wb, t_wx in ((gb, tg, wgb, t_wg), (ub, tu, wub, t_wu)):
                    mk.g_start((b0, 512)); mk.g_start((b0 + 512, 512))
                    mms = []
                    for di in range(NDC):
                        for qt in range(2):
                            mms.append((ps[:, b0 + 512 * qt:b0 + 512 * (qt + 1)],
                                        wb[:, D * sl + 128 * di:D * sl + 128 * (di + 1)],
                                        y_sb[:, TC * di + 512 * qt:TC * di + 512 * (qt + 1)],
                                        di == 0, di == NDC - 1, [t_wx[sl], t_y[di]]))
                    mk.mm_group(tt, mms)
                    mk.g_stop((b0, 512)); mk.g_stop((b0 + 512, 512))
                mk.op("scalar", lambda o=ssb[:, TC * sl:TC * (sl + 1)],
                                 i=ps[:, gb:gb + TC]:
                      nc.scalar.activation(out=o, in_=i, func=AF.Silu),
                      reads=tg, writes=[t_ss[sl]])
                mk.op("vector", lambda o=hb[:, TC * sl:TC * (sl + 1)],
                                 i0=ssb[:, TC * sl:TC * (sl + 1)], i1=ps[:, ub:ub + TC]:
                      nc.vector.tensor_tensor(out=o, in0=i0, in1=i1, op=ALU.mult),
                      reads=[t_ss[sl]] + tu, writes=[t_hb[sl]])
                mk.dma("gpsimd", h_sp[128 * hc:128 * (hc + 1), :],
                       hb[:, TC * sl:TC * (sl + 1)],
                       reads=[t_hb[sl]], writes=[t_h[hc]])
        mk.phase_barrier()
        with (
            nc.sbuf_tensor("hrb" + sfx, [128, 4 * TC], F32R) as hrb,
            nc.sbuf_tensor("wdb" + sfx, [128, 4 * D], F32R) as wdb,
        ):
            t_hr = [T(f"hr{i}") for i in range(4)]
            t_wd = [T(f"wd{i}") for i in range(4)]
            NGRP = NHC // 2
            for gi in range(NGRP):
                for j in range(2):
                    hc = 2 * gi + j
                    slot = hc % 4
                    eng = "sync" if hc % 2 == 0 else "scalar"
                    mk.dma(eng, hrb[:, TC * slot:TC * (slot + 1)],
                           h_sp[128 * hc:128 * (hc + 1), :],
                           reads=[t_h[hc]], writes=[t_hr[slot]])
                    mk.dma(eng, wdb[:, D * slot:D * (slot + 1)],
                           di_["fd_w"][hc, :, :], writes=[t_wd[slot]])
                sl0, sl1 = (2 * gi) % 4, (2 * gi + 1) % 4
                for dc in range(NDC):
                    bank = dc % 4
                    tp = pst(1024 * bank, 1024)
                    mk.g_start((1024 * bank, 512)); mk.g_start((1024 * bank + 512, 512))
                    mms = []
                    for j, slj in ((0, sl0), (1, sl1)):
                        for qt in range(2):
                            mms.append((ps[:, 1024 * bank + 512 * qt:1024 * bank + 512 * (qt + 1)],
                                        wdb[:, D * slj + 128 * dc:D * slj + 128 * (dc + 1)],
                                        hrb[:, TC * slj + 512 * qt:TC * slj + 512 * (qt + 1)],
                                        j == 0, j == 1, [t_wd[slj], t_hr[slj]]))
                    mk.mm_group(tp, mms)
                    mk.g_stop((1024 * bank, 512)); mk.g_stop((1024 * bank + 512, 512))
                    if dc % 2 == 1:
                        d0 = dc - 1
                        b0 = 1024 * (bank - 1)
                        tpp = pst(b0, 2048)
                        mk.op("vector",
                              lambda o=x_sb[:, TC * d0:TC * (d0 + 2)],
                                     i=ps[:, b0:b0 + 2048]:
                              nc.vector.tensor_tensor(out=o, in0=o, in1=i, op=ALU.add),
                              reads=tpp + [t_x[d0], t_x[d0 + 1]],
                              writes=[t_x[d0], t_x[d0 + 1]])
                        if final and gi == NGRP - 1:
                            for dd in (d0, d0 + 1):
                                mk.dma("gpsimd", out_fm[128 * dd:128 * (dd + 1), :],
                                       x_sb[:, TC * dd:TC * (dd + 1)],
                                       reads=[t_x[dd]], writes=[])

    def emit_attn_b():
        mk.phase_barrier()
        with (
            nc.sbuf_tensor("wbuf3", [128, 2 * D], F32R) as wbuf,
            nc.sbuf_tensor("wmb3", [128, 2 * 512], F32R) as wmb,
            nc.sbuf_tensor("cosb", [128, TC], F32) as cosb,
            nc.sbuf_tensor("sinb", [128, TC], F32) as sinb,
            nc.sbuf_tensor("ktmp", [128, 2 * TC], F32R) as ktmp,
            nc.sbuf_tensor("krot", [128, 2 * TC], F32R) as krot,
        ):
            t_w = [T("w30"), T("w31")]
            t_wm = [T("wm30"), T("wm31")]
            t_cos = T("cos")
            t_ktmp = [T("kt0"), T("kt1")]
            t_krot = [T("kr0"), T("kr1")]
            mk.dma("sync", cosb[:, :], di_["rope_cos"][:, :], writes=[t_cos])
            mk.dma("sync", sinb[:, :], di_["rope_sinS"][:, :], writes=[t_cos])

            def rope_chunk(sl, tp_in, base):
                tsh = pst(2048, 1024)
                mk.g_start((2048, 512)); mk.g_start((2560, 512))
                mms = []
                for qt in range(2):
                    mms.append((ps[:, 2048 + 512 * qt:2048 + 512 * (qt + 1)],
                                perm_sb[:, :],
                                ktmp[:, TC * sl + 512 * qt:TC * sl + 512 * (qt + 1)],
                                True, True, [t_consts, t_ktmp[sl]]))
                mk.mm_group(tsh, mms)
                mk.g_stop((2048, 512)); mk.g_stop((2560, 512))
                mk.op("vector", lambda o=krot[:, TC * sl:TC * (sl + 1)],
                                 i=ktmp[:, TC * sl:TC * (sl + 1)]:
                      nc.vector.tensor_tensor(out=o, in0=i, in1=cosb[:, :], op=ALU.mult),
                      reads=[t_ktmp[sl], t_cos], writes=[t_krot[sl]])
                mk.op("vector", lambda o=ktmp[:, TC * sl:TC * (sl + 1)],
                                 i=ps[:, 2048:2048 + TC]:
                      nc.vector.scalar_tensor_tensor(out=o, in0=i, scalar=1.0,
                                                     in1=sinb[:, :],
                                                     op0=ALU.mult, op1=ALU.mult),
                      reads=tsh + [t_cos], writes=[t_ktmp[sl]])
                mk.op("vector", lambda o=krot[:, TC * sl:TC * (sl + 1)],
                                 i=ktmp[:, TC * sl:TC * (sl + 1)]:
                      nc.vector.tensor_tensor(out=o, in0=o, in1=i, op=ALU.add),
                      reads=[t_ktmp[sl], t_krot[sl]], writes=[t_krot[sl]])

            for dkv in range(NKV):
                sl = dkv % 2
                base = 1024 * sl
                mk.dma("sync", wbuf[:, D * sl:D * (sl + 1)], di_["bk_w"][dkv, :, :],
                       writes=[t_w[sl]])
                tp = pst(base, 1024)
                mk.g_start((base, 512)); mk.g_start((base + 512, 512))
                mms = []
                for di in range(NDC):
                    for qt in range(2):
                        mms.append((ps[:, base + 512 * qt:base + 512 * (qt + 1)],
                                    wbuf[:, D * sl + 128 * di:D * sl + 128 * (di + 1)],
                                    y_sb[:, TC * di + 512 * qt:TC * di + 512 * (qt + 1)],
                                    di == 0, di == NDC - 1, [t_w[sl], t_y[di]]))
                mk.mm_group(tp, mms)
                mk.g_stop((base, 512)); mk.g_stop((base + 512, 512))
                mk.op("scalar", lambda o=ktmp[:, TC * sl:TC * (sl + 1)],
                                 i=ps[:, base:base + TC]:
                      nc.scalar.copy(out=o, in_=i), reads=tp, writes=[t_ktmp[sl]])
                rope_chunk(sl, tp, base)
                mk.dma("gpsimd", ccb_in[128 * dkv:128 * (dkv + 1), :],
                       krot[:, TC * sl:TC * (sl + 1)],
                       reads=[t_krot[sl]], writes=[t_ccb_in])
                mk.dma("gpsimd", kloc_sp[128 * dkv:128 * (dkv + 1), :],
                       krot[:, TC * sl:TC * (sl + 1)],
                       reads=[t_krot[sl]], writes=[t_kloc])
            tvs = [pst(512 * t, 512) for t in range(8)]
            for t in range(8):
                mk.g_start((512 * t, 512))
            for di in range(NDC):
                sl = di % 2
                mk.dma("sync", wmb[:, 512 * sl:512 * (sl + 1)],
                       di_["bv_wm"][128 * di:128 * (di + 1), :], writes=[t_wm[sl]])
                mms = []
                for t in range(8):
                    mms.append((ps[:, 512 * t:512 * (t + 1)],
                                y_sb[:, TC * di + 128 * t:TC * di + 128 * (t + 1)],
                                wmb[:, 512 * sl:512 * (sl + 1)],
                                di == 0, di == NDC - 1, [t_wm[sl], t_y[di]]))
                mk.mm_group(pst(0, 4096), mms)
            for t in range(8):
                mk.g_stop((512 * t, 512))
                sl = t % 2
                mk.op("scalar", lambda o=ktmp[:, TC * sl:TC * sl + 512],
                                 i=ps[:, 512 * t:512 * (t + 1)]:
                      nc.scalar.copy(out=o, in_=i),
                      reads=tvs[t], writes=[t_ktmp[sl]])
                dst = ccb_in.rearrange("a b -> (a b)")[
                    512 * TC + t * 65536: 512 * TC + (t + 1) * 65536]
                mk.dma("gpsimd", dst.rearrange("(p c) -> p c", p=128),
                       ktmp[:, TC * sl:TC * sl + 512],
                       reads=[t_ktmp[sl]], writes=[t_ccb_in])
                mk.dma("gpsimd", vloc_sp[128 * t:128 * (t + 1), :],
                       ktmp[:, TC * sl:TC * sl + 512],
                       reads=[t_ktmp[sl]], writes=[t_vloc])
            mk.op("gpsimd", lambda: nc.gpsimd.collective_compute(
                "AllGather", ALU.bypass,
                replica_groups=[[0, 1, 2, 3], [4, 5, 6, 7]],
                ins=[ccb_in.rearrange("a b -> (a b)").opt()],
                outs=[ccb_out.rearrange("a b -> (a b)").opt()]),
                reads=[t_ccb_in], writes=[t_ccb_out], sem="sCC", is_async=True)
            for d in range(NDC):
                sl = d % 2
                base = 1024 * sl
                mk.dma("sync", wbuf[:, D * sl:D * (sl + 1)], di_["bq_w"][d, :, :],
                       writes=[t_w[sl]])
                tp = pst(base, 1024)
                mk.g_start((base, 512)); mk.g_start((base + 512, 512))
                mms = []
                for di in range(NDC):
                    for qt in range(2):
                        mms.append((ps[:, base + 512 * qt:base + 512 * (qt + 1)],
                                    wbuf[:, D * sl + 128 * di:D * sl + 128 * (di + 1)],
                                    y_sb[:, TC * di + 512 * qt:TC * di + 512 * (qt + 1)],
                                    di == 0, di == NDC - 1, [t_w[sl], t_y[di]]))
                mk.mm_group(tp, mms)
                mk.g_stop((base, 512)); mk.g_stop((base + 512, 512))
                mk.op("scalar", lambda o=ktmp[:, TC * sl:TC * (sl + 1)],
                                 i=ps[:, base:base + TC]:
                      nc.scalar.copy(out=o, in_=i), reads=tp, writes=[t_ktmp[sl]])
                rope_chunk(sl, tp, base)
                mk.dma("gpsimd", qb_sp[128 * d:128 * (d + 1), :],
                       krot[:, TC * sl:TC * (sl + 1)],
                       reads=[t_krot[sl]], writes=[t_qb])

        mk.phase_barrier()
        with (
            nc.sbuf_tensor("kbufb", [128, TF], F32R) as kbufb,
            nc.sbuf_tensor("vbufb", [128, TF], F32R) as vbufb,
            nc.sbuf_tensor("qbufb", [128, TC], F32R) as qbufb,
            nc.sbuf_tensor("pbufb", [128, 2 * TC], F32R) as pbufb,
            nc.sbuf_tensor("kband", [128, 512], F32R) as kband,
            nc.sbuf_tensor("vband", [128, 512], F32R) as vband,
            nc.sbuf_tensor("dmb", [128, 2048], F32) as dmb,
        ):
            t_kb = T("kbB")
            t_vb = T("vbB")
            t_q = [T("qB0"), T("qB1")]
            t_pb = [T(f"pBr{i}") for i in range(4)]
            t_kbd = [T("kbd0"), T("kbd1")]
            t_vbd = [T("vbd0"), T("vbd1")]
            t_dm = T("dmask")
            t_out = pst(2048, 1024)
            t_l = pst(3072, 1024)
            t_ao = [T(f"aob{i}") for i in range(NDC)]
            for _j in range(4):
                mk.dma("sync", dmb[:, 512 * _j:512 * (_j + 1)],
                       di_["dmask"][128 * _j:128 * (_j + 1), :], writes=[t_dm])

            def load_group(g):
                for r in range(4):
                    mk.dma("sync", kbufb[:, TC * r:TC * (r + 1)],
                           ccb_out[TC * r + 128 * g:TC * r + 128 * (g + 1), :],
                           reads=[t_ccb_out], writes=[t_kb])
                    src = ccb_out.rearrange("a b -> (a b)")
                    base = (TC * r + 512) * TC + 128 * g
                    sap = bass.AP(src.tensor, src.offset + base,
                                  [[512, 128], [65536, 8], [1, 128]])
                    dap = bass.AP(vbufb.ap().tensor, vbufb.ap().offset + TC * r,
                                  [[TF, 128], [128, 8], [1, 128]])
                    mk.dma("sync", dap, sap, reads=[t_ccb_out], writes=[t_vb])

            for h in range(HEADS):
                g = h // 4
                qsl = 0
                if h % 4 == 0:
                    load_group(g)
                mk.dma("sync", qbufb[:, 0:TC],
                       qb_sp[128 * h:128 * (h + 1), :], reads=[t_qb], writes=[t_q[0]])
                mk.g_start((2048, 512)); mk.g_start((2560, 512))

                def lav_b(kc, psl):
                    lm = []
                    for qt in range(2):
                        lm.append((ps[0:2, 3072 + 512 * qt:3584 + 512 * qt],
                                   cst_sb[:, 0:2],
                                   pbufb[:, TC * psl + 512 * qt:TC * psl + 512 * (qt + 1)],
                                   kc == 0, False, [t_pb[2 * psl + qt], t_consts]))
                    mk.mm_group(t_l, lm)
                    av = []
                    for qt in range(2):
                        av.append((ps[:, 2048 + 512 * qt:2048 + 512 * (qt + 1)],
                                   vbufb[:, 128 * kc:128 * (kc + 1)],
                                   pbufb[:, TC * psl + 512 * qt:TC * psl + 512 * (qt + 1)],
                                   kc == 0, False, [t_vb, t_pb[2 * psl + qt]]))
                    mk.mm_group(t_out, av)

                pend = None
                for kc in range(32):
                    psl = kc % 2
                    base = 1024 * psl
                    tp = pst(base, 1024)
                    mk.g_start((base, 512)); mk.g_start((base + 512, 512))
                    mms = []
                    for qt in range(2):
                        mms.append((ps[:, base + 512 * qt:base + 512 * (qt + 1)],
                                    kbufb[:, 128 * kc:128 * (kc + 1)],
                                    qbufb[:, 512 * qt:512 * (qt + 1)],
                                    True, True, [t_kb, t_q[qsl]]))
                    mk.mm_group(tp, mms)
                    mk.g_stop((base, 512)); mk.g_stop((base + 512, 512))
                    for qt in range(2):
                        mk.op("scalar",
                              lambda o=pbufb[:, TC * psl + 512 * qt:TC * psl + 512 * (qt + 1)],
                                     i=ps[:, base + 512 * qt:base + 512 * (qt + 1)],
                                     b=btbl_sb[:, 32 * qt + kc:32 * qt + kc + 1]:
                              nc.scalar.activation(out=o, in_=i, func=AF.Exp,
                                                   scale=SCALE, bias=b),
                              reads=tp + [t_consts], writes=[t_pb[2 * psl + qt]])
                    if pend is not None:
                        lav_b(*pend)
                    pend = (kc, psl)
                lav_b(*pend)
                for qt in range(2):
                    bsl = 0
                    mk.dma("sync", kband[:, 0:512],
                           kloc_sp[128 * g:128 * (g + 1), 512 * qt:512 * (qt + 1)],
                           reads=[t_kloc], writes=[t_kbd[bsl]])
                    src = vloc_sp.rearrange("a b -> (a b)")
                    base_v = (512 * qt) * KVD + 128 * g
                    sap = bass.AP(src.tensor, src.offset + base_v,
                                  [[KVD, 128], [128 * KVD, 4], [1, 128]])
                    dap = bass.AP(vband.ap().tensor, vband.ap().offset,
                                  [[512, 128], [128, 4], [1, 128]])
                    mk.dma("sync", dap, sap, reads=[t_vloc], writes=[t_vbd[bsl]])

                    def lav_band(j, r, qt=qt, bsl=bsl):
                        mk.mm_group(t_l, [(ps[0:2, 3072 + 512 * qt:3584 + 512 * qt],
                                           cst_sb[:, 0:2],
                                           pbufb[:, 512 * r:512 * (r + 1)],
                                           False, j == 3, [t_pb[r], t_consts])])
                        mk.mm_group(t_out, [(ps[:, 2048 + 512 * qt:2048 + 512 * (qt + 1)],
                                             vband[:, 128 * j:128 * (j + 1)],
                                             pbufb[:, 512 * r:512 * (r + 1)],
                                             False, j == 3, [t_vbd[bsl], t_pb[r]])])

                    pendb = None
                    for j in range(4):
                        psl = j % 2
                        base = 1024 * psl
                        tp = pst(base, 512)
                        mk.g_start((base, 512))
                        mk.mm_group(tp, [(ps[:, base:base + 512],
                                          kband[:, 128 * j:128 * (j + 1)],
                                          qbufb[:, 512 * qt:512 * (qt + 1)],
                                          True, True, [t_kbd[bsl], t_q[qsl]])])
                        mk.g_stop((base, 512))
                        mk.op("vector", lambda o=ps[:, base:base + 512],
                                         m=dmb[:, 512 * j:512 * (j + 1)]:
                              nc.vector.tensor_tensor(out=o, in0=o, in1=m, op=ALU.add),
                              reads=tp + [t_dm], writes=tp)
                        mk.op("scalar",
                              lambda o=pbufb[:, 512 * psl:512 * (psl + 1)],
                                     i=ps[:, base:base + 512]:
                              nc.scalar.activation(out=o, in_=i, func=AF.Exp,
                                                   scale=SCALE, bias=-BEXP),
                              reads=tp, writes=[t_pb[psl]])
                        if pendb is not None:
                            lav_band(*pendb)
                        pendb = (j, j % 2)
                    lav_band(*pendb)
                mk.g_stop((2048, 512)); mk.g_stop((2560, 512))
                emit_norm_head(h, t_out, t_l, t_ao[h])

        mk.phase_barrier()
        with nc.sbuf_tensor("wbuf4", [128, 2 * D], F32R) as wbuf:
            t_w = [T("w40"), T("w41")]
            for d in range(NDC):
                sl = d % 2
                base = 1024 * sl
                mk.dma("sync", wbuf[:, D * sl:D * (sl + 1)], di_["bp_w"][d, :, :],
                       writes=[t_w[sl]])
                tp = pst(base, 1024)
                mk.g_start((base, 512)); mk.g_start((base + 512, 512))
                mms = []
                for di in range(NDC):
                    for qt in range(2):
                        mms.append((ps[:, base + 512 * qt:base + 512 * (qt + 1)],
                                    wbuf[:, D * sl + 128 * di:D * sl + 128 * (di + 1)],
                                    y_sb[:, TC * di + 512 * qt:TC * di + 512 * (qt + 1)],
                                    di == 0, di == NDC - 1, [t_w[sl], t_ao[di]]))
                mk.mm_group(tp, mms)
                mk.g_stop((base, 512)); mk.g_stop((base + 512, 512))
                mk.op("vector", lambda o=x_sb[:, TC * d:TC * (d + 1)],
                                 i=ps[:, base:base + TC]:
                      nc.vector.tensor_tensor(out=o, in0=o, in1=i, op=ALU.add),
                      reads=tp + [t_x[d]], writes=[t_x[d]])

    if _STAGES >= 1:
        emit_ln()
        if _DBG:
            for di in range(NDC):
                mk.dma("gpsimd", dbg["dbg_y"][128 * di:128 * (di + 1), :],
                       y_sb[:, TC * di:TC * (di + 1)], reads=[t_y[di]], writes=[])
        emit_attn_a()
    if _STAGES >= 2:
        emit_ln()
        emit_ffn()
    if _STAGES >= 3:
        emit_ln()
        emit_attn_b()
    if _STAGES >= 4:
        emit_ln()
        emit_ffn(final=True)

    for _ch in mk.chans["gpsimd"]:
        if mk.cnt[_ch]:
            mk._wait("gpsimd", _ch, mk.cnt[_ch])

    mk.final_cnt.update(mk.cnt)
    with nc.Block() as block:
        @block.tensor
        def _(te):
            for f in mk.ops["tensor"]:
                f()

        @block.scalar
        def _(sc):
            for f in mk.ops["scalar"]:
                f()

        @block.vector
        def _(ve):
            for f in mk.ops["vector"]:
                f()

        @block.gpsimd
        def _(gp):
            for f in mk.ops["gpsimd"]:
                f()

        @block.sync
        def _(sy):
            for f in mk.ops["sync"]:
                f()

    stack.close()
    return nc


_NC_CACHE = None


def kernel(**inputs):
    global _NC_CACHE
    in_maps = _prep_host(inputs)
    if _NC_CACHE is None:
        _NC_CACHE = build_nc()
    res = run_bass_kernel_spmd(_NC_CACHE, in_maps, core_ids=list(range(8)))
    out = np.empty((2, TF, D), dtype=np.float32)
    for c in range(8):
        b, g = c // 4, c % 4
        out[b, TC * g:TC * (g + 1), :] = res.results[c]["out_fm"].T
    return out
